# version 06
# baseline (speedup 1.0000x reference)
import numpy as np
import contextlib
import concourse.bass as bass
import concourse.mybir as mybir

F32 = mybir.dt.float32
BF16 = mybir.dt.bfloat16
AF = mybir.ActivationFunctionType
ALU = mybir.AluOpType
AX = mybir.AxisListType

ENGINES = ("sync", "tensor", "vector", "scalar", "gpsimd")

class Buf:
    __slots__ = ("h", "name", "last_writer", "readers", "state")
    def __init__(self, h, name, state=None):
        self.h = h; self.name = name
        self.last_writer = None
        self.readers = []
        self.state = state if state is not None else self
    def ap(self):
        a = getattr(self.h, "ap", None)
        return self.h.ap() if callable(a) else self.h
    def __getitem__(self, idx):
        return self.ap()[idx]

class Prog:
    def __init__(self, name="prog"):
        self.nc = bass.Bass("TRN2", target_bir_lowering=False)
        self.stack = contextlib.ExitStack()
        self.sems = {}
        self.NDMAGRP = 6
        self._dma_rr = {e: 0 for e in ENGINES}
        for e in ENGINES:
            self.sems[e] = self.stack.enter_context(self.nc.semaphore(f"s_{e}"))
            for g in range(self.NDMAGRP):
                self.sems[f"{e}_dma{g}"] = self.stack.enter_context(self.nc.semaphore(f"s_{e}d{g}"))
        self.count = {k: 0 for k in self.sems}
        self.tasks = {e: [] for e in ENGINES}
        self.final_deps = []
        self.nbuf = 0
        self.current_barrier = ()
        self.NCONST = 64
        self.const_vals = {}
        self.const_pool = Buf(self.nc.alloc_sbuf_tensor("c_pool_sb", [128, self.NCONST], F32), "c_pool_sb")
        self.const_dram = Buf(self.nc.dram_tensor("c_pool", [128, self.NCONST], F32, kind="ExternalInput"), "c_pool")
        self.dma("sync", self.const_pool.ap(), self.const_dram.ap(),
                 reads=[self.const_dram], writes=[self.const_pool])
        self.ARENA_BYTES = 195 * 1024
        self._arena_h = self.nc.alloc_sbuf_tensor("arena", [128, self.ARENA_BYTES], mybir.dt.uint8)
        self._arena_views = {}
        self._arena_off = 0
        self.current_barrier = ()
        self.rowtmp = Buf(self.nc.alloc_sbuf_tensor("rowtmp", [1, 2048], F32), "rowtmp")
        self.psum_f = [Buf(self.nc.alloc_psum_tensor(f"ppf{i}", [128, 1024], F32), f"ppf{i}") for i in range(4)]
        self._psum_bf_view = self.psum_f[3].h.bitcast(BF16).ap()

    def persist(self, name, shape, dtype):
        return Buf(self.nc.alloc_sbuf_tensor(name, list(shape), dtype), name)
    def sbuf(self, name, shape, dtype):
        assert len(shape) == 2 and shape[0] <= 128, shape
        esz = {F32: 4, BF16: 2, mybir.dt.int32: 4}[dtype]
        off = (self._arena_off + 31) & ~31
        nbytes = shape[1] * esz
        assert off + nbytes <= self.ARENA_BYTES, f"arena overflow at {name}: {off+nbytes}"
        self._arena_off = off + nbytes
        if dtype not in self._arena_views:
            self._arena_views[dtype] = self._arena_h.bitcast(dtype).ap()
        v = self._arena_views[dtype]
        ap = v[0:shape[0], off // esz : off // esz + shape[1]]
        return Buf(ap, name)
    def arena_save(self):
        return self._arena_off
    def arena_restore(self, mark):
        self._arena_off = mark
        self.stage_barrier()
    def stage_barrier(self):
        toks = []
        for e in ENGINES:
            if self.count[e] > 0:
                toks.append((e, self.count[e]))
            for g in range(self.NDMAGRP):
                sname = f"{e}_dma{g}"
                if self.count[sname] > 0:
                    toks.append((sname, self.count[sname]))
        self.current_barrier = tuple(toks)
    def finish_all(self):
        toks = []
        for e in ENGINES:
            if self.count[e] > 0:
                toks.append((e, self.count[e]))
            for g in range(self.NDMAGRP):
                sname = f"{e}_dma{g}"
                if self.count[sname] > 0:
                    toks.append((sname, self.count[sname]))
        self.finish(toks)
    def psum(self, name, shape, dtype=F32):
        return Buf(self.nc.alloc_psum_tensor(name, list(shape), dtype), name)
    def pf(self, idx, c0=0, c1=1024, p0=0, p1=128, name=None):
        return Buf(self.psum_f[idx].h.ap()[p0:p1, c0:c1], name or f"pf{idx}_{c0}_{p0}",
                   state=self.psum_f[idx])
    def pb(self, idx, name=None):
        return Buf(self._psum_bf_view[:, idx*128:(idx+1)*128], name or f"pb{idx}", state=self.psum_f[3])
    def dram_in(self, name, shape, dtype):
        return Buf(self.nc.dram_tensor(name, list(shape), dtype, kind="ExternalInput"), name)
    def dram_out(self, name, shape, dtype):
        return Buf(self.nc.dram_tensor(name, list(shape), dtype, kind="ExternalOutput"), name)
    def dram_tmp(self, name, shape, dtype):
        return Buf(self.nc.dram_tensor(name, list(shape), dtype, kind="Internal"), name)

    def _deps_for(self, engine, reads, writes, extra):
        deps = list(self.current_barrier)
        for b in reads:
            b = b.state
            if b.last_writer is not None:
                deps.append(b.last_writer)
        for b in writes:
            b = b.state
            deps.extend(b.readers)
            if b.last_writer is not None:
                deps.append(b.last_writer)
        deps.extend(t for t in extra if t is not None)
        best = {}
        for (s, v) in deps:
            if "_dma" in s:
                v = self.count[s]
            if best.get(s, -1) < v:
                best[s] = v
        return tuple(best.items())

    def _update(self, tok, reads, writes):
        for b in reads:
            b.state.readers.append(tok)
        for b in writes:
            b.state.last_writer = tok
            b.state.readers = []

    def constf(self, v, p=128):
        import numpy as _np
        v = float(_np.float32(v))
        if v not in self.const_vals:
            assert len(self.const_vals) < self.NCONST, "const pool full"
            self.const_vals[v] = len(self.const_vals)
        i = self.const_vals[v]
        return self.const_pool.ap()[0:p, i:i+1]

    def host_const_pool(self):
        import numpy as _np
        arr = _np.zeros((128, self.NCONST), dtype=_np.float32)
        for v, i in self.const_vals.items():
            arr[:, i] = v
        return {"c_pool": arr}

    def task(self, engine, body, reads=(), writes=(), deps=(), strict=False):
        reads = list(reads) + [self.const_pool]
        dd = self._deps_for(engine, reads, writes, deps)
        self.tasks[engine].append((dd, "c", body, strict) if strict else (dd, "c", body))
        self.count[engine] += 1
        tok = (engine, self.count[engine])
        self._update(tok, reads, writes)
        return tok

    def dma(self, engine, out_ap, in_ap, reads=(), writes=(), deps=(), slot=None, **kw):
        dd = self._deps_for(engine, reads, writes, deps)
        if slot is None:
            slot = self._dma_rr[engine]
            self._dma_rr[engine] = (slot + 1) % self.NDMAGRP
        sname = f"{engine}_dma{slot}"
        if self.count[sname] > 0:
            dd = tuple(list(dd) + [(sname, self.count[sname])])
        self.tasks[engine].append((dd, "d", (out_ap, in_ap, sname, kw)))
        self.count[sname] += 16
        tok = (sname, self.count[sname])
        self._update(tok, reads, writes)
        return tok

    def finish(self, deps):
        self.final_deps.extend(d for d in deps if d is not None)

    def build(self):
        waited = {e: {} for e in ENGINES}
        def emit_stream(eng_obj, ename):
            for entry in self.tasks[ename]:
                if len(entry) == 4:
                    deps, kind, payload, strict = entry
                else:
                    (deps, kind, payload), strict = entry, False
                for (sname, val) in deps:
                    if sname == ename and ename == "tensor" and kind == "c":
                        continue
                    if waited[ename].get(sname, 0) < val:
                        eng_obj.wait_ge(self.sems[sname], val)
                        waited[ename][sname] = val
                if kind == "d":
                    out_ap, in_ap, sname_d, kw = payload
                    ins = eng_obj.dma_start(out=out_ap, in_=in_ap, **kw)
                    ins.then_inc(self.sems[sname_d], 16)
                else:
                    last = payload(eng_obj)
                    if isinstance(last, (list, tuple)):
                        last = last[-1]
                    assert last is not None, f"task body on {ename} returned None"
                    last.then_inc(self.sems[ename], 1)
            if ename == "sync":
                best = {}
                for (s, v) in self.final_deps:
                    if best.get(s, -1) < v:
                        best[s] = v
                for s, v in best.items():
                    if s == ename:
                        continue
                    if waited[ename].get(s, 0) < v:
                        eng_obj.wait_ge(self.sems[s], v)
                        waited[ename][s] = v
        with self.nc.Block() as block:
            @block.sync
            def _(e): emit_stream(e, "sync")
            if self.tasks["tensor"]:
                @block.tensor
                def _(e): emit_stream(e, "tensor")
            if self.tasks["vector"]:
                @block.vector
                def _(e): emit_stream(e, "vector")
            if self.tasks["scalar"]:
                @block.scalar
                def _(e): emit_stream(e, "scalar")
            if self.tasks["gpsimd"]:
                @block.gpsimd
                def _(e): emit_stream(e, "gpsimd")
        return self.nc


def enable_ldw_opt():
    import concourse.bass_utils as _bu
    if getattr(_bu, "_ldw_patched", False):
        return
    _orig = _bu.run_command
    def _patched(cmd, *a, **kw):
        if isinstance(cmd, list):
            cmd = ["--enable-ldw-opt=true" if c == "--enable-ldw-opt=false" else c for c in cmd]
        return _orig(cmd, *a, **kw)
    _bu.run_command = _patched
    _bu._ldw_patched = True


TLOC, SROW, D, DI = 4096, 2048, 1024, 2048
INV01 = float(np.float32(1.0) / np.float32(0.1))

def emit_consts(P):
    c = {}
    ident_f = P.persist("ident_f", [128, 128], F32)
    c["ident_f"] = ident_f
    c["ident_bf"] = P.persist("ident_bf", [128, 128], BF16)
    c["ones_f"] = P.persist("ones_f", [128, 1], F32)
    c["ones_row"] = P.persist("ones_row", [1, 128], F32)
    c["ones_all"] = P.persist("ones_all", [64, 128], F32)
    id_dram = P.dram_in("c_ident", [128, 128], F32)
    ones_dram = P.dram_in("c_ones", [128, 1], F32)
    onesr_dram = P.dram_in("c_ones_row", [1, 128], F32)
    onesa_dram = P.dram_in("c_ones_all", [64, 128], F32)
    P.dma("sync", ident_f.ap(), id_dram.ap(), reads=[id_dram], writes=[ident_f])
    P.dma("sync", c["ones_f"].ap(), ones_dram.ap(), reads=[ones_dram], writes=[c["ones_f"]])
    P.dma("sync", c["ones_row"].ap(), onesr_dram.ap(), reads=[onesr_dram], writes=[c["ones_row"]])
    P.dma("sync", c["ones_all"].ap(), onesa_dram.ap(), reads=[onesa_dram], writes=[c["ones_all"]])
    P.task("scalar", lambda e: e.copy(c["ident_bf"].ap(), ident_f.ap()),
           reads=[ident_f], writes=[c["ident_bf"]])
    return c

def host_consts():
    return {"c_ident": np.eye(128, dtype=np.float32),
            "c_ones": np.ones((128, 1), dtype=np.float32),
            "c_ones_row": np.ones((1, 128), dtype=np.float32),
            "c_ones_all": np.ones((64, 128), dtype=np.float32)}

def stage_ln(P, c, x_tm, xn_tm, xn_fm_bf, name="ln0", eps=1e-5):
    nt = TLOC // 128
    sx = [P.sbuf(f"{name}_x{i}", [128, D], F32) for i in range(2)]
    sq = P.sbuf(f"{name}_sq", [128, D], F32)
    cols = [P.sbuf(f"{name}_cols{i}", [128, 8], F32) for i in range(2)]
    xnbf = [P.sbuf(f"{name}_xnbf{i}", [128, D], BF16) for i in range(2)]
    tp = [Buf(P.psum_f[2].h.bitcast(BF16).ap()[:, 0:1024], f"{name}_tp0", state=P.psum_f[2]),
          Buf(P.psum_f[3].h.bitcast(BF16).ap()[:, 0:1024], f"{name}_tp1", state=P.psum_f[3])]
    tpo = [P.sbuf(f"{name}_tpo{i}", [128, 1024], BF16) for i in range(2)]
    for i in range(nt):
        b = i % 2
        X, CL = sx[b], cols[b]
        P.dma("sync", X.ap(), x_tm[i*128:(i+1)*128, :], reads=[x_tm], writes=[X])
        P.task("vector", lambda e, X=X, CL=CL: e.tensor_reduce(CL[:, 0:1], X.ap(), AX.X, ALU.add),
               reads=[X], writes=[CL])
        P.task("vector", lambda e, CL=CL: e.tensor_scalar(CL[:, 1:2], CL[:, 0:1], -1.0/D, None, ALU.mult),
               reads=[CL], writes=[CL], strict=True)
        P.task("vector", lambda e, X=X, CL=CL: e.tensor_scalar(X.ap(), X.ap(), CL[:, 1:2], None, ALU.add),
               reads=[X, CL], writes=[X], strict=True)
        P.task("scalar", lambda e, X=X, CL=CL: e.activation(sq.ap(), X.ap(), AF.Square, accum_out=CL[:, 2:3]),
               reads=[X], writes=[sq, CL])
        P.task("scalar", lambda e, CL=CL: e.activation(CL[:, 3:4], CL[:, 2:3], AF.Ln, bias=P.constf(eps), scale=P.constf(1.0/D)),
               reads=[CL], writes=[CL], strict=True)
        P.task("scalar", lambda e, CL=CL: e.activation(CL[:, 4:5], CL[:, 3:4], AF.Exp, scale=P.constf(-0.5)),
               reads=[CL], writes=[CL], strict=True)
        P.task("vector", lambda e, X=X, CL=CL: e.tensor_scalar(X.ap(), X.ap(), CL[:, 4:5], None, ALU.mult),
               reads=[X, CL], writes=[X])
        P.dma("gpsimd", xn_tm[i*128:(i+1)*128, :], X.ap(), reads=[X], writes=[xn_tm])
        if xn_fm_bf is None:
            continue
        XB = xnbf[b]
        P.task("scalar", lambda e, X=X, XB=XB: e.copy(XB.ap(), X.ap()), reads=[X], writes=[XB])
        TPB, TOB = tp[b], tpo[b]
        def trp(e, XB=XB, TPB=TPB):
            ins = None
            for j in range(8):
                ins = e.transpose(TPB[:, j*128:(j+1)*128], XB[:, j*128:(j+1)*128], c["ident_bf"].ap())
            return ins
        P.task("tensor", trp, reads=[XB, c["ident_bf"]], writes=[TPB])
        P.task("scalar", lambda e, TPB=TPB, TOB=TOB: e.copy(TOB.ap(), TPB.ap()), reads=[TPB], writes=[TOB])
        for j in range(8):
            P.dma("gpsimd", xn_fm_bf[j*128:(j+1)*128, i*128:(i+1)*128], TOB[:, j*128:(j+1)*128],
                  reads=[TOB], writes=[xn_fm_bf])

def load_cast_bf(P, name, w_dram, R, Ccols, stage_bufs=None):
    wbf = P.sbuf(f"{name}_bf", [128, (R // 128) * Ccols], BF16)
    if stage_bufs is None:
        stage_bufs = [P.sbuf(f"{name}_st0", [128, Ccols], F32)]
    for i in range(R // 128):
        S = stage_bufs[i % len(stage_bufs)]
        P.dma("sync", S[0:128, 0:Ccols], w_dram[i*128:(i+1)*128, :], reads=[w_dram], writes=[S])
        P.task("scalar", lambda e, S=S, i=i: e.copy(wbf[:, i*Ccols:(i+1)*Ccols], S[0:128, 0:Ccols]),
               reads=[S], writes=[wbf])
    return wbf

def wtile(wbf, i, Ccols, j0, j1):
    return wbf[:, i*Ccols + j0 : i*Ccols + j1]

def stage_mamba_front(P, c, W, xn_fm_bf, xc_fm, zs_fm, dt_fm, xdbl_fm, bc_fm, name="mf"):
    TC = 512
    nchunk = TLOC // TC
    w_in_bf = load_cast_bf(P, f"{name}_win", W["W_inT"], 1024, 4096)
    xproj_bf = load_cast_bf(P, f"{name}_xp", W["x_projT"], 2048, 96)
    dtproj_bf = P.sbuf(f"{name}_dtp_bf", [64, 2048], BF16)
    dtp_st = P.sbuf(f"{name}_dtp_st", [64, 2048], F32)
    P.dma("sync", dtp_st.ap(), W["dt_projT"].ap(), reads=[W["dt_projT"]], writes=[dtp_st])
    P.task("scalar", lambda e: e.copy(dtproj_bf.ap(), dtp_st.ap()), reads=[dtp_st], writes=[dtproj_bf])
    convw = P.sbuf(f"{name}_convw", [128, 16*4], F32)
    convb = P.sbuf(f"{name}_convb", [128, 16], F32)
    dtb = P.sbuf(f"{name}_dtb", [128, 16], F32)
    for i in range(16):
        P.dma("sync", convw[:, i*4:(i+1)*4], W["conv_w"][i*128:(i+1)*128, :], reads=[W["conv_w"]], writes=[convw])
        P.dma("sync", convb[:, i:i+1], W["conv_b"][i*128:(i+1)*128, :], reads=[W["conv_b"]], writes=[convb])
        P.dma("sync", dtb[:, i:i+1], W["dt_b"][i*128:(i+1)*128, :], reads=[W["dt_b"]], writes=[dtb])
    xn_a = P.sbuf(f"{name}_xna", [128, 8 * TC], BF16)
    xn_t = [Buf(xn_a.ap()[:, i*TC:(i+1)*TC], f"{name}_xn{i}", state=xn_a) for i in range(8)]
    pmm = [P.pf(0, 0, TC), P.pf(1, 0, TC)]
    xzx2 = [P.sbuf(f"{name}_xzx{b}", [128, TC+3], F32) for b in range(2)]
    carry = [P.sbuf(f"{name}_carry{i}", [128, 3*16], F32) for i in range(2)]
    conv_u2 = [P.sbuf(f"{name}_convu{b}", [128, TC], F32) for b in range(2)]
    expb2 = [P.sbuf(f"{name}_expb{b}", [128, TC], F32) for b in range(2)]
    zsf2 = [P.sbuf(f"{name}_zsf{b}", [128, TC], F32) for b in range(2)]
    xc_f = [P.sbuf(f"{name}_xcf{i}", [128, TC], F32) for i in range(2)]
    xc_bf = [P.sbuf(f"{name}_xcbf{i}", [128, TC], BF16) for i in range(16)]
    xdbl = P.sbuf(f"{name}_xdbl", [128, TC], F32)
    xdbl_bf = P.sbuf(f"{name}_xdblbf", [64, TC], BF16)
    pdbl = P.pf(2, 0, TC)
    dtt = P.sbuf(f"{name}_dtt", [128, TC], F32)
    dtt2 = P.sbuf(f"{name}_dtt2", [128, TC], F32)

    for ch in range(nchunk):
        row = (ch * TC) // SROW
        first_in_row = (ch * TC) % SROW == 0
        t0 = ch * TC
        CARRY = carry[0]
        P.dma("sync", xn_a.ap().rearrange("p (k c) -> p k c", k=8),
              xn_fm_bf[0:1024, t0:t0+TC].rearrange("(k p) c -> p k c", p=128),
              reads=[xn_fm_bf], writes=[xn_a])
        XN = xn_t
        for et in range(16):
            PM = pmm[et % 2]
            eb = et % 2
            xzx, conv_u, expb = xzx2[eb], conv_u2[eb], expb2[eb]
            if et % 2 == 0:
                def mm(e, et=et, XN=XN):
                    ins = None
                    for ei in range(2):
                        PMx = pmm[ei]
                        for dk in range(8):
                            ins = e.matmul(PMx.ap(), wtile(w_in_bf, dk, 4096, (et+ei)*128, (et+ei+1)*128),
                                           XN[dk].ap(), start=(dk == 0), stop=(dk == 7))
                    return ins
                P.task("tensor", mm, reads=[w_in_bf] + XN, writes=[pmm[0], pmm[1]])
                for ei in range(2):
                    P.task("scalar", lambda e, PM=pmm[ei], xzx=xzx2[ei]: e.copy(xzx[:, 3:TC+3], PM.ap()),
                           reads=[pmm[ei]], writes=[xzx2[ei]])
            def convt(e, et=et, xzx=xzx, conv_u=conv_u, fir=first_in_row, CARRY=CARRY):
                if fir:
                    e.memset(xzx[:, 0:3], 0.0)
                else:
                    e.tensor_copy(xzx[:, 0:3], CARRY[:, et*3:et*3+3])
                e.tensor_copy(CARRY[:, et*3:et*3+3], xzx[:, TC:TC+3])
                e.tensor_scalar(conv_u.ap(), xzx[:, 0:TC], convw[:, et*4:et*4+1], convb[:, et:et+1], ALU.mult, ALU.add)
                ins = None
                for k2 in (1, 2, 3):
                    ins = e.scalar_tensor_tensor(conv_u.ap(), xzx[:, k2:TC+k2], convw[:, et*4+k2:et*4+k2+1],
                                                 conv_u.ap(), ALU.mult, ALU.add)
                return ins
            P.task("vector", convt, reads=[xzx, convw, convb, CARRY], writes=[conv_u, xzx, CARRY])
            P.task("scalar", lambda e, conv_u=conv_u, expb=expb: e.activation(expb.ap(), conv_u.ap(), AF.Exp, scale=P.constf(-1.0)),
                   reads=[conv_u], writes=[expb])
            XCF, XCB = xc_f[et % 2], xc_bf[et]
            def silut(e, XCF=XCF, conv_u=conv_u, expb=expb):
                e.tensor_scalar(expb.ap(), expb.ap(), 1.0, None, ALU.add)
                e.reciprocal(expb.ap(), expb.ap())
                return e.tensor_tensor(XCF.ap(), conv_u.ap(), expb.ap(), ALU.mult)
            P.task("vector", silut, reads=[conv_u, expb], writes=[XCF, expb])
            P.task("scalar", lambda e, XCF=XCF, XCB=XCB: e.copy(XCB.ap(), XCF.ap()), reads=[XCF], writes=[XCB])
            P.dma("gpsimd", xc_fm[et*128:(et+1)*128, t0:t0+TC], XCF.ap(), reads=[XCF], writes=[xc_fm])
        for et in range(16):
            PM = pmm[et % 2]
            eb = et % 2
            expb, zsf = expb2[eb], zsf2[eb]
            if et % 2 == 0:
                def mmz(e, et=et, XN=XN):
                    ins = None
                    for ei in range(2):
                        PMx = pmm[ei]
                        for dk in range(8):
                            ins = e.matmul(PMx.ap(), wtile(w_in_bf, dk, 4096, 2048+(et+ei)*128, 2048+(et+ei+1)*128),
                                           XN[dk].ap(), start=(dk == 0), stop=(dk == 7))
                    return ins
                P.task("tensor", mmz, reads=[w_in_bf] + XN, writes=[pmm[0], pmm[1]])
            P.task("scalar", lambda e, PM=PM, expb=expb: e.activation(expb.ap(), PM.ap(), AF.Exp, scale=P.constf(-1.0)),
                   reads=[PM], writes=[expb])
            def siluz(e, PM=PM, expb=expb, zsf=zsf):
                e.tensor_scalar(expb.ap(), expb.ap(), 1.0, None, ALU.add)
                e.reciprocal(expb.ap(), expb.ap())
                return e.tensor_tensor(zsf.ap(), PM.ap(), expb.ap(), ALU.mult)
            P.task("vector", siluz, reads=[PM, expb], writes=[zsf, expb])
            P.dma("gpsimd", zs_fm[et*128:(et+1)*128, t0:t0+TC], zsf.ap(), reads=[zsf], writes=[zs_fm])
        def mdbl(e):
            ins = None
            for et in range(16):
                ins = e.matmul(pdbl.ap()[0:96, :], wtile(xproj_bf, et, 96, 0, 96),
                               xc_bf[et].ap(), start=(et == 0), stop=(et == 15))
            return ins
        P.task("tensor", mdbl, reads=[xproj_bf] + xc_bf, writes=[pdbl])
        P.task("scalar", lambda e: e.copy(xdbl[0:96, :], pdbl.ap()[0:96, :]), reads=[pdbl], writes=[xdbl])
        P.dma("gpsimd", xdbl_fm[:, t0:t0+TC], xdbl[0:96, :], reads=[xdbl], writes=[xdbl_fm])
        P.dma("gpsimd", bc_fm[:, t0:t0+TC], xdbl[64:96, :], reads=[xdbl], writes=[bc_fm])
        P.task("scalar", lambda e: e.copy(xdbl_bf.ap(), xdbl[0:64, :]), reads=[xdbl], writes=[xdbl_bf])
        for dk in range(16):
            PM = pmm[dk % 2]
            P.task("tensor", lambda e, dk=dk, PM=PM: e.matmul(
                PM.ap(), dtproj_bf[:, dk*128:(dk+1)*128], xdbl_bf.ap(), start=True, stop=True),
                reads=[dtproj_bf, xdbl_bf], writes=[PM])
            def sp1(e, PM=PM, dk=dk):
                e.activation(dtt.ap(), PM.ap(), AF.Relu, bias=dtb[:, dk:dk+1])
                e.activation(dtt2.ap(), PM.ap(), AF.Abs, bias=dtb[:, dk:dk+1])
                e.activation(dtt2.ap(), dtt2.ap(), AF.Exp, scale=P.constf(-1.0))
                return e.activation(dtt2.ap(), dtt2.ap(), AF.Ln, bias=P.constf(1.0))
            P.task("scalar", sp1, reads=[PM, dtb], writes=[dtt, dtt2])
            P.task("vector", lambda e: e.tensor_tensor(dtt.ap(), dtt.ap(), dtt2.ap(), ALU.add),
                   reads=[dtt, dtt2], writes=[dtt])
            P.dma("gpsimd", dt_fm[dk*128:(dk+1)*128, t0:t0+TC], dtt.ap(), reads=[dtt], writes=[dt_fm])


def stage_scan_gen(P, c, W, dt_fm, xc_fm, zs_fm, bc_fm, y_fm_bf, name="sc"):
    TC = 512
    NB = 4
    oh_dram = P.dram_in(f"{name}_onehot", [32, 32*128], F32)
    oh = P.sbuf(f"{name}_oh", [32, 32*128], F32)
    P.dma("sync", oh.ap(), oh_dram.ap(), reads=[oh_dram], writes=[oh])
    A_all = P.sbuf(f"{name}_A", [128, 256], F32)
    Dcol = P.sbuf(f"{name}_D", [128, 16], F32)
    ast = P.sbuf(f"{name}_ast", [128, 16], F32)
    for k in range(16):
        P.dma("sync", ast.ap(), W["A_log"][k*128:(k+1)*128, :], reads=[W["A_log"]], writes=[ast])
        P.task("scalar", lambda e, k=k: e.activation(A_all[:, k*16:(k+1)*16], ast.ap(), AF.Exp),
               reads=[ast], writes=[A_all])
        P.dma("sync", Dcol[:, k:k+1], W["Dd"][k*128:(k+1)*128, :], reads=[W["Dd"]], writes=[Dcol])
    P.task("vector", lambda e: e.tensor_scalar(A_all.ap(), A_all.ap(), -1.0, None, ALU.mult),
           reads=[A_all], writes=[A_all])
    states = P.sbuf(f"{name}_st", [128, 256], F32)
    state_g = [Buf(states.ap()[:, g*4:(g+1)*4], f"stg{g}") for g in range(64)]
    bc_sb = P.sbuf(f"{name}_bc", [32, TC], F32)
    pB = P.pf(0, 0, TC)
    pC = P.pf(1, 0, TC)
    Bb = P.sbuf(f"{name}_Bb", [128, 16 * TC], F32)
    Cb = P.sbuf(f"{name}_Cb", [128, 16 * TC], F32)
    Bb_q = [Buf(Bb.ap()[:, q*NB*TC:(q+1)*NB*TC], f"{name}_Bbq{q}") for q in range(4)]
    Cb_q = [Buf(Cb.ap()[:, q*NB*TC:(q+1)*NB*TC], f"{name}_Cbq{q}") for q in range(4)]
    dtb_ = [P.sbuf(f"{name}_dt{i}", [128, TC], F32) for i in range(2)]
    xcb_ = [P.sbuf(f"{name}_xc{i}", [128, TC], F32) for i in range(2)]
    zsb_ = [P.sbuf(f"{name}_zs{i}", [128, TC], F32) for i in range(2)]
    wbuf = [P.sbuf(f"{name}_w{i}", [128, TC], F32) for i in range(2)]
    dA4 = [P.sbuf(f"{name}_dA{i}", [128, NB * TC], BF16) for i in range(2)]
    dB4 = [P.sbuf(f"{name}_dB{i}", [128, NB * TC], BF16) for i in range(2)]
    hh4 = [P.sbuf(f"{name}_hh{i}", [128, NB * TC], F32) for i in range(2)]
    tr4 = [P.sbuf(f"{name}_tr{i}", [128, NB * TC], F32) for i in range(2)]
    yacc_ = [P.sbuf(f"{name}_y{i}", [128, TC], F32) for i in range(2)]
    ybf_ = [P.sbuf(f"{name}_ybf{i}", [128, TC], BF16) for i in range(2)]
    nchunk = TLOC // TC
    seg = lambda ap_, w_=TC: ap_.rearrange("p (s c) -> p s c", s=NB)
    for ch in range(nchunk):
        t0 = ch * TC
        first_in_row = t0 % SROW == 0
        if first_in_row:
            P.task("vector", lambda e: e.memset(states.ap(), 0.0), writes=state_g)
        P.dma("sync", bc_sb.ap(), bc_fm[:, t0:t0+TC], reads=[bc_fm], writes=[bc_sb])
        for n in range(16):
            P.task("tensor", lambda e, n=n: e.matmul(pB.ap(), oh[:, n*128:(n+1)*128], bc_sb.ap(),
                                                     start=True, stop=True),
                   reads=[oh, bc_sb], writes=[pB])
            P.task("scalar", lambda e, n=n: e.copy(Bb[:, n*TC:(n+1)*TC], pB.ap()), reads=[pB], writes=[Bb_q[n//4]])
            P.task("tensor", lambda e, n=n: e.matmul(pC.ap(), oh[:, (16+n)*128:(17+n)*128], bc_sb.ap(),
                                                     start=True, stop=True),
                   reads=[oh, bc_sb], writes=[pC])
            P.task("scalar", lambda e, n=n: e.copy(Cb[:, n*TC:(n+1)*TC], pC.ap()), reads=[pC], writes=[Cb_q[n//4]])
        for k in range(16):
            b = k % 2
            DT, XC, WW, ZS = dtb_[b], xcb_[b], zsb_[b], wbuf[b]
            P.dma("sync", DT.ap(), dt_fm[k*128:(k+1)*128, t0:t0+TC], reads=[dt_fm], writes=[DT])
            P.dma("sync", XC.ap(), xc_fm[k*128:(k+1)*128, t0:t0+TC], reads=[xc_fm], writes=[XC])
            P.dma("sync", ZS.ap(), zs_fm[k*128:(k+1)*128, t0:t0+TC], reads=[zs_fm], writes=[ZS])
            P.task("vector", lambda e, DT=DT, XC=XC, WW=WW: e.tensor_tensor(WW.ap(), DT.ap(), XC.ap(), ALU.mult),
                   reads=[DT, XC], writes=[WW])
            wrep = WW.ap().rearrange("p (o c) -> p o c", o=1)
            yacc, ybf = yacc_[k % 2], ybf_[k % 2]
            def emit_save_trtg(nb):
                bb_ = nb % 2
                HH, TR = hh4[bb_], tr4[bb_]
                SG = state_g[k*4 + nb]
                P.task("gpsimd", lambda e, HH=HH, SG=SG: e.tensor_copy(
                    SG.ap().rearrange("p (s o) -> p s o", s=NB), seg(HH.ap())[:, :, TC-1:TC]),
                    reads=[HH], writes=[SG])
                def trt_g(e, HH=HH, nb=nb, TR=TR):
                    return e.tensor_tensor(TR.ap()[:, 0:2*TC], HH.ap()[:, 0:2*TC],
                                           Cb.ap()[:, nb*NB*TC:nb*NB*TC+2*TC], ALU.mult)
                P.task("gpsimd", trt_g, reads=[HH, Cb_q[nb]], writes=[TR])
            def emit_trt_v(nb):
                bb_ = nb % 2
                HH, TR = hh4[bb_], tr4[bb_]
                def trt_v(e, HH=HH, nb=nb, TR=TR):
                    return e.tensor_tensor(TR.ap()[:, 2*TC:4*TC], HH.ap()[:, 2*TC:4*TC],
                                           Cb.ap()[:, nb*NB*TC+2*TC:(nb+1)*NB*TC], ALU.mult)
                P.task("vector", trt_v, reads=[hh4[bb_], Cb_q[nb]], writes=[TR])
            def emit_yadd(nb):
                TR0 = tr4[nb % 2]
                def yadd(e, TR0=TR0, first=(nb == 0), yacc=yacc):
                    e.tensor_tensor(TR0.ap()[:, 0:2*TC], TR0.ap()[:, 0:2*TC], TR0.ap()[:, 2*TC:4*TC], ALU.add)
                    e.tensor_tensor(TR0.ap()[:, 0:TC], TR0.ap()[:, 0:TC], TR0.ap()[:, TC:2*TC], ALU.add)
                    if first:
                        return e.tensor_copy(yacc.ap(), TR0.ap()[:, 0:TC])
                    return e.tensor_tensor(yacc.ap(), yacc.ap(), TR0.ap()[:, 0:TC], ALU.add)
                P.task("vector", yadd, reads=[TR0, yacc], writes=[yacc, TR0])
            for nb in range(4):
                bb_ = nb % 2
                DA, DB, HH = dA4[bb_], dB4[bb_], hh4[bb_]
                SG = state_g[k*4 + nb]
                def mk_da(e, DT=DT, k=k, nb=nb, DA=DA):
                    ins = None
                    for j in range(NB):
                        n = nb*NB + j
                        ins = e.activation(DA.ap()[:, j*TC:(j+1)*TC], DT.ap(), AF.Exp,
                                           scale=A_all[:, k*16+n:k*16+n+1])
                    return ins
                P.task("scalar", mk_da, reads=[DT, A_all], writes=[DA])
                def gpd(e, WW=WW, nb=nb, DA=DA, DB=DB, SG=SG):
                    e.memset(seg(DA.ap())[:, :, 0:1], 0.0)
                    e.tensor_tensor(
                        DB.ap().rearrange("p (s c) -> p s c", s=NB),
                        WW.ap().rearrange("p (o c) -> p o c", o=1).broadcast_to([128, NB, TC]),
                        Bb.ap()[:, nb*NB*TC:(nb+1)*NB*TC].rearrange("p (s c) -> p s c", s=NB), ALU.mult)
                    return e.tensor_copy(seg(DB.ap())[:, :, 0:1], SG.ap().rearrange("p (s o) -> p s o", s=NB))
                P.task("gpsimd", gpd, reads=[WW, Bb_q[nb], SG], writes=[DA, DB])
                if nb > 0:
                    emit_save_trtg(nb - 1)
                P.task("vector", lambda e, DA=DA, DB=DB, HH=HH: e.tensor_tensor_scan(
                    HH.ap(), DA.ap(), DB.ap(), 0.0, ALU.mult, ALU.add),
                    reads=[DA, DB], writes=[HH])
                if nb > 0:
                    emit_trt_v(nb - 1)
                if nb > 1:
                    emit_yadd(nb - 2)
            emit_save_trtg(3)
            emit_trt_v(3)
            emit_yadd(2)
            emit_yadd(3)
            def gatet(e, XC=XC, k=k, ZS=ZS, yacc=yacc):
                e.scalar_tensor_tensor(yacc.ap(), XC.ap(), Dcol[:, k:k+1], yacc.ap(), ALU.mult, ALU.add)
                return e.tensor_tensor(yacc.ap(), yacc.ap(), ZS.ap(), ALU.mult)
            P.task("vector", gatet, reads=[XC, Dcol, yacc, ZS], writes=[yacc])
            P.task("scalar", lambda e, ybf=ybf, yacc=yacc: e.copy(ybf.ap(), yacc.ap()), reads=[yacc], writes=[ybf])
            P.dma("gpsimd", y_fm_bf[k*128:(k+1)*128, t0:t0+TC], ybf.ap(), reads=[ybf], writes=[y_fm_bf])
        yield ch

def host_onehot():
    oh = np.zeros((32, 32*128), dtype=np.float32)
    for n in range(32):
        oh[n, n*128:(n+1)*128] = 1.0
    return {"sc_onehot": oh}

def stage_precast(P, jobs, name="pc"):
    stg = [P.persist(f"{name}_s{i}", [128, 1024], F32) for i in range(1)]
    obf = [P.persist(f"{name}_o{i}", [128, 1024], BF16) for i in range(2)]
    gc = P.persist(f"{name}_g", [128, 16], F32)
    for (src, dst, gcol) in jobs:
        R, C = src.h.shape if hasattr(src.h, "shape") else (None, None)
        R, C = dst.h.shape
        ng = 0
        if gcol is not None:
            nd_ = R // 128
            for k in range(nd_):
                P.dma("scalar", gc[:, k:k+1], gcol[k*128:(k+1)*128, :], reads=[gcol], writes=[gc])
        i = 0
        for r0 in range(0, R, 128):
            for c0 in range(0, C, 1024):
                cw = min(1024, C - c0)
                b = i % 2; i += 1
                S_, O_ = stg[0], obf[b]
                P.dma("scalar", S_[0:128, 0:cw], src[r0:r0+128, c0:c0+cw], reads=[src], writes=[S_])
                if gcol is not None:
                    k = r0 // 128
                    P.task("scalar", lambda e, S_=S_, O_=O_, cw=cw, k=k: e.activation(
                        O_[0:128, 0:cw], S_[0:128, 0:cw], AF.Copy, scale=gc[:, k:k+1]),
                        reads=[S_, gc], writes=[O_])
                else:
                    P.task("scalar", lambda e, S_=S_, O_=O_, cw=cw: e.copy(O_[0:128, 0:cw], S_[0:128, 0:cw]),
                           reads=[S_], writes=[O_])
                P.dma("gpsimd", dst[r0:r0+128, c0:c0+cw], O_[0:128, 0:cw], reads=[O_], writes=[dst])

def bcast_row(P, c, name, row_dram_or_buf, N, f32=True):
    row = Buf(P.rowtmp.ap()[0:1, 0:N], f"{name}_row", state=P.rowtmp)
    src = row_dram_or_buf.ap() if isinstance(row_dram_or_buf, Buf) else row_dram_or_buf
    P.dma("sync", row[0:1, 0:N], src,
          reads=[row_dram_or_buf] if isinstance(row_dram_or_buf, Buf) else [], writes=[row])
    ps = P.pf(2, 0, min(N, 512), name=f"{name}_ps")
    out = P.sbuf(f"{name}_bc", [128, N], F32)
    for j0 in range(0, N, 512):
        j1 = min(j0 + 512, N)
        P.task("tensor", lambda e, j0=j0, j1=j1: e.matmul(ps.ap()[:, 0:j1-j0], c["ones_row"].ap(),
                                                          P.rowtmp.ap()[0:1, j0:j1], start=True, stop=True),
               reads=[row, c["ones_row"]], writes=[ps])
        P.task("scalar", lambda e, j0=j0, j1=j1: e.copy(out[:, j0:j1], ps.ap()[:, 0:j1-j0]),
               reads=[ps], writes=[out])
    return out

def rms_cols(P, name_bufs, X_ap, reads, eps, D_):
    sq, cols = name_bufs
    P.task("scalar", lambda e: e.activation(sq.ap(), X_ap, AF.Square, accum_out=cols[:, 0:1]),
           reads=reads, writes=[sq, cols])
    P.task("scalar", lambda e: e.activation(cols[:, 1:2], cols[:, 0:1], AF.Ln, bias=P.constf(eps), scale=P.constf(1.0/D_)),
           reads=[cols], writes=[cols], strict=True)
    P.task("scalar", lambda e: e.activation(cols[:, 2:3], cols[:, 1:2], AF.Exp, scale=P.constf(-0.5)),
           reads=[cols], writes=[cols], strict=True)

def stage_outproj_rms_gen(P, c, W, y_fm_bf, xn_tm, zmam_tm, zmam_fm_bf, name="s4a"):
    opj_bf = load_cast_bf(P, f"{name}_opj", W["out_projT"], 2048, 1024)
    n1w = bcast_row(P, c, f"{name}_n1w", W["norm1_w"], 1024)
    ybig = [P.sbuf(f"{name}_yb{i}", [128, 16 * 512], BF16) for i in range(1)]
    pm = [P.pf(0), P.pf(1)]
    pre = [P.sbuf(f"{name}_pre{i}", [128, 1024], F32) for i in range(2)]
    sq = P.sbuf(f"{name}_sq", [128, 1024], F32)
    cols = P.sbuf(f"{name}_cols", [128, 4], F32)
    zbf = [P.sbuf(f"{name}_zbf{i}", [128, 1024], BF16) for i in range(2)]
    tp = [Buf(P.psum_f[2].h.bitcast(BF16).ap()[:, 0:1024], f"{name}_tp0", state=P.psum_f[2]),
          Buf(P.psum_f[3].h.bitcast(BF16).ap()[:, 0:1024], f"{name}_tp1", state=P.psum_f[3])]
    tpo = [P.sbuf(f"{name}_tpo{i}", [128, 1024], BF16) for i in range(2)]
    for i in range(TLOC // 128):
        b = i % 2
        PM, PRE, ZBF = pm[b], pre[b], zbf[b]
        ch = i // 4
        sub = i % 4
        YB = ybig[0]
        if sub == 0:
            P.dma("sync", YB.ap().rearrange("p (k c) -> p k c", k=16),
                  y_fm_bf[0:2048, ch*512:(ch+1)*512].rearrange("(k p) c -> p k c", p=128),
                  reads=[y_fm_bf], writes=[YB])
        P.dma("sync", PRE.ap(), xn_tm[i*128:(i+1)*128, :], reads=[xn_tm], writes=[PRE])
        def mm(e, PM=PM, YB=YB, sub=sub):
            ins = None
            for half in range(2):
                for k in range(16):
                    ins = e.matmul(PM.ap()[:, half*512:(half+1)*512],
                                   YB[:, k*512 + sub*128 : k*512 + (sub+1)*128],
                                   wtile(opj_bf, k, 1024, half*512, (half+1)*512),
                                   start=(k == 0), stop=(k == 15))
            return ins
        P.task("tensor", mm, reads=[YB, opj_bf], writes=[PM])
        P.task("vector", lambda e, PM=PM, PRE=PRE: e.scalar_tensor_tensor(
            PRE.ap(), PM.ap(), 1.0, PRE.ap(), ALU.mult, ALU.add), reads=[PM, PRE], writes=[PRE])
        rms_cols(P, (sq, cols), PRE.ap(), [PRE], 1e-6, 1024)
        def rmsap(e, PRE=PRE):
            e.tensor_scalar(PRE.ap(), PRE.ap(), cols[:, 2:3], None, ALU.mult)
            return e.tensor_tensor(PRE.ap(), PRE.ap(), n1w.ap(), ALU.mult)
        P.task("vector", rmsap, reads=[PRE, cols, n1w], writes=[PRE])
        P.dma("gpsimd", zmam_tm[i*128:(i+1)*128, :], PRE.ap(), reads=[PRE], writes=[zmam_tm])
        P.task("scalar", lambda e, PRE=PRE, ZBF=ZBF: e.copy(ZBF.ap(), PRE.ap()), reads=[PRE], writes=[ZBF])
        TPB, TOB = tp[b], tpo[b]
        def trp(e, ZBF=ZBF, TPB=TPB):
            ins = None
            for j in range(8):
                ins = e.transpose(TPB[:, j*128:(j+1)*128], ZBF[:, j*128:(j+1)*128], c["ident_bf"].ap())
            return ins
        P.task("tensor", trp, reads=[ZBF, c["ident_bf"]], writes=[TPB])
        P.task("scalar", lambda e, TPB=TPB, TOB=TOB: e.copy(TOB.ap(), TPB.ap()), reads=[TPB], writes=[TOB])
        for j in range(8):
            P.dma("gpsimd", zmam_fm_bf[j*128:(j+1)*128, i*128:(i+1)*128], TOB[:, j*128:(j+1)*128],
                  reads=[TOB], writes=[zmam_fm_bf])
        if i % 4 == 3:
            yield i // 4

def ffn_fm(P, c, name, x_fm_bf, w1T_dram, b1_dram, w2T_dram, b2_row_dram, DIN, DHID, DOUT,
           out_cb, gamma_col_dram=None, act="gelu", TC=None, sc_prolog=None, pre_bf16=False):
    nd, nf = DIN // 128, DHID // 128
    SC = 512
    if gamma_col_dram is not None and not pre_bf16:
        gcol = P.sbuf(f"{name}_gcol", [128, nd], F32)
        for k in range(nd):
            P.dma("sync", gcol[:, k:k+1], gamma_col_dram[k*128:(k+1)*128, :],
                  reads=[gamma_col_dram], writes=[gcol])
    w1bf = P.sbuf(f"{name}_w1bf", [128, nd * DHID], BF16)
    if pre_bf16:
        for k in range(nd):
            P.dma("sync", w1bf[:, k*DHID:(k+1)*DHID], w1T_dram[k*128:(k+1)*128, :],
                  reads=[w1T_dram], writes=[w1bf])
        w2bf = P.sbuf(f"{name}_w2bf", [128, (DHID // 128) * DOUT], BF16)
        for fidx in range(DHID // 128):
            P.dma("sync", w2bf[:, fidx*DOUT:(fidx+1)*DOUT], w2T_dram[fidx*128:(fidx+1)*128, :],
                  reads=[w2T_dram], writes=[w2bf])
    else:
        CW = min(1024, DHID)
        st2_ = [P.sbuf(f"{name}_st{i}", [128, CW], F32) for i in range(2)]
        st = st2_[0]
        ii = 0
        for k in range(nd):
            for c0 in range(0, DHID, CW):
                S_ = st2_[ii % 2]; ii += 1
                P.dma("sync", S_[0:128, 0:CW], w1T_dram[k*128:(k+1)*128, c0:c0+CW],
                      reads=[w1T_dram], writes=[S_])
                if gamma_col_dram is not None:
                    P.task("scalar", lambda e, k=k, c0=c0, S_=S_: e.activation(
                        w1bf[:, k*DHID + c0 : k*DHID + c0 + CW], S_[0:128, 0:CW],
                        AF.Copy, scale=gcol[:, k:k+1]), reads=[S_, gcol], writes=[w1bf])
                else:
                    P.task("scalar", lambda e, k=k, c0=c0, S_=S_: e.copy(
                        w1bf[:, k*DHID + c0 : k*DHID + c0 + CW], S_[0:128, 0:CW]),
                        reads=[S_], writes=[w1bf])
        stw = 2048 if DHID >= 2048 else DHID
        w2bf = load_cast_bf(P, f"{name}_w2", w2T_dram, DHID, DOUT,
                            stage_bufs=[st] if stw >= DOUT else None)
    b1c = P.sbuf(f"{name}_b1c", [128, nf], F32)
    for f in range(nf):
        P.dma("sync", b1c[:, f:f+1], b1_dram[f*128:(f+1)*128, :], reads=[b1_dram], writes=[b1c])
    b2b = bcast_row(P, c, f"{name}_b2", b2_row_dram, DOUT)
    xta = P.sbuf(f"{name}_xta", [128, nd * SC], BF16)
    xt = [Buf(xta.ap()[:, k*SC:(k+1)*SC], f"{name}_xt{k}", state=xta) for k in range(nd)]
    hid = P.sbuf(f"{name}_hid", [128, nf * SC], BF16)
    ph8 = [P.pf(2, 0, 512), P.pf(2, 512, 1024), P.pf(3, 0, 512), P.pf(3, 512, 1024),
           P.pf(0, 0, 512), P.pf(0, 512, 1024), P.pf(1, 0, 512), P.pf(1, 512, 1024)]
    po = [P.pf(0, 0, DOUT), P.pf(1, 0, DOUT)]
    for sc in range(TLOC // SC):
        t0 = sc * SC
        if sc_prolog is not None:
            sc_prolog(sc)
        P.dma("scalar", xta.ap().rearrange("p (k c) -> p k c", k=nd),
              x_fm_bf[0:nd*128, t0:t0+SC].rearrange("(k p) c -> p k c", p=128),
              reads=[x_fm_bf], writes=[xta])
        for f0 in range(0, nf, 4):
            nfg = min(4, nf - f0)
            PH4 = ph8[((f0 // 4) % 2) * 4 : ((f0 // 4) % 2) * 4 + 4]
            def mm1(e, f0=f0, nfg=nfg, PH4=PH4):
                ins = None
                for fi in range(nfg):
                    f = f0 + fi
                    for k in range(nd):
                        ins = e.matmul(PH4[fi].ap(), w1bf[:, k*DHID + f*128 : k*DHID + (f+1)*128],
                                       xt[k].ap(), start=(k == 0), stop=(k == nd - 1))
                return ins
            P.task("tensor", mm1, reads=[w1bf] + xt, writes=PH4)
            def gel(e, f0=f0, nfg=nfg, PH4=PH4):
                ins = None
                for fi in range(nfg):
                    f = f0 + fi
                    ins = e.activation(hid[:, f*SC:(f+1)*SC], PH4[fi].ap(),
                                       AF.Gelu, bias=b1c[:, f:f+1])
                return ins
            P.task("scalar", gel, reads=PH4 + [b1c], writes=[hid])
        for s2 in range(SC // 128):
            i = (t0 + s2*128) // 128
            PO = po[s2 % 2]
            def mm2(e, s2=s2, PO=PO):
                ins = None
                for half in range(0, DOUT, 512):
                    for f in range(nf):
                        ins = e.matmul(PO.ap()[:, half:half+512], hid[:, f*SC + s2*128 : f*SC + (s2+1)*128],
                                       wtile(w2bf, f, DOUT, half, half+512), start=(f == 0), stop=(f == nf - 1))
                return ins
            P.task("tensor", mm2, reads=[hid, w2bf], writes=[PO])
            out_cb(i, PO, b2b)

def stage_mlp(P, c, W, zmam_fm_bf, zmam_tm, mlpz_tm, name="s4b"):
    zt = [P.sbuf(f"{name}_zt{i}", [128, 1024], F32) for i in range(2)]
    mo = [P.sbuf(f"{name}_mo{i}", [128, 1024], F32) for i in range(1)]
    def cb(i, PO, b2b):
        b = i % 2
        ZT, MO = zt[b], mo[0]
        P.dma("sync", ZT.ap(), zmam_tm[i*128:(i+1)*128, :], reads=[zmam_tm], writes=[ZT])
        P.task("vector", lambda e, PO=PO, MO=MO: e.scalar_tensor_tensor(
            MO.ap(), PO.ap(), 1.0, b2b.ap(), ALU.mult, ALU.add), reads=[PO, b2b], writes=[MO])
        P.task("vector", lambda e, MO=MO, ZT=ZT: e.tensor_tensor(MO.ap(), MO.ap(), ZT.ap(), ALU.add),
               reads=[MO, ZT], writes=[MO])
        P.dma("gpsimd", mlpz_tm[i*128:(i+1)*128, :], MO.ap(), reads=[MO], writes=[mlpz_tm])
    ffn_fm(P, c, name, zmam_fm_bf, W["mlp_w1T"], W["mlp_b1"], W["mlp_w2T"], W["mlp_b2"],
           1024, 4096, 1024, cb)

def stage_rms_out(P, c, W, mlpz_tm, attn0_tm, name="s4c"):
    n1w = bcast_row(P, c, f"{name}_n1w", W["norm1_w"], 1024)
    xt = [P.sbuf(f"{name}_x{i}", [128, 1024], F32) for i in range(2)]
    sq = P.sbuf(f"{name}_sq", [128, 1024], F32)
    cols = P.sbuf(f"{name}_cols", [128, 4], F32)
    for i in range(TLOC // 128):
        X = xt[i % 2]
        P.dma("sync", X.ap(), mlpz_tm[i*128:(i+1)*128, :], reads=[mlpz_tm], writes=[X])
        rms_cols(P, (sq, cols), X.ap(), [X], 1e-6, 1024)
        def rmsap(e, X=X):
            e.tensor_scalar(X.ap(), X.ap(), cols[:, 2:3], None, ALU.mult)
            return e.tensor_tensor(X.ap(), X.ap(), n1w.ap(), ALU.mult)
        P.task("vector", rmsap, reads=[X, cols, n1w], writes=[X])
        P.dma("gpsimd", attn0_tm[i*128:(i+1)*128, :], X.ap(), reads=[X], writes=[attn0_tm])

def stage_router(P, c, W, xn_tm, wgt_all, name="rt"):
    rwb = []
    for e8 in range(8):
        rwb.append(bcast_row(P, c, f"{name}_rw{e8}", W["router_w"].ap()[e8:e8+1, :], 1024))
    xt = [P.sbuf(f"{name}_x{i}", [128, 1024], F32) for i in range(2)]
    junk = P.sbuf(f"{name}_junk", [128, 1024], F32)
    L = P.sbuf(f"{name}_L", [128, 8], F32)
    E = P.sbuf(f"{name}_E", [128, 8], F32)
    CN = P.sbuf(f"{name}_CN", [128, 8], F32)
    TMPQ = P.sbuf(f"{name}_TQ", [128, 8], F32)
    cols = P.sbuf(f"{name}_cols", [128, 4], F32)
    for i in range(TLOC // 128):
        X = xt[i % 2]
        P.dma("sync", X.ap(), xn_tm[i*128:(i+1)*128, :], reads=[xn_tm], writes=[X])
        for e8 in range(8):
            P.task("vector", lambda e, X=X, e8=e8: e.scalar_tensor_tensor(
                junk.ap(), X.ap(), 1.0, rwb[e8].ap(), ALU.mult, ALU.mult, accum_out=L[:, e8:e8+1]),
                reads=[X, rwb[e8]], writes=[junk, L])
        P.task("vector", lambda e: e.tensor_reduce(cols[:, 0:1], L.ap(), AX.X, ALU.max), reads=[L], writes=[cols])
        P.task("vector", lambda e: e.tensor_scalar(cols[:, 1:2], cols[:, 0:1], -1.0, None, ALU.mult),
               reads=[cols], writes=[cols], strict=True)
        P.task("scalar", lambda e: e.activation(E.ap(), L.ap(), AF.Exp, bias=cols[:, 1:2],
                                                 accum_out=cols[:, 2:3]),
               reads=[L, cols], writes=[E, cols])
        P.task("vector", lambda e: e.reciprocal(cols[:, 3:4], cols[:, 2:3]), reads=[cols], writes=[cols], strict=True)
        P.task("vector", lambda e: e.tensor_scalar(E.ap(), E.ap(), cols[:, 3:4], None, ALU.mult),
               reads=[E, cols], writes=[E], strict=True)
        P.task("vector", lambda e: e.memset(CN.ap(), 0.0), writes=[CN])
        for dlt in range(1, 8):
            P.task("vector", lambda e, dlt=dlt: e.tensor_tensor(TMPQ[:, 0:8-dlt], E[:, dlt:8], E[:, 0:8-dlt], ALU.is_gt),
                   reads=[E], writes=[TMPQ], strict=True)
            P.task("vector", lambda e, dlt=dlt: e.tensor_tensor(CN[:, 0:8-dlt], CN[:, 0:8-dlt], TMPQ[:, 0:8-dlt], ALU.add),
                   reads=[CN, TMPQ], writes=[CN], strict=True)
            P.task("vector", lambda e, dlt=dlt: e.tensor_tensor(TMPQ[:, 8-dlt:8], E[:, 0:dlt], E[:, 8-dlt:8], ALU.is_ge),
                   reads=[E], writes=[TMPQ], strict=True)
            P.task("vector", lambda e, dlt=dlt: e.tensor_tensor(CN[:, 8-dlt:8], CN[:, 8-dlt:8], TMPQ[:, 8-dlt:8], ALU.add),
                   reads=[CN, TMPQ], writes=[CN], strict=True)
        P.task("vector", lambda e: e.tensor_scalar(CN.ap(), CN.ap(), 2.0, None, ALU.is_lt),
               reads=[CN], writes=[CN], strict=True)
        P.task("vector", lambda e, i=i: e.tensor_tensor(wgt_all[:, i*8:(i+1)*8], CN.ap(), E.ap(), ALU.mult),
               reads=[CN, E], writes=[wgt_all], strict=True)

def stage_moe(P, c, W, xn_fm_bf, wgt_all, name="moe"):
    moe_acc = P.dram_tmp(f"{name}_acc", [TLOC, 1024], F32)
    for ex in range(8):
        mk = P.arena_save()
        nta = P.sbuf(f"{name}_nta{ex}", [128, 2 * 1024], F32)
        nt_ = [Buf(nta.ap()[:, (s2 % 2)*1024:((s2 % 2)+1)*1024], f"{name}_n{ex}_{s2}", state=nta) for s2 in range(4)]
        def pre(sc, ex=ex, nta=nta, half=0):
            if ex == 0:
                return
            P.dma("gpsimd", nta.ap().rearrange("p (s c) -> p s c", s=2),
                  moe_acc.ap()[sc*512 + half*256 : sc*512 + half*256 + 256, :].rearrange("(s p) c -> p s c", p=128),
                  reads=[moe_acc], writes=[nta])
        def cb(i, PO, b2b, ex=ex, nt_=nt_):
            NT = nt_[i % 4]
            if ex > 0:
                P.task("vector", lambda e, PO=PO, NT=NT, i=i, ex=ex: e.scalar_tensor_tensor(
                    NT.ap(), PO.ap(), wgt_all[:, i*8+ex:i*8+ex+1], NT.ap(), ALU.mult, ALU.add),
                    reads=[PO, wgt_all, NT], writes=[NT])
            else:
                P.task("vector", lambda e, PO=PO, NT=NT, i=i, ex=ex: e.tensor_scalar(
                    NT.ap(), PO.ap(), wgt_all[:, i*8+ex:i*8+ex+1], None, ALU.mult),
                    reads=[PO, wgt_all], writes=[NT])
            P.task("vector", lambda e, NT=NT, i=i, ex=ex, b2b=b2b: e.scalar_tensor_tensor(
                NT.ap(), b2b.ap(), wgt_all[:, i*8+ex:i*8+ex+1], NT.ap(), ALU.mult, ALU.add),
                reads=[b2b, wgt_all, NT], writes=[NT])
            if i % 2 == 1:
                P.dma("gpsimd", moe_acc.ap()[(i-1)*128:(i+1)*128, :].rearrange("(s p) c -> p s c", p=128),
                      nta.ap().rearrange("p (s c) -> p s c", s=2),
                      reads=[nta], writes=[moe_acc])
                if i % 4 == 1 and ex > 0:
                    P.dma("gpsimd", nta.ap().rearrange("p (s c) -> p s c", s=2),
                          moe_acc.ap()[(i+1)*128:(i+3)*128, :].rearrange("(s p) c -> p s c", p=128),
                          reads=[moe_acc], writes=[nta])
        ffn_fm(P, c, f"{name}{ex}", xn_fm_bf,
               Buf(W["moe_w1"].ap()[ex*1024:(ex+1)*1024, :], "w1s", state=W["moe_w1"]),
               Buf(W["moe_b1"].ap()[ex*4096:(ex+1)*4096, :], "b1s", state=W["moe_b1"]),
               Buf(W["moe_w2"].ap()[ex*4096:(ex+1)*4096, :], "w2s", state=W["moe_w2"]),
               Buf(W["moe_b2"].ap()[ex:ex+1, :], "b2s", state=W["moe_b2"]),
               1024, 4096, 1024, cb, sc_prolog=pre)
        P.arena_restore(mk)
    return moe_acc

def stage_x1(P, c, moe_acc, x_tm, attn0_tm, x1_tm, name="x1f"):
    xt = [P.sbuf(f"{name}_fx{i}", [128, 1024], F32) for i in range(2)]
    a0 = [P.sbuf(f"{name}_fa{i}", [128, 1024], F32) for i in range(2)]
    mc = [P.sbuf(f"{name}_fm{i}", [128, 1024], F32) for i in range(2)]
    for i in range(TLOC // 128):
        b = i % 2
        XT, A0, MC = xt[b], a0[b], mc[b]
        P.dma("sync", XT.ap(), x_tm[i*128:(i+1)*128, :], reads=[x_tm], writes=[XT])
        P.dma("sync", A0.ap(), attn0_tm[i*128:(i+1)*128, :], reads=[attn0_tm], writes=[A0])
        P.dma("sync", MC.ap(), moe_acc.ap()[i*128:(i+1)*128, :], reads=[moe_acc], writes=[MC])
        P.task("vector", lambda e, XT=XT, A0=A0: e.tensor_tensor(XT.ap(), XT.ap(), A0.ap(), ALU.add),
               reads=[XT, A0], writes=[XT])
        P.task("vector", lambda e, XT=XT, MC=MC: e.tensor_tensor(XT.ap(), XT.ap(), MC.ap(), ALU.add),
               reads=[XT, MC], writes=[XT])
        P.dma("gpsimd", x1_tm[i*128:(i+1)*128, :], XT.ap(), reads=[XT], writes=[x1_tm])

def stage_pe_h(P, c, W, xn1_tm, h_tm, h_fm_bf, name="pe"):
    pet = [P.sbuf(f"{name}_pe{i}", [128, 1024], F32) for i in range(2)]
    xt = [P.sbuf(f"{name}_x{i}", [128, 1024], F32) for i in range(2)]
    hbf = [P.sbuf(f"{name}_hbf{i}", [128, 1024], BF16) for i in range(2)]
    tp = [Buf(P.psum_f[2].h.bitcast(BF16).ap()[:, 0:1024], f"{name}_tp0", state=P.psum_f[2]),
          Buf(P.psum_f[3].h.bitcast(BF16).ap()[:, 0:1024], f"{name}_tp1", state=P.psum_f[3])]
    tpo = [P.sbuf(f"{name}_tpo{i}", [128, 1024], BF16) for i in range(2)]
    for i in range(TLOC // 128):
        b = i % 2
        XT, PT, HB = xt[b], pet[b], hbf[b]
        P.dma("sync", XT.ap(), xn1_tm[i*128:(i+1)*128, :], reads=[xn1_tm], writes=[XT])
        P.dma("sync", PT.ap(), W["pe_tm"].ap()[(i % 16)*128:((i % 16)+1)*128, :], reads=[W["pe_tm"]], writes=[PT])
        P.task("vector", lambda e, XT=XT, PT=PT: e.tensor_tensor(XT.ap(), XT.ap(), PT.ap(), ALU.add),
               reads=[XT, PT], writes=[XT])
        P.dma("gpsimd", h_tm[i*128:(i+1)*128, :], XT.ap(), reads=[XT], writes=[h_tm])
        P.task("scalar", lambda e, XT=XT, HB=HB: e.copy(HB.ap(), XT.ap()), reads=[XT], writes=[HB])
        TPB, TOB = tp[b], tpo[b]
        def trp(e, HB=HB, TPB=TPB):
            ins = None
            for j in range(8):
                ins = e.transpose(TPB[:, j*128:(j+1)*128], HB[:, j*128:(j+1)*128], c["ident_bf"].ap())
            return ins
        P.task("tensor", trp, reads=[HB, c["ident_bf"]], writes=[TPB])
        P.task("scalar", lambda e, TPB=TPB, TOB=TOB: e.copy(TOB.ap(), TPB.ap()), reads=[TPB], writes=[TOB])
        for j in range(8):
            P.dma("gpsimd", h_fm_bf[j*128:(j+1)*128, i*128:(i+1)*128], TOB[:, j*128:(j+1)*128],
                  reads=[TOB], writes=[h_fm_bf])

def stage_qkv(P, c, W, h_fm_bf, q_fm_bf, k_fm_bf, v_tm_bf, name="qkv"):
    wq = load_cast_bf(P, f"{name}_wq", W["Wq"], 1024, 1024)
    wk = load_cast_bf(P, f"{name}_wk", W["Wk"], 1024, 1024)
    wv = load_cast_bf(P, f"{name}_wv", W["Wv"], 1024, 1024)
    bqc = P.sbuf(f"{name}_bq", [128, 8], F32)
    bkc = P.sbuf(f"{name}_bk", [128, 8], F32)
    for j in range(8):
        P.dma("sync", bqc[:, j:j+1], W["bq"].ap()[j*128:(j+1)*128, :], reads=[W["bq"]], writes=[bqc])
        P.dma("sync", bkc[:, j:j+1], W["bk"].ap()[j*128:(j+1)*128, :], reads=[W["bk"]], writes=[bkc])
    bvb = bcast_row(P, c, f"{name}_bv", W["bv_row"], 1024)
    TC = 512
    hta = P.sbuf(f"{name}_hta", [128, 8 * TC], BF16)
    ht = [Buf(hta.ap()[:, k*TC:(k+1)*TC], f"{name}_h{k}", state=hta) for k in range(8)]
    pm = [P.pf(0, 0, TC), P.pf(1, 0, TC)]
    ob = P.sbuf(f"{name}_ob", [128, TC], BF16)
    obf = P.sbuf(f"{name}_obf", [128, TC], F32)
    for ch in range(TLOC // TC):
        t0 = ch * TC
        P.dma("sync", hta.ap().rearrange("p (k c) -> p k c", k=8),
              h_fm_bf[0:1024, t0:t0+TC].rearrange("(k p) c -> p k c", p=128),
              reads=[h_fm_bf], writes=[hta])
        for (wbf, bc, dst) in ((wq, bqc, q_fm_bf), (wk, bkc, k_fm_bf)):
            for et in range(8):
                PM = pm[et % 2]
                def mm(e, wbf=wbf, et=et, PM=PM):
                    ins = None
                    for dk in range(8):
                        ins = e.matmul(PM.ap(), wtile(wbf, dk, 1024, et*128, (et+1)*128), ht[dk].ap(),
                                       start=(dk == 0), stop=(dk == 7))
                    return ins
                P.task("tensor", mm, reads=[wbf] + ht, writes=[PM])
                P.task("vector", lambda e, PM=PM, bc=bc, et=et: e.tensor_scalar(
                    obf.ap(), PM.ap(), bc[:, et:et+1], None, ALU.add), reads=[PM, bc], writes=[obf])
                P.task("scalar", lambda e: e.copy(ob.ap(), obf.ap()), reads=[obf], writes=[ob])
                P.dma("gpsimd", dst[et*128:(et+1)*128, t0:t0+TC], ob.ap(), reads=[ob], writes=[dst])
    vt = [P.sbuf(f"{name}_vt{i}", [128, 1024], F32) for i in range(2)]
    vb = [P.sbuf(f"{name}_vb{i}", [128, 1024], BF16) for i in range(2)]
    pmv = [P.pf(0), P.pf(1)]
    hlbig = [P.sbuf(f"{name}_hlb{i}", [128, 8 * 512], BF16) for i in range(2)]
    for i in range(TLOC // 128):
        b = i % 2
        PM, VT, VB = pmv[b], vt[b], vb[b]
        ch = i // 4
        sub = i % 4
        HL = hlbig[ch % 2]
        if sub == 0:
            P.dma("sync", HL.ap().rearrange("p (k c) -> p k c", k=8),
                  h_fm_bf[0:1024, ch*512:(ch+1)*512].rearrange("(k p) c -> p k c", p=128),
                  reads=[h_fm_bf], writes=[HL])
        def mmv(e, PM=PM, HL=HL, sub=sub):
            ins = None
            for half in range(2):
                for dk in range(8):
                    ins = e.matmul(PM.ap()[:, half*512:(half+1)*512],
                                   HL[:, dk*512 + sub*128 : dk*512 + (sub+1)*128],
                                   wtile(wv, dk, 1024, half*512, (half+1)*512),
                                   start=(dk == 0), stop=(dk == 7))
            return ins
        P.task("tensor", mmv, reads=[wv, HL], writes=[PM])
        P.task("vector", lambda e, PM=PM, VT=VT: e.scalar_tensor_tensor(
            VT.ap(), PM.ap(), 1.0, bvb.ap(), ALU.mult, ALU.add), reads=[PM, bvb], writes=[VT])
        P.task("scalar", lambda e, VT=VT, VB=VB: e.copy(VB.ap(), VT.ap()), reads=[VT], writes=[VB])
        P.dma("gpsimd", v_tm_bf[i*128:(i+1)*128, :], VB.ap(), reads=[VB], writes=[v_tm_bf])

def stage_attn_core_gen(P, c, q_fm_bf, k_fm_bf, v_tm_bf, attnP_fm_bf, name="att"):
    ksl = [P.sbuf(f"{name}_k{b}", [64, SROW], BF16) for b in range(2)]
    qsl = [P.sbuf(f"{name}_q{b}", [64, 512], BF16) for b in range(2)]
    vva = [P.sbuf(f"{name}_vva{b}", [128, 16 * 65], BF16) for b in range(2)]
    vv = [[Buf(vva[b].ap()[:, kt*65:(kt+1)*65], f"{name}_v{b}_{kt}", state=vva[b]) for kt in range(16)] for b in range(2)]
    Ebf = [[P.sbuf(f"{name}_E{b}_{kt}", [128, 512], BF16) for kt in range(16)] for b in range(2)]
    rr = [P.sbuf(f"{name}_rr{b}", [1, 512], F32) for b in range(2)]
    rb = [P.sbuf(f"{name}_rb{b}", [64, 512], F32) for b in range(2)]
    ab = [P.sbuf(f"{name}_ab{b}", [64, 512], BF16) for b in range(2)]
    ps4 = [P.pf(0, 0, 512), P.pf(0, 512, 1024), P.pf(1, 0, 512), P.pf(1, 512, 1024)]
    pav = [P.pf(2, 0, 512, 0, 65, name=f"{name}_pav0"), P.pf(3, 0, 512, 0, 65, name=f"{name}_pav1")]
    prb = [P.pf(2, 512, 1024, 0, 64, name=f"{name}_prb0"), P.pf(3, 512, 1024, 0, 64, name=f"{name}_prb1")]
    for b in range(2):
        for kt in range(16):
            P.task("vector", lambda e, b=b, kt=kt: e.memset(vv[b][kt].ap()[:, 64:65], 1.0), writes=[vv[b][kt]])
    for r in range(2):
        for h in range(16):
            hb = h % 2
            KS, E1, V1 = ksl[hb], Ebf[hb], vv[hb]
            P.dma("scalar", KS.ap(), k_fm_bf[h*64:(h+1)*64, r*SROW:(r+1)*SROW], reads=[k_fm_bf], writes=[KS])
            P.dma("scalar", vva[hb].ap().rearrange("p (k c) -> p k c", k=16)[:, :, 0:64],
                  v_tm_bf[r*SROW:(r+1)*SROW, h*64:(h+1)*64].rearrange("(k p) c -> p k c", p=128),
                  reads=[v_tm_bf], writes=[vva[hb]])
            for qc in range(4):
                q0 = r*SROW + qc*512
                QS = qsl[qc % 2]
                P.dma("sync", QS.ap(), q_fm_bf[h*64:(h+1)*64, q0:q0+512], reads=[q_fm_bf], writes=[QS])
                for k4 in range(0, 16, 4):
                    PSg = [ps4[(k4 // 4 + i) % 4] for i in range(1)]
                    PS4 = [ps4[0], ps4[1], ps4[2], ps4[3]] if k4 == 0 else None
                    def mmsc(e, k4=k4, KS=KS, QS=QS):
                        ins = None
                        for i in range(4):
                            kt = k4 + i
                            ins = e.matmul(ps4[i].ap(), KS[:, kt*128:(kt+1)*128], QS.ap(),
                                           start=True, stop=True)
                        return ins
                    P.task("tensor", mmsc, reads=[KS, QS], writes=ps4)
                    def expv(e, k4=k4, E1=E1):
                        ins = None
                        for i in range(4):
                            kt = k4 + i
                            ins = e.activation(E1[kt].ap(), ps4[i].ap(), AF.Exp, scale=P.constf(0.125))
                        return ins
                    P.task("scalar", expv, reads=ps4, writes=[E1[k4], E1[k4+1], E1[k4+2], E1[k4+3]])
                PAV, PRB = pav[hb], prb[hb]
                def mmav(e, E1=E1, V1=V1, PAV=PAV):
                    ins = None
                    for kt in range(16):
                        ins = e.matmul(PAV.ap(), V1[kt].ap(), E1[kt].ap(), start=(kt == 0), stop=(kt == 15))
                    return ins
                P.task("tensor", mmav, reads=V1 + E1, writes=[PAV])
                RR, RB, AB = rr[hb], rb[hb], ab[hb]
                P.task("vector", lambda e, PAV=PAV, RR=RR: e.reciprocal(RR.ap(), PAV.ap()[64:65, :]),
                       reads=[PAV], writes=[RR])
                P.task("tensor", lambda e, RR=RR, PRB=PRB: e.matmul(PRB.ap(), c["ones_row"][0:1, 0:64], RR.ap(),
                                                    start=True, stop=True),
                       reads=[c["ones_row"], RR], writes=[PRB])
                P.task("scalar", lambda e, PRB=PRB, RB=RB: e.copy(RB.ap(), PRB.ap()), reads=[PRB], writes=[RB])
                P.task("vector", lambda e, PAV=PAV, RB=RB, AB=AB: e.tensor_tensor(AB.ap(), PAV.ap()[0:64, :], RB.ap(), ALU.mult),
                       reads=[PAV, RB], writes=[AB])
                P.dma("gpsimd", attnP_fm_bf[h*64:(h+1)*64, q0:q0+512], AB.ap(), reads=[AB], writes=[attnP_fm_bf])
            yield (r, h)

def stage_wo_ln_gen(P, c, W, attnP_fm_bf, h_tm, attn1_tm, name="wo"):
    wo = load_cast_bf(P, f"{name}_wo", W["Wo"], 1024, 1024)
    bob = bcast_row(P, c, f"{name}_bo", W["bo_row"], 1024)
    gb = bcast_row(P, c, f"{name}_g", W["gamma_row"], 1024)
    bb = bcast_row(P, c, f"{name}_b", W["beta_row"], 1024)
    albig = [P.sbuf(f"{name}_alb{i}", [128, 8 * 512], BF16) for i in range(1)]
    pm = [P.pf(0), P.pf(1)]
    ht = [P.sbuf(f"{name}_h{i}", [128, 1024], F32) for i in range(2)]
    yt = [P.sbuf(f"{name}_y{i}", [128, 1024], F32) for i in range(1)]
    sq = P.sbuf(f"{name}_sq", [128, 1024], F32)
    cols = P.sbuf(f"{name}_cols", [128, 8], F32)
    for i in range(TLOC // 128):
        b = i % 2
        PM, HT, YT = pm[b], ht[b], yt[0]
        ch = i // 4
        sub = i % 4
        AL = albig[0]
        if sub == 0:
            P.dma("sync", AL.ap().rearrange("p (k c) -> p k c", k=8),
                  attnP_fm_bf[0:1024, ch*512:(ch+1)*512].rearrange("(k p) c -> p k c", p=128),
                  reads=[attnP_fm_bf], writes=[AL])
        P.dma("sync", HT.ap(), h_tm[i*128:(i+1)*128, :], reads=[h_tm], writes=[HT])
        def mm(e, PM=PM, AL=AL, sub=sub):
            ins = None
            for half in range(2):
                for k in range(8):
                    ins = e.matmul(PM.ap()[:, half*512:(half+1)*512],
                                   AL[:, k*512 + sub*128 : k*512 + (sub+1)*128],
                                   wtile(wo, k, 1024, half*512, (half+1)*512),
                                   start=(k == 0), stop=(k == 7))
            return ins
        P.task("tensor", mm, reads=[AL, wo], writes=[PM])
        def boh(e, PM=PM, YT=YT, HT=HT):
            e.scalar_tensor_tensor(YT.ap(), PM.ap(), 1.0, bob.ap(), ALU.mult, ALU.add)
            return e.tensor_tensor(YT.ap(), YT.ap(), HT.ap(), ALU.add)
        P.task("vector", boh, reads=[PM, bob, HT], writes=[YT])
        P.task("vector", lambda e, YT=YT: e.tensor_reduce(cols[:, 0:1], YT.ap(), AX.X, ALU.add),
               reads=[YT], writes=[cols])
        P.task("vector", lambda e: e.tensor_scalar(cols[:, 1:2], cols[:, 0:1], -1.0/1024, None, ALU.mult),
               reads=[cols], writes=[cols], strict=True)
        P.task("vector", lambda e, YT=YT: e.tensor_scalar(YT.ap(), YT.ap(), cols[:, 1:2], None, ALU.add),
               reads=[YT, cols], writes=[YT], strict=True)
        P.task("scalar", lambda e, YT=YT: e.activation(sq.ap(), YT.ap(), AF.Square, accum_out=cols[:, 2:3]),
               reads=[YT], writes=[sq, cols])
        P.task("scalar", lambda e: e.activation(cols[:, 3:4], cols[:, 2:3], AF.Ln, bias=P.constf(1e-5), scale=P.constf(1.0/1024)),
               reads=[cols], writes=[cols], strict=True)
        P.task("scalar", lambda e: e.activation(cols[:, 4:5], cols[:, 3:4], AF.Exp, scale=P.constf(-0.5)),
               reads=[cols], writes=[cols], strict=True)
        def lnap(e, YT=YT):
            e.tensor_scalar(YT.ap(), YT.ap(), cols[:, 4:5], None, ALU.mult)
            e.tensor_tensor(YT.ap(), YT.ap(), gb.ap(), ALU.mult)
            return e.tensor_tensor(YT.ap(), YT.ap(), bb.ap(), ALU.add)
        P.task("vector", lnap, reads=[YT, cols, gb, bb], writes=[YT])
        P.dma("gpsimd", attn1_tm[i*128:(i+1)*128, :], YT.ap(), reads=[YT], writes=[attn1_tm])
        yield i

def stage_route_gen(P, c, W, xn1_tm, normed_fm_bf, mask_cols, name="rte"):
    SQ1024 = float(np.float32(np.sqrt(np.float32(1024.0))))
    rtb = bcast_row(P, c, f"{name}_rt", W["rt_row"], 1024)
    iota_bc_ps = [P.pf(0, 0, 512), P.pf(1, 0, 512)]
    iota_bc = P.sbuf(f"{name}_iota", [128, SROW], F32)
    iota_row = Buf(P.rowtmp.ap()[0:1, 0:SROW], f"{name}_ior", state=P.rowtmp)
    idx_col = P.sbuf(f"{name}_idx", [128, 2], F32)
    P.dma("sync", iota_row[0:1, 0:SROW], W["iota_row"].ap(), reads=[W["iota_row"]], writes=[iota_row])
    P.dma("sync", idx_col[:, 0:1], W["idx_col"].ap(), reads=[W["idx_col"]], writes=[idx_col])
    for j0 in range(0, SROW, 512):
        PSI = iota_bc_ps[(j0 // 512) % 2]
        P.task("tensor", lambda e, j0=j0, PSI=PSI: e.matmul(PSI.ap(), c["ones_row"].ap(),
               P.rowtmp.ap()[0:1, j0:j0+512], start=True, stop=True), reads=[c["ones_row"], iota_row], writes=[PSI])
        P.task("scalar", lambda e, j0=j0, PSI=PSI: e.copy(iota_bc[:, j0:j0+512], PSI.ap()),
               reads=[PSI], writes=[iota_bc])
    xt = [P.sbuf(f"{name}_x{i}", [128, 1024], F32) for i in range(2)]
    nb = [P.sbuf(f"{name}_nb{i}", [128, 1024], BF16) for i in range(2)]
    sq = P.sbuf(f"{name}_sq", [128, 1024], F32)
    cols = P.sbuf(f"{name}_cols", [128, 8], F32)
    s_cols = P.sbuf(f"{name}_scols", [128, 32], F32)
    tp = [Buf(P.psum_f[2].h.bitcast(BF16).ap()[:, 0:1024], f"{name}_tp0", state=P.psum_f[2]),
          Buf(P.psum_f[3].h.bitcast(BF16).ap()[:, 0:1024], f"{name}_tp1", state=P.psum_f[3])]
    tpo = [P.sbuf(f"{name}_tpo{i}", [128, 1024], BF16) for i in range(2)]
    for i in range(TLOC // 128):
        b = i % 2
        X, NB = xt[b], nb[b]
        P.dma("sync", X.ap(), xn1_tm[i*128:(i+1)*128, :], reads=[xn1_tm], writes=[X])
        P.task("scalar", lambda e, X=X: e.activation(sq.ap(), X.ap(), AF.Square, accum_out=cols[:, 0:1]),
               reads=[X], writes=[sq, cols])
        P.task("scalar", lambda e: e.activation(cols[:, 1:2], cols[:, 0:1], AF.Ln),
               reads=[cols], writes=[cols], strict=True)
        P.task("scalar", lambda e: e.activation(cols[:, 2:3], cols[:, 1:2], AF.Exp, scale=P.constf(0.5)),
               reads=[cols], writes=[cols], strict=True)
        P.task("vector", lambda e: e.tensor_scalar(cols[:, 3:4], cols[:, 2:3], 1e-12, None, ALU.max),
               reads=[cols], writes=[cols], strict=True)
        P.task("vector", lambda e: e.reciprocal(cols[:, 4:5], cols[:, 3:4]), reads=[cols], writes=[cols], strict=True)
        P.task("vector", lambda e, X=X, i=i: e.scalar_tensor_tensor(
            sq.ap(), X.ap(), 1.0, rtb.ap(), ALU.mult, ALU.mult, accum_out=s_cols[:, i:i+1]),
            reads=[X, rtb], writes=[sq, s_cols])
        P.task("vector", lambda e, X=X: e.tensor_scalar(X.ap(), X.ap(), cols[:, 4:5], SQ1024, ALU.mult, ALU.mult),
               reads=[X, cols], writes=[X], strict=True)
        P.task("scalar", lambda e, X=X, NB=NB: e.copy(NB.ap(), X.ap()), reads=[X], writes=[NB])
        TPB, TOB = tp[b], tpo[b]
        def trp(e, NB=NB, TPB=TPB):
            ins = None
            for j in range(8):
                ins = e.transpose(TPB[:, j*128:(j+1)*128], NB[:, j*128:(j+1)*128], c["ident_bf"].ap())
            return ins
        P.task("tensor", trp, reads=[NB, c["ident_bf"]], writes=[TPB])
        P.task("scalar", lambda e, TPB=TPB, TOB=TOB: e.copy(TOB.ap(), TPB.ap()), reads=[TPB], writes=[TOB])
        for j in range(8):
            P.dma("gpsimd", normed_fm_bf[j*128:(j+1)*128, i*128:(i+1)*128], TOB[:, j*128:(j+1)*128],
                  reads=[TOB], writes=[normed_fm_bf])
        yield ("n", i)
    s_lin = P.dram_tmp(f"{name}_slin", [32, 128], F32)
    pt = P.pf(2, 0, 128, 0, 32, name=f"{name}_pt")
    sro = P.sbuf(f"{name}_sro", [32, 128], F32)
    P.task("tensor", lambda e: e.transpose(pt.ap(), s_cols.ap(), c["ident_f"].ap()), reads=[s_cols, c["ident_f"]], writes=[pt])
    P.task("scalar", lambda e: e.copy(sro.ap(), pt.ap()), reads=[pt], writes=[sro])
    P.dma("gpsimd", s_lin.ap(), sro.ap(), reads=[sro], writes=[s_lin])
    CT = P.sbuf(f"{name}_ct", [64, SROW], F32)
    NBv = P.sbuf(f"{name}_nbv", [64, SROW], F32)
    SB = P.sbuf(f"{name}_sb", [64, SROW], F32)
    EX = P.sbuf(f"{name}_ex", [64, SROW], F32)
    CC = P.sbuf(f"{name}_cc", [64, 8], F32)
    P.task("vector", lambda e: e.memset(CT.ap(), 0.0), writes=[CT])
    P.dma("sync", CT.ap()[0:1, :], s_lin.ap()[0:16, :], reads=[s_lin], writes=[CT])
    P.dma("sync", CT.ap()[32:33, :], s_lin.ap()[16:32, :], reads=[s_lin], writes=[CT])
    P.task("vector", lambda e: e.tensor_scalar(NBv.ap(), CT.ap(), 0.0, None, ALU.max),
           reads=[CT], writes=[NBv])
    INV = INV01
    CONSTLN = float(np.float32(0.1) * np.float32(np.log(np.float32(8.0))))
    yield ("coor0", 0)
    for it in range(50):
        P.task("vector", lambda e: e.tensor_tensor(SB.ap(), CT.ap(), NBv.ap(), ALU.subtract),
               reads=[CT, NBv], writes=[SB])
        P.task("vector", lambda e: e.tensor_reduce(CC[0:64, 0:1], SB.ap(), AX.X, ALU.max),
               reads=[SB], writes=[CC])
        P.task("vector", lambda e: e.tensor_scalar(CC[0:64, 1:2], CC[0:64, 0:1], -INV, None, ALU.mult),
               reads=[CC], writes=[CC], strict=True)
        P.task("scalar", lambda e: e.activation(EX.ap(), SB.ap(), AF.Exp, bias=CC[0:64, 1:2],
                                                scale=P.constf(INV, 64), accum_out=CC[0:64, 2:3]),
               reads=[SB, CC], writes=[EX, CC])
        P.task("scalar", lambda e: e.activation(CC[0:64, 3:4], CC[0:64, 2:3], AF.Ln), reads=[CC], writes=[CC], strict=True)
        P.task("vector", lambda e: e.tensor_scalar(CC[0:64, 4:5], CC[0:64, 0:1], -1.0, CONSTLN, ALU.mult, ALU.add),
               reads=[CC], writes=[CC], strict=True)
        P.task("vector", lambda e: e.tensor_scalar(CC[0:64, 5:6], CC[0:64, 3:4], -float(np.float32(0.1)), None, ALU.mult),
               reads=[CC], writes=[CC], strict=True)
        P.task("vector", lambda e: e.tensor_tensor(CC[0:64, 6:7], CC[0:64, 5:6], CC[0:64, 4:5], ALU.add),
               reads=[CC], writes=[CC], strict=True)
        P.task("vector", lambda e: e.tensor_scalar(NBv.ap(), CT.ap(), CC[0:64, 6:7], 0.0, ALU.add, ALU.max),
               reads=[CT, CC], writes=[NBv], strict=True)
        if it % 5 == 4:
            yield ("coor", it)
    P.task("vector", lambda e: e.tensor_scalar(SB.ap(), CT.ap(), CC[0:64, 6:7], 0.0, ALU.add, ALU.min),
           reads=[CT, CC], writes=[SB], strict=True)
    P.task("scalar", lambda e: e.activation(EX.ap(), SB.ap(), AF.Exp, scale=P.constf(INV, 64)),
           reads=[SB], writes=[EX])
    FLTMIN = float(np.float32(1.17549435e-38))
    P.task("vector", lambda e: e.tensor_scalar(SB.ap(), EX.ap(), FLTMIN, None, ALU.is_ge),
           reads=[EX], writes=[SB])
    P.task("vector", lambda e: e.tensor_tensor(EX.ap(), EX.ap(), SB.ap(), ALU.mult),
           reads=[EX, SB], writes=[EX])
    Sfree = P.sbuf(f"{name}_sf", [128, SROW], F32)
    sccol = P.sbuf(f"{name}_sc", [128, 16], F32)
    gsum = P.sbuf(f"{name}_gs", [128, 8], F32)
    mlt = P.sbuf(f"{name}_ml", [128, SROW], F32)
    junk2 = mlt
    pr0 = P.pf(0, 0, 512); pr1 = P.pf(1, 0, 512)
    prt = P.pf(2, 0, 64, 0, 128, name=f"{name}_prt")
    for r in range(2):
        base = 32 * r
        for j0 in range(0, SROW, 512):
            PS = pr0 if (j0 // 512) % 2 == 0 else pr1
            P.task("tensor", lambda e, j0=j0, PS=PS, base=base: e.matmul(
                PS.ap(), c["ones_all"][base:base+1, :], EX[base:base+1, j0:j0+512], start=True, stop=True),
                reads=[c["ones_all"], EX], writes=[PS])
            P.task("scalar", lambda e, j0=j0, PS=PS: e.copy(Sfree[:, j0:j0+512], PS.ap()),
                   reads=[PS], writes=[Sfree])
        for ic in range(16):
            P.task("tensor", lambda e, ic=ic, base=base: e.matmul(
                prt.ap()[:, 0:1], EX[base:base+1, ic*128:(ic+1)*128], c["ones_all"][base:base+1, 0:1],
                start=True, stop=True), reads=[EX, c["ones_all"]], writes=[prt])
            P.task("scalar", lambda e, ic=ic: e.copy(sccol[:, ic:ic+1], prt.ap()[:, 0:1]),
                   reads=[prt], writes=[sccol])
        for ic in range(16):
            P.task("vector", lambda e, ic=ic, r=r: e.tensor_scalar(idx_col[:, 1:2], idx_col[:, 0:1],
                   float(ic*128), None, ALU.add), reads=[idx_col], writes=[idx_col])
            P.task("vector", lambda e: e.tensor_scalar(mlt.ap(), iota_bc.ap(), idx_col[:, 1:2], None, ALU.is_lt),
                   reads=[iota_bc, idx_col], writes=[mlt], strict=True)
            P.task("vector", lambda e, ic=ic: e.scalar_tensor_tensor(junk2.ap(), Sfree.ap(), sccol[:, ic:ic+1],
                   mlt.ap(), ALU.is_equal, ALU.mult, accum_out=gsum[:, 0:1]),
                   reads=[Sfree, sccol, mlt], writes=[junk2, gsum])
            P.task("vector", lambda e, ic=ic: e.tensor_scalar(junk2.ap(), Sfree.ap(), sccol[:, ic:ic+1],
                   0.0, ALU.is_gt, ALU.add, accum_out=gsum[:, 1:2]),
                   reads=[Sfree, sccol], writes=[junk2, gsum])
            P.task("vector", lambda e: e.tensor_tensor(gsum[:, 2:3], gsum[:, 0:1], gsum[:, 1:2], ALU.add),
                   reads=[gsum], writes=[gsum], strict=True)
            P.task("vector", lambda e, r=r, ic=ic: e.tensor_scalar(mask_cols[:, r*16+ic:r*16+ic+1],
                   gsum[:, 2:3], 1024.0, None, ALU.is_lt), reads=[gsum], writes=[mask_cols], strict=True)
            yield ("rank", r*16+ic)

def stage_ffn1(P, c, W, normed_fm_bf, mask_cols, x1_tm, attn1_tm, out_tm, name="f1"):
    light_tm = P.dram_tmp(f"{name}_lt", [TLOC, 1024], F32)
    mk = P.arena_save()
    lt = [P.sbuf(f"{name}_l{i}", [128, 1024], F32) for i in range(2)]
    def cbl(i, PO, b2b):
        LT = lt[i % 2]
        P.task("vector", lambda e, PO=PO, LT=LT, b2b=b2b: e.scalar_tensor_tensor(
            LT.ap(), PO.ap(), 1.0, b2b.ap(), ALU.mult, ALU.add), reads=[PO, b2b], writes=[LT])
        P.dma("gpsimd", light_tm.ap()[i*128:(i+1)*128, :], LT.ap(), reads=[LT], writes=[light_tm])
    ffn_fm(P, c, f"{name}l", normed_fm_bf, W["w1_light"], W["b1_light"], W["w2_light"], W["b2_light_row"],
           1024, 512, 1024, cbl, gamma_col_dram=W["gamma_light_col"])
    P.arena_restore(mk)
    xt = [P.sbuf(f"{name}_x{i}", [128, 1024], F32) for i in range(2)]
    a1 = [P.sbuf(f"{name}_a{i}", [128, 1024], F32) for i in range(1)]
    def cbh(i, PO, b2b):
        b = i % 2
        XT, A1 = xt[b], a1[0]
        P.dma("scalar", XT.ap(), x1_tm[i*128:(i+1)*128, :], reads=[x1_tm], writes=[XT])
        P.dma("scalar", A1.ap(), attn1_tm[i*128:(i+1)*128, :], reads=[attn1_tm], writes=[A1])
        P.task("vector", lambda e, XT=XT, A1=A1: e.tensor_tensor(XT.ap(), XT.ap(), A1.ap(), ALU.add),
               reads=[XT, A1], writes=[XT])
        P.dma("scalar", A1.ap(), light_tm.ap()[i*128:(i+1)*128, :], reads=[light_tm], writes=[A1])
        P.task("vector", lambda e, XT=XT, A1=A1: e.tensor_tensor(XT.ap(), XT.ap(), A1.ap(), ALU.add),
               reads=[XT, A1], writes=[XT])
        P.task("vector", lambda e, PO=PO, XT=XT, i=i: e.scalar_tensor_tensor(
            XT.ap(), PO.ap(), mask_cols[:, i:i+1], XT.ap(), ALU.mult, ALU.add),
            reads=[PO, mask_cols, XT], writes=[XT])
        P.task("vector", lambda e, XT=XT, i=i, b2b=b2b: e.scalar_tensor_tensor(
            XT.ap(), b2b.ap(), mask_cols[:, i:i+1], XT.ap(), ALU.mult, ALU.add),
            reads=[b2b, mask_cols, XT], writes=[XT])
        P.dma("gpsimd", out_tm[i*128:(i+1)*128, :], XT.ap(), reads=[XT], writes=[out_tm])
    ffn_fm(P, c, f"{name}h", normed_fm_bf, W["w1_heavy"], W["b1_heavy"], W["w2_heavy"], W["b2_heavy_row"],
           1024, 4096, 1024, cbh, gamma_col_dram=W["gamma_heavy_col"])


def stage_x1_ln1_pe(P, c, W, mlpz_tm, moe_acc, x_tm, x1_tm, xn1_tm, h_tm, h_fm_bf, name="xp"):
    n1w = bcast_row(P, c, f"{name}_n1w", W["norm1_w"], 1024)
    xt = [P.sbuf(f"{name}_x{i}", [128, 1024], F32) for i in range(2)]
    mz = [P.sbuf(f"{name}_mz{i}", [128, 1024], F32) for i in range(2)]
    mc = [P.sbuf(f"{name}_mc{i}", [128, 1024], F32) for i in range(2)]
    pet = [P.sbuf(f"{name}_pe{i}", [128, 1024], F32) for i in range(2)]
    hbf = [P.sbuf(f"{name}_hbf{i}", [128, 1024], BF16) for i in range(2)]
    sq = P.sbuf(f"{name}_sq", [128, 1024], F32)
    cols = [P.sbuf(f"{name}_cols{i}", [128, 8], F32) for i in range(2)]
    tp = [Buf(P.psum_f[2].h.bitcast(BF16).ap()[:, 0:1024], f"{name}_tp0", state=P.psum_f[2]),
          Buf(P.psum_f[3].h.bitcast(BF16).ap()[:, 0:1024], f"{name}_tp1", state=P.psum_f[3])]
    tpo = [P.sbuf(f"{name}_tpo{i}", [128, 1024], BF16) for i in range(2)]
    for i in range(TLOC // 128):
        b = i % 2
        X, MZ, MC, PT, HB, CL = xt[b], mz[b], mc[b], pet[b], hbf[b], cols[b]
        P.dma("sync", MZ.ap(), mlpz_tm[i*128:(i+1)*128, :], reads=[mlpz_tm], writes=[MZ])
        P.dma("sync", X.ap(), x_tm[i*128:(i+1)*128, :], reads=[x_tm], writes=[X])
        P.dma("gpsimd", MC.ap(), moe_acc.ap()[i*128:(i+1)*128, :], reads=[moe_acc], writes=[MC])
        P.dma("scalar", PT.ap(), W["pe_tm"].ap()[(i % 16)*128:((i % 16)+1)*128, :], reads=[W["pe_tm"]], writes=[PT])
        rms_cols(P, (sq, CL), MZ.ap(), [MZ], 1e-6, 1024)
        def a0x1(e, X=X, MZ=MZ, MC=MC, CL=CL):
            e.scalar_tensor_tensor(MZ.ap(), MZ.ap(), CL[:, 2:3], n1w.ap(), ALU.mult, ALU.mult)
            e.tensor_tensor(X.ap(), X.ap(), MZ.ap(), ALU.add)
            return e.tensor_tensor(X.ap(), X.ap(), MC.ap(), ALU.add)
        P.task("vector", a0x1, reads=[X, MZ, MC, CL, n1w], writes=[X, MZ], strict=True)
        P.dma("gpsimd", x1_tm[i*128:(i+1)*128, :], X.ap(), reads=[X], writes=[x1_tm])
        P.task("vector", lambda e, X=X, CL=CL: e.tensor_reduce(CL[:, 0:1], X.ap(), AX.X, ALU.add),
               reads=[X], writes=[CL])
        P.task("vector", lambda e, CL=CL: e.tensor_scalar(CL[:, 1:2], CL[:, 0:1], -1.0/1024, None, ALU.mult),
               reads=[CL], writes=[CL], strict=True)
        P.task("vector", lambda e, X=X, CL=CL: e.tensor_scalar(X.ap(), X.ap(), CL[:, 1:2], None, ALU.add),
               reads=[X, CL], writes=[X], strict=True)
        P.task("scalar", lambda e, X=X, CL=CL: e.activation(sq.ap(), X.ap(), AF.Square, accum_out=CL[:, 2:3]),
               reads=[X], writes=[sq, CL])
        P.task("scalar", lambda e, CL=CL: e.activation(CL[:, 3:4], CL[:, 2:3], AF.Ln, bias=P.constf(1e-5), scale=P.constf(1.0/1024)),
               reads=[CL], writes=[CL], strict=True)
        P.task("scalar", lambda e, CL=CL: e.activation(CL[:, 4:5], CL[:, 3:4], AF.Exp, scale=P.constf(-0.5)),
               reads=[CL], writes=[CL], strict=True)
        def xnh(e, X=X, CL=CL, PT=PT):
            e.tensor_scalar(X.ap(), X.ap(), CL[:, 4:5], None, ALU.mult)
            return e.tensor_tensor(PT.ap(), X.ap(), PT.ap(), ALU.add)
        P.task("vector", xnh, reads=[X, CL, PT], writes=[X, PT], strict=True)
        P.dma("gpsimd", xn1_tm[i*128:(i+1)*128, :], X.ap(), reads=[X], writes=[xn1_tm])
        P.dma("gpsimd", h_tm[i*128:(i+1)*128, :], PT.ap(), reads=[PT], writes=[h_tm])
        P.task("scalar", lambda e, PT=PT, HB=HB: e.copy(HB.ap(), PT.ap()), reads=[PT], writes=[HB])
        TPB, TOB = tp[b], tpo[b]
        def trp(e, HB=HB, TPB=TPB):
            ins = None
            for j in range(8):
                ins = e.transpose(TPB[:, j*128:(j+1)*128], HB[:, j*128:(j+1)*128], c["ident_bf"].ap())
            return ins
        P.task("tensor", trp, reads=[HB, c["ident_bf"]], writes=[TPB])
        P.task("scalar", lambda e, TPB=TPB, TOB=TOB: e.copy(TOB.ap(), TPB.ap()), reads=[TPB], writes=[TOB])
        for j in range(8):
            P.dma("gpsimd", h_fm_bf[j*128:(j+1)*128, i*128:(i+1)*128], TOB[:, j*128:(j+1)*128],
                  reads=[TOB], writes=[h_fm_bf])

def build_program():
    TL = TLOC
    P = Prog()
    c = emit_consts(P)
    wgt_all = P.persist("wgt_all", [128, 256], F32)
    mask_cols = P.persist("mask_cols", [128, 32], F32)
    D_in = lambda n, sh: P.dram_in(n, sh, F32)
    W = {k: D_in(k, sh) for k, sh in [
        ("W_inT", [1024, 4096]), ("x_projT", [2048, 96]), ("dt_projT", [64, 2048]),
        ("conv_w", [2048, 4]), ("conv_b", [2048, 1]), ("dt_b", [2048, 1]),
        ("A_log", [2048, 16]), ("Dd", [2048, 1]),
        ("out_projT", [2048, 1024]), ("norm1_w", [1, 1024]),
        ("mlp_w1T", [1024, 4096]), ("mlp_b1", [4096, 1]), ("mlp_w2T", [4096, 1024]), ("mlp_b2", [1, 1024]),
        ("router_w", [8, 1024]),
        ("moe_w1", [8192, 4096]), ("moe_b1", [32768, 1]), ("moe_w2", [32768, 1024]), ("moe_b2", [8, 1024]),
        ("pe_tm", [2048, 1024]),
        ("Wq", [1024, 1024]), ("Wk", [1024, 1024]), ("Wv", [1024, 1024]), ("Wo", [1024, 1024]),
        ("bq", [1024, 1]), ("bk", [1024, 1]), ("bv_row", [1, 1024]), ("bo_row", [1, 1024]),
        ("gamma_row", [1, 1024]), ("beta_row", [1, 1024]),
        ("rt_row", [1, 1024]), ("iota_row", [1, 2048]), ("idx_col", [128, 1]),
        ("w1_light", [1024, 512]), ("b1_light", [512, 1]), ("w2_light", [512, 1024]),
        ("b2_light_row", [1, 1024]), ("gamma_light_col", [1024, 1]),
        ("w1_heavy", [1024, 4096]), ("b1_heavy", [4096, 1]), ("w2_heavy", [4096, 1024]),
        ("b2_heavy_row", [1, 1024]), ("gamma_heavy_col", [1024, 1]),
    ]}
    oh_dram = None
    x_in = P.dram_in("x", [TL, 1024], F32)
    out_tm = P.dram_out("out", [TL, 1024], F32)
    t = lambda n, sh, dt=F32: P.dram_tmp(n, sh, dt)
    import os
    dbg = os.environ.get("FULL_DBG")
    if dbg:
        t = lambda n, sh, dt=F32: (P.dram_out(n, sh, dt) if n in ("attn0_tm", "x1_tm", "attn1_tm", "zmam_tm", "mlpz_tm") else P.dram_tmp(n, sh, dt))
    xn_tm = t("xn_tm", [TL, 1024]); xn_fm_bf = t("xn_fm", [1024, TL], BF16)
    xc_fm = t("xc_fm", [2048, TL]); zs_fm = t("zs_fm", [2048, TL]); dt_fm = t("dt_fm", [2048, TL])
    xdbl_fm = t("xdbl_fm", [96, TL]); bc_fm = t("bc_fm", [32, TL])
    y_fm_bf = t("y_fm", [2048, TL], BF16)
    zmam_tm = t("zmam_tm", [TL, 1024]); zmam_fm_bf = t("zmam_fm", [1024, TL], BF16)
    mlpz_tm = t("mlpz_tm", [TL, 1024]); attn0_tm = t("attn0_tm", [TL, 1024])
    x1_tm = t("x1_tm", [TL, 1024]); xn1_tm = t("xn1_tm", [TL, 1024])
    h_tm = t("h_tm", [TL, 1024]); h_fm_bf = t("h_fm", [1024, TL], BF16)
    q_fm_bf = t("q_fm", [1024, TL], BF16); k_fm_bf = t("k_fm", [1024, TL], BF16)
    v_tm_bf = t("v_tm", [TL, 1024], BF16)
    attnP_fm_bf = t("attnP_fm", [1024, TL], BF16); attn1_tm = t("attn1_tm", [TL, 1024])
    normed_fm_bf = t("normed_fm", [1024, TL], BF16)

    m0 = P.arena_save()
    P.stage_marks = []
    def MARK(nm):
        P.stage_marks.append((nm, dict(P.count)))
    P.MARK = MARK
    stage_ln(P, c, x_in, xn_tm, xn_fm_bf, name="ln0"); MARK("ln0"); P.arena_restore(m0)
    stage_mamba_front(P, c, W, xn_fm_bf, xc_fm, zs_fm, dt_fm, xdbl_fm, bc_fm, name="mf"); MARK("mf"); P.arena_restore(m0)
    for _ in stage_scan_gen(P, c, W, dt_fm, xc_fm, zs_fm, bc_fm, y_fm_bf, name="sc"):
        pass
    MARK("sc"); P.arena_restore(m0)
    for _ in stage_outproj_rms_gen(P, c, W, y_fm_bf, xn_tm, zmam_tm, zmam_fm_bf, name="s4a"):
        pass
    MARK("s4a"); P.arena_restore(m0)
    stage_router(P, c, W, xn_tm, wgt_all, name="rt"); MARK("rt"); P.arena_restore(m0)
    stage_mlp(P, c, W, zmam_fm_bf, zmam_tm, mlpz_tm, name="s4b"); MARK("s4b"); P.arena_restore(m0)
    moe_acc = stage_moe(P, c, W, xn_fm_bf, wgt_all, name="moe"); MARK("moe"); P.arena_restore(m0)
    stage_x1_ln1_pe(P, c, W, mlpz_tm, moe_acc, x_in, x1_tm, xn1_tm, h_tm, h_fm_bf, name="xp"); MARK("xp"); P.arena_restore(m0)
    stage_qkv(P, c, W, h_fm_bf, q_fm_bf, k_fm_bf, v_tm_bf, name="qkv"); MARK("qkv"); P.arena_restore(m0)
    g_att = stage_attn_core_gen(P, c, q_fm_bf, k_fm_bf, v_tm_bf, attnP_fm_bf, name="att")
    g_wo = stage_wo_ln_gen(P, c, W, attnP_fm_bf, h_tm, attn1_tm, name="wo")
    g_rte = stage_route_gen(P, c, W, xn1_tm, normed_fm_bf, mask_cols, name="rte")
    def pump(g, n):
        for _ in range(n):
            try:
                next(g)
            except StopIteration:
                return False
        return True
    for step in range(32):
        pump(g_att, 1)
        pump(g_rte, 3)
        if step >= 16:
            pump(g_wo, 1)
    while pump(g_rte, 1):
        pass
    while pump(g_wo, 1):
        pass
    MARK("att+wo+rte"); P.arena_restore(m0)
    stage_ffn1(P, c, W, normed_fm_bf, mask_cols, x1_tm, attn1_tm, out_tm, name="f1"); MARK("f1")
    P.finish_all()
    return P

def host_weights(z):
    asc = np.ascontiguousarray
    S_ = 2048; D_ = 1024
    pos = np.arange(S_, dtype=np.float64)[:, None]
    div = np.exp(np.arange(0, D_, 2, dtype=np.float64) * (-np.log(10000.0) / D_))
    pe = np.zeros((S_, D_), dtype=np.float32)
    pe[:, 0::2] = np.sin(pos * div).astype(np.float32)
    pe[:, 1::2] = np.cos(pos * div).astype(np.float32)
    w = {
        "W_inT": asc(z["L0_attn_in_proj_w"].T), "x_projT": asc(z["L0_attn_x_proj_w"].T),
        "dt_projT": asc(z["L0_attn_dt_proj_w"].T), "conv_w": asc(z["L0_attn_conv_w"]),
        "conv_b": z["L0_attn_conv_b"].reshape(-1, 1), "dt_b": z["L0_attn_dt_proj_b"].reshape(-1, 1),
        "A_log": asc(z["L0_attn_A_log"]), "Dd": z["L0_attn_D"].reshape(-1, 1),
        "out_projT": asc(z["L0_attn_out_proj_w"].T), "norm1_w": z["L0_attn_norm1_w"].reshape(1, -1),
        "mlp_w1T": asc(z["L0_attn_mlp_w1"].T), "mlp_b1": z["L0_attn_mlp_b1"].reshape(-1, 1),
        "mlp_w2T": asc(z["L0_attn_mlp_w2"].T), "mlp_b2": z["L0_attn_mlp_b2"].reshape(1, -1),
        "router_w": asc(z["L0_ffn_router_w"]),
        "moe_w1": asc(z["L0_ffn_w1"].reshape(8 * 1024, 4096)),
        "moe_b1": z["L0_ffn_b1"].reshape(-1, 1),
        "moe_w2": asc(z["L0_ffn_w2"].reshape(8 * 4096, 1024)),
        "moe_b2": asc(z["L0_ffn_b2"]),
        "pe_tm": pe,
        "Wq": asc(z["L1_attn_Wq"]), "Wk": asc(z["L1_attn_Wk"]),
        "Wv": asc(z["L1_attn_Wv"]), "Wo": asc(z["L1_attn_Wo"]),
        "bq": z["L1_attn_bq"].reshape(-1, 1), "bk": z["L1_attn_bk"].reshape(-1, 1),
        "bv_row": z["L1_attn_bv"].reshape(1, -1), "bo_row": z["L1_attn_bo"].reshape(1, -1),
        "gamma_row": z["L1_attn_gamma"].reshape(1, -1), "beta_row": z["L1_attn_beta"].reshape(1, -1),
        "rt_row": z["L1_ffn_routing_token"].reshape(1, -1),
        "iota_row": np.arange(2048, dtype=np.float32).reshape(1, -1),
        "idx_col": np.arange(128, dtype=np.float32).reshape(-1, 1),
        "w1_light": asc(z["L1_ffn_w1_light"]), "b1_light": z["L1_ffn_b1_light"].reshape(-1, 1),
        "w2_light": asc(z["L1_ffn_w2_light"]), "b2_light_row": z["L1_ffn_b2_light"].reshape(1, -1),
        "gamma_light_col": z["L1_ffn_gamma_light"].reshape(-1, 1),
        "w1_heavy": asc(z["L1_ffn_w1_heavy"]), "b1_heavy": z["L1_ffn_b1_heavy"].reshape(-1, 1),
        "w2_heavy": asc(z["L1_ffn_w2_heavy"]), "b2_heavy_row": z["L1_ffn_b2_heavy"].reshape(1, -1),
        "gamma_heavy_col": z["L1_ffn_gamma_heavy"].reshape(-1, 1),
    }
    w.update(host_consts())
    w.update(host_onehot())
    return w



_CACHE = {}

def kernel(**inputs):
    import numpy as np
    from concourse.bass_utils import run_bass_kernel_spmd
    enable_ldw_opt()
    if "prog" not in _CACHE:
        P = build_program()
        nc = P.build()
        _CACHE["prog"] = (P, nc)
    P, nc = _CACHE["prog"]
    w = host_weights(inputs)
    w.update(P.host_const_pool())
    x = np.ascontiguousarray(inputs["L0_attn_Z"].astype(np.float32, copy=False))
    in_maps = []
    for cid in range(8):
        m = dict(w)
        m["x"] = np.ascontiguousarray(x[2*cid:2*cid+2].reshape(TLOC, 1024))
        in_maps.append(m)
    res = run_bass_kernel_spmd(nc, in_maps, core_ids=list(range(8)))
    out = np.empty((16, 2048, 1024), dtype=np.float32)
    for cid in range(8):
        out[2*cid:2*cid+2] = res.results[cid]["out"].reshape(2, 2048, 1024)
    return out


# version 07
# speedup vs baseline: 1.0277x; 1.0277x over previous
import numpy as np
import contextlib
import concourse.bass as bass
import concourse.mybir as mybir

F32 = mybir.dt.float32
BF16 = mybir.dt.bfloat16
AF = mybir.ActivationFunctionType
ALU = mybir.AluOpType
AX = mybir.AxisListType

ENGINES = ("sync", "tensor", "vector", "scalar", "gpsimd")

class Buf:
    __slots__ = ("h", "name", "last_writer", "readers", "state")
    def __init__(self, h, name, state=None):
        self.h = h; self.name = name
        self.last_writer = None
        self.readers = []
        self.state = state if state is not None else self
    def ap(self):
        a = getattr(self.h, "ap", None)
        return self.h.ap() if callable(a) else self.h
    def __getitem__(self, idx):
        return self.ap()[idx]

class Prog:
    def __init__(self, name="prog"):
        self.nc = bass.Bass("TRN2", target_bir_lowering=False)
        self.stack = contextlib.ExitStack()
        self.sems = {}
        self.NDMAGRP = 6
        self._dma_rr = {e: 0 for e in ENGINES}
        for e in ENGINES:
            self.sems[e] = self.stack.enter_context(self.nc.semaphore(f"s_{e}"))
            for g in range(self.NDMAGRP):
                self.sems[f"{e}_dma{g}"] = self.stack.enter_context(self.nc.semaphore(f"s_{e}d{g}"))
        self.count = {k: 0 for k in self.sems}
        self.tasks = {e: [] for e in ENGINES}
        self.final_deps = []
        self.nbuf = 0
        self.current_barrier = ()
        self.NCONST = 64
        self.const_vals = {}
        self.const_pool = Buf(self.nc.alloc_sbuf_tensor("c_pool_sb", [128, self.NCONST], F32), "c_pool_sb")
        self.const_dram = Buf(self.nc.dram_tensor("c_pool", [128, self.NCONST], F32, kind="ExternalInput"), "c_pool")
        self.dma("sync", self.const_pool.ap(), self.const_dram.ap(),
                 reads=[self.const_dram], writes=[self.const_pool])
        self.ARENA_BYTES = 195 * 1024
        self._arena_h = self.nc.alloc_sbuf_tensor("arena", [128, self.ARENA_BYTES], mybir.dt.uint8)
        self._arena_views = {}
        self._arena_off = 0
        self.current_barrier = ()
        self.rowtmp = Buf(self.nc.alloc_sbuf_tensor("rowtmp", [1, 2048], F32), "rowtmp")
        self.psum_f = [Buf(self.nc.alloc_psum_tensor(f"ppf{i}", [128, 1024], F32), f"ppf{i}") for i in range(4)]
        self._psum_bf_view = self.psum_f[3].h.bitcast(BF16).ap()

    def persist(self, name, shape, dtype):
        return Buf(self.nc.alloc_sbuf_tensor(name, list(shape), dtype), name)
    def sbuf(self, name, shape, dtype):
        assert len(shape) == 2 and shape[0] <= 128, shape
        esz = {F32: 4, BF16: 2, mybir.dt.int32: 4}[dtype]
        off = (self._arena_off + 31) & ~31
        nbytes = shape[1] * esz
        assert off + nbytes <= self.ARENA_BYTES, f"arena overflow at {name}: {off+nbytes}"
        self._arena_off = off + nbytes
        if dtype not in self._arena_views:
            self._arena_views[dtype] = self._arena_h.bitcast(dtype).ap()
        v = self._arena_views[dtype]
        ap = v[0:shape[0], off // esz : off // esz + shape[1]]
        return Buf(ap, name)
    def arena_save(self):
        return self._arena_off
    def arena_restore(self, mark):
        self._arena_off = mark
        self.stage_barrier()
    def stage_barrier(self):
        toks = []
        for e in ENGINES:
            if self.count[e] > 0:
                toks.append((e, self.count[e]))
            for g in range(self.NDMAGRP):
                sname = f"{e}_dma{g}"
                if self.count[sname] > 0:
                    toks.append((sname, self.count[sname]))
        self.current_barrier = tuple(toks)
    def finish_all(self):
        toks = []
        for e in ENGINES:
            if self.count[e] > 0:
                toks.append((e, self.count[e]))
            for g in range(self.NDMAGRP):
                sname = f"{e}_dma{g}"
                if self.count[sname] > 0:
                    toks.append((sname, self.count[sname]))
        self.finish(toks)
    def psum(self, name, shape, dtype=F32):
        return Buf(self.nc.alloc_psum_tensor(name, list(shape), dtype), name)
    def pf(self, idx, c0=0, c1=1024, p0=0, p1=128, name=None):
        return Buf(self.psum_f[idx].h.ap()[p0:p1, c0:c1], name or f"pf{idx}_{c0}_{p0}",
                   state=self.psum_f[idx])
    def pb(self, idx, name=None):
        return Buf(self._psum_bf_view[:, idx*128:(idx+1)*128], name or f"pb{idx}", state=self.psum_f[3])
    def dram_in(self, name, shape, dtype):
        return Buf(self.nc.dram_tensor(name, list(shape), dtype, kind="ExternalInput"), name)
    def dram_out(self, name, shape, dtype):
        return Buf(self.nc.dram_tensor(name, list(shape), dtype, kind="ExternalOutput"), name)
    def dram_tmp(self, name, shape, dtype):
        return Buf(self.nc.dram_tensor(name, list(shape), dtype, kind="Internal"), name)

    def _deps_for(self, engine, reads, writes, extra):
        deps = list(self.current_barrier)
        for b in reads:
            b = b.state
            if b.last_writer is not None:
                deps.append(b.last_writer)
        for b in writes:
            b = b.state
            deps.extend(b.readers)
            if b.last_writer is not None:
                deps.append(b.last_writer)
        deps.extend(t for t in extra if t is not None)
        best = {}
        for (s, v) in deps:
            if "_dma" in s:
                v = self.count[s]
            if best.get(s, -1) < v:
                best[s] = v
        return tuple(best.items())

    def _update(self, tok, reads, writes):
        for b in reads:
            b.state.readers.append(tok)
        for b in writes:
            b.state.last_writer = tok
            b.state.readers = []

    def constf(self, v, p=128):
        import numpy as _np
        v = float(_np.float32(v))
        if v not in self.const_vals:
            assert len(self.const_vals) < self.NCONST, "const pool full"
            self.const_vals[v] = len(self.const_vals)
        i = self.const_vals[v]
        return self.const_pool.ap()[0:p, i:i+1]

    def host_const_pool(self):
        import numpy as _np
        arr = _np.zeros((128, self.NCONST), dtype=_np.float32)
        for v, i in self.const_vals.items():
            arr[:, i] = v
        return {"c_pool": arr}

    def task(self, engine, body, reads=(), writes=(), deps=(), strict=False):
        reads = list(reads) + [self.const_pool]
        dd = self._deps_for(engine, reads, writes, deps)
        self.tasks[engine].append((dd, "c", body, strict) if strict else (dd, "c", body))
        self.count[engine] += 1
        tok = (engine, self.count[engine])
        self._update(tok, reads, writes)
        return tok

    def dma(self, engine, out_ap, in_ap, reads=(), writes=(), deps=(), slot=None, **kw):
        dd = self._deps_for(engine, reads, writes, deps)
        if slot is None:
            slot = self._dma_rr[engine]
            self._dma_rr[engine] = (slot + 1) % self.NDMAGRP
        sname = f"{engine}_dma{slot}"
        if self.count[sname] > 0:
            dd = tuple(list(dd) + [(sname, self.count[sname])])
        self.tasks[engine].append((dd, "d", (out_ap, in_ap, sname, kw)))
        self.count[sname] += 16
        tok = (sname, self.count[sname])
        self._update(tok, reads, writes)
        return tok

    def finish(self, deps):
        self.final_deps.extend(d for d in deps if d is not None)

    def build(self):
        waited = {e: {} for e in ENGINES}
        def emit_stream(eng_obj, ename):
            for entry in self.tasks[ename]:
                if len(entry) == 4:
                    deps, kind, payload, strict = entry
                else:
                    (deps, kind, payload), strict = entry, False
                for (sname, val) in deps:
                    if sname == ename and ename == "tensor" and kind == "c":
                        continue
                    if waited[ename].get(sname, 0) < val:
                        eng_obj.wait_ge(self.sems[sname], val)
                        waited[ename][sname] = val
                if kind == "d":
                    out_ap, in_ap, sname_d, kw = payload
                    ins = eng_obj.dma_start(out=out_ap, in_=in_ap, **kw)
                    ins.then_inc(self.sems[sname_d], 16)
                else:
                    last = payload(eng_obj)
                    if isinstance(last, (list, tuple)):
                        last = last[-1]
                    assert last is not None, f"task body on {ename} returned None"
                    last.then_inc(self.sems[ename], 1)
            if ename == "sync":
                best = {}
                for (s, v) in self.final_deps:
                    if best.get(s, -1) < v:
                        best[s] = v
                for s, v in best.items():
                    if s == ename:
                        continue
                    if waited[ename].get(s, 0) < v:
                        eng_obj.wait_ge(self.sems[s], v)
                        waited[ename][s] = v
        with self.nc.Block() as block:
            @block.sync
            def _(e): emit_stream(e, "sync")
            if self.tasks["tensor"]:
                @block.tensor
                def _(e): emit_stream(e, "tensor")
            if self.tasks["vector"]:
                @block.vector
                def _(e): emit_stream(e, "vector")
            if self.tasks["scalar"]:
                @block.scalar
                def _(e): emit_stream(e, "scalar")
            if self.tasks["gpsimd"]:
                @block.gpsimd
                def _(e): emit_stream(e, "gpsimd")
        return self.nc


def enable_ldw_opt():
    import concourse.bass_utils as _bu
    if getattr(_bu, "_ldw_patched", False):
        return
    _orig = _bu.run_command
    def _patched(cmd, *a, **kw):
        if isinstance(cmd, list):
            cmd = ["--enable-ldw-opt=true" if c == "--enable-ldw-opt=false" else c for c in cmd]
        return _orig(cmd, *a, **kw)
    _bu.run_command = _patched
    _bu._ldw_patched = True


TLOC, SROW, D, DI = 4096, 2048, 1024, 2048
INV01 = float(np.float32(1.0) / np.float32(0.1))

def emit_consts(P):
    c = {}
    ident_f = P.persist("ident_f", [128, 128], F32)
    c["ident_f"] = ident_f
    c["ident_bf"] = P.persist("ident_bf", [128, 128], BF16)
    c["ones_f"] = P.persist("ones_f", [128, 1], F32)
    c["ones_row"] = P.persist("ones_row", [1, 128], F32)
    c["ones_all"] = P.persist("ones_all", [64, 128], F32)
    id_dram = P.dram_in("c_ident", [128, 128], F32)
    ones_dram = P.dram_in("c_ones", [128, 1], F32)
    onesr_dram = P.dram_in("c_ones_row", [1, 128], F32)
    onesa_dram = P.dram_in("c_ones_all", [64, 128], F32)
    P.dma("sync", ident_f.ap(), id_dram.ap(), reads=[id_dram], writes=[ident_f])
    P.dma("sync", c["ones_f"].ap(), ones_dram.ap(), reads=[ones_dram], writes=[c["ones_f"]])
    P.dma("sync", c["ones_row"].ap(), onesr_dram.ap(), reads=[onesr_dram], writes=[c["ones_row"]])
    P.dma("sync", c["ones_all"].ap(), onesa_dram.ap(), reads=[onesa_dram], writes=[c["ones_all"]])
    P.task("scalar", lambda e: e.copy(c["ident_bf"].ap(), ident_f.ap()),
           reads=[ident_f], writes=[c["ident_bf"]])
    return c

def host_consts():
    return {"c_ident": np.eye(128, dtype=np.float32),
            "c_ones": np.ones((128, 1), dtype=np.float32),
            "c_ones_row": np.ones((1, 128), dtype=np.float32),
            "c_ones_all": np.ones((64, 128), dtype=np.float32)}

def stage_ln(P, c, x_tm, xn_tm, xn_fm_bf, name="ln0", eps=1e-5):
    nt = TLOC // 128
    sx = [P.sbuf(f"{name}_x{i}", [128, D], F32) for i in range(2)]
    sq = P.sbuf(f"{name}_sq", [128, D], F32)
    cols = [P.sbuf(f"{name}_cols{i}", [128, 8], F32) for i in range(2)]
    xnbf = [P.sbuf(f"{name}_xnbf{i}", [128, D], BF16) for i in range(2)]
    tp = [Buf(P.psum_f[2].h.bitcast(BF16).ap()[:, 0:1024], f"{name}_tp0", state=P.psum_f[2]),
          Buf(P.psum_f[3].h.bitcast(BF16).ap()[:, 0:1024], f"{name}_tp1", state=P.psum_f[3])]
    tpo = [P.sbuf(f"{name}_tpo{i}", [128, 1024], BF16) for i in range(2)]
    for i in range(nt):
        b = i % 2
        X, CL = sx[b], cols[b]
        P.dma("sync", X.ap(), x_tm[i*128:(i+1)*128, :], reads=[x_tm], writes=[X])
        P.task("vector", lambda e, X=X, CL=CL: e.tensor_reduce(CL[:, 0:1], X.ap(), AX.X, ALU.add),
               reads=[X], writes=[CL])
        P.task("vector", lambda e, CL=CL: e.tensor_scalar(CL[:, 1:2], CL[:, 0:1], -1.0/D, None, ALU.mult),
               reads=[CL], writes=[CL], strict=True)
        P.task("vector", lambda e, X=X, CL=CL: e.tensor_scalar(X.ap(), X.ap(), CL[:, 1:2], None, ALU.add),
               reads=[X, CL], writes=[X], strict=True)
        P.task("scalar", lambda e, X=X, CL=CL: e.activation(sq.ap(), X.ap(), AF.Square, accum_out=CL[:, 2:3]),
               reads=[X], writes=[sq, CL])
        P.task("scalar", lambda e, CL=CL: e.activation(CL[:, 3:4], CL[:, 2:3], AF.Ln, bias=P.constf(eps), scale=P.constf(1.0/D)),
               reads=[CL], writes=[CL], strict=True)
        P.task("scalar", lambda e, CL=CL: e.activation(CL[:, 4:5], CL[:, 3:4], AF.Exp, scale=P.constf(-0.5)),
               reads=[CL], writes=[CL], strict=True)
        P.task("vector", lambda e, X=X, CL=CL: e.tensor_scalar(X.ap(), X.ap(), CL[:, 4:5], None, ALU.mult),
               reads=[X, CL], writes=[X])
        P.dma("gpsimd", xn_tm[i*128:(i+1)*128, :], X.ap(), reads=[X], writes=[xn_tm])
        if xn_fm_bf is None:
            continue
        XB = xnbf[b]
        P.task("scalar", lambda e, X=X, XB=XB: e.copy(XB.ap(), X.ap()), reads=[X], writes=[XB])
        TPB, TOB = tp[b], tpo[b]
        def trp(e, XB=XB, TPB=TPB):
            ins = None
            for j in range(8):
                ins = e.transpose(TPB[:, j*128:(j+1)*128], XB[:, j*128:(j+1)*128], c["ident_bf"].ap())
            return ins
        P.task("tensor", trp, reads=[XB, c["ident_bf"]], writes=[TPB])
        P.task("scalar", lambda e, TPB=TPB, TOB=TOB: e.copy(TOB.ap(), TPB.ap()), reads=[TPB], writes=[TOB])
        for j in range(8):
            P.dma("gpsimd", xn_fm_bf[j*128:(j+1)*128, i*128:(i+1)*128], TOB[:, j*128:(j+1)*128],
                  reads=[TOB], writes=[xn_fm_bf])

def load_cast_bf(P, name, w_dram, R, Ccols, stage_bufs=None):
    wbf = P.sbuf(f"{name}_bf", [128, (R // 128) * Ccols], BF16)
    if stage_bufs is None:
        stage_bufs = [P.sbuf(f"{name}_st0", [128, Ccols], F32)]
    for i in range(R // 128):
        S = stage_bufs[i % len(stage_bufs)]
        P.dma("sync", S[0:128, 0:Ccols], w_dram[i*128:(i+1)*128, :], reads=[w_dram], writes=[S])
        P.task("scalar", lambda e, S=S, i=i: e.copy(wbf[:, i*Ccols:(i+1)*Ccols], S[0:128, 0:Ccols]),
               reads=[S], writes=[wbf])
    return wbf

def wtile(wbf, i, Ccols, j0, j1):
    return wbf[:, i*Ccols + j0 : i*Ccols + j1]

def stage_mamba_front(P, c, W, xn_fm_bf, xc_fm, zs_fm, dt_fm, xdbl_fm, bc_fm, name="mf"):
    TC = 512
    nchunk = TLOC // TC
    w_in_bf = load_cast_bf(P, f"{name}_win", W["W_inT"], 1024, 4096)
    xproj_bf = load_cast_bf(P, f"{name}_xp", W["x_projT"], 2048, 96)
    dtproj_bf = P.sbuf(f"{name}_dtp_bf", [64, 2048], BF16)
    dtp_st = P.sbuf(f"{name}_dtp_st", [64, 2048], F32)
    P.dma("sync", dtp_st.ap(), W["dt_projT"].ap(), reads=[W["dt_projT"]], writes=[dtp_st])
    P.task("scalar", lambda e: e.copy(dtproj_bf.ap(), dtp_st.ap()), reads=[dtp_st], writes=[dtproj_bf])
    convw = P.sbuf(f"{name}_convw", [128, 16*4], F32)
    convb = P.sbuf(f"{name}_convb", [128, 16], F32)
    dtb = P.sbuf(f"{name}_dtb", [128, 16], F32)
    for i in range(16):
        P.dma("sync", convw[:, i*4:(i+1)*4], W["conv_w"][i*128:(i+1)*128, :], reads=[W["conv_w"]], writes=[convw])
        P.dma("sync", convb[:, i:i+1], W["conv_b"][i*128:(i+1)*128, :], reads=[W["conv_b"]], writes=[convb])
        P.dma("sync", dtb[:, i:i+1], W["dt_b"][i*128:(i+1)*128, :], reads=[W["dt_b"]], writes=[dtb])
    xn_a = P.sbuf(f"{name}_xna", [128, 8 * TC], BF16)
    xn_t = [Buf(xn_a.ap()[:, i*TC:(i+1)*TC], f"{name}_xn{i}", state=xn_a) for i in range(8)]
    pmm = [P.pf(0, 0, TC), P.pf(1, 0, TC)]
    xzx2 = [P.sbuf(f"{name}_xzx{b}", [128, TC+3], F32) for b in range(2)]
    carry = [P.sbuf(f"{name}_carry{i}", [128, 3*16], F32) for i in range(2)]
    conv_u2 = [P.sbuf(f"{name}_convu{b}", [128, TC], F32) for b in range(2)]
    expb2 = [P.sbuf(f"{name}_expb{b}", [128, TC], F32) for b in range(2)]
    zsf2 = [P.sbuf(f"{name}_zsf{b}", [128, TC], F32) for b in range(2)]
    xc_f = [P.sbuf(f"{name}_xcf{i}", [128, TC], F32) for i in range(2)]
    xc_bf = [P.sbuf(f"{name}_xcbf{i}", [128, TC], BF16) for i in range(16)]
    xdbl = P.sbuf(f"{name}_xdbl", [128, TC], F32)
    xdbl_bf = P.sbuf(f"{name}_xdblbf", [64, TC], BF16)
    pdbl = P.pf(2, 0, TC)
    dtt = P.sbuf(f"{name}_dtt", [128, TC], F32)
    dtt2 = P.sbuf(f"{name}_dtt2", [128, TC], F32)

    for ch in range(nchunk):
        row = (ch * TC) // SROW
        first_in_row = (ch * TC) % SROW == 0
        t0 = ch * TC
        CARRY = carry[0]
        P.dma("sync", xn_a.ap().rearrange("p (k c) -> p k c", k=8),
              xn_fm_bf[0:1024, t0:t0+TC].rearrange("(k p) c -> p k c", p=128),
              reads=[xn_fm_bf], writes=[xn_a])
        XN = xn_t
        for et in range(16):
            PM = pmm[et % 2]
            eb = et % 2
            xzx, conv_u, expb = xzx2[eb], conv_u2[eb], expb2[eb]
            if et % 2 == 0:
                def mm(e, et=et, XN=XN):
                    ins = None
                    for ei in range(2):
                        PMx = pmm[ei]
                        for dk in range(8):
                            ins = e.matmul(PMx.ap(), wtile(w_in_bf, dk, 4096, (et+ei)*128, (et+ei+1)*128),
                                           XN[dk].ap(), start=(dk == 0), stop=(dk == 7))
                    return ins
                P.task("tensor", mm, reads=[w_in_bf] + XN, writes=[pmm[0], pmm[1]])
                for ei in range(2):
                    P.task("scalar", lambda e, PM=pmm[ei], xzx=xzx2[ei]: e.copy(xzx[:, 3:TC+3], PM.ap()),
                           reads=[pmm[ei]], writes=[xzx2[ei]])
            def convt(e, et=et, xzx=xzx, conv_u=conv_u, fir=first_in_row, CARRY=CARRY):
                if fir:
                    e.memset(xzx[:, 0:3], 0.0)
                else:
                    e.tensor_copy(xzx[:, 0:3], CARRY[:, et*3:et*3+3])
                e.tensor_copy(CARRY[:, et*3:et*3+3], xzx[:, TC:TC+3])
                e.tensor_scalar(conv_u.ap(), xzx[:, 0:TC], convw[:, et*4:et*4+1], convb[:, et:et+1], ALU.mult, ALU.add)
                ins = None
                for k2 in (1, 2, 3):
                    ins = e.scalar_tensor_tensor(conv_u.ap(), xzx[:, k2:TC+k2], convw[:, et*4+k2:et*4+k2+1],
                                                 conv_u.ap(), ALU.mult, ALU.add)
                return ins
            P.task("vector", convt, reads=[xzx, convw, convb, CARRY], writes=[conv_u, xzx, CARRY])
            P.task("scalar", lambda e, conv_u=conv_u, expb=expb: e.activation(expb.ap(), conv_u.ap(), AF.Exp, scale=P.constf(-1.0)),
                   reads=[conv_u], writes=[expb])
            XCF, XCB = xc_f[et % 2], xc_bf[et]
            def silut(e, XCF=XCF, conv_u=conv_u, expb=expb):
                e.tensor_scalar(expb.ap(), expb.ap(), 1.0, None, ALU.add)
                e.reciprocal(expb.ap(), expb.ap())
                return e.tensor_tensor(XCF.ap(), conv_u.ap(), expb.ap(), ALU.mult)
            P.task("vector", silut, reads=[conv_u, expb], writes=[XCF, expb])
            P.task("scalar", lambda e, XCF=XCF, XCB=XCB: e.copy(XCB.ap(), XCF.ap()), reads=[XCF], writes=[XCB])
            P.dma("gpsimd", xc_fm[et*128:(et+1)*128, t0:t0+TC], XCF.ap(), reads=[XCF], writes=[xc_fm])
        for et in range(16):
            PM = pmm[et % 2]
            eb = et % 2
            expb, zsf = expb2[eb], zsf2[eb]
            if et % 2 == 0:
                def mmz(e, et=et, XN=XN):
                    ins = None
                    for ei in range(2):
                        PMx = pmm[ei]
                        for dk in range(8):
                            ins = e.matmul(PMx.ap(), wtile(w_in_bf, dk, 4096, 2048+(et+ei)*128, 2048+(et+ei+1)*128),
                                           XN[dk].ap(), start=(dk == 0), stop=(dk == 7))
                    return ins
                P.task("tensor", mmz, reads=[w_in_bf] + XN, writes=[pmm[0], pmm[1]])
            P.task("scalar", lambda e, PM=PM, expb=expb: e.activation(expb.ap(), PM.ap(), AF.Exp, scale=P.constf(-1.0)),
                   reads=[PM], writes=[expb])
            def siluz(e, PM=PM, expb=expb, zsf=zsf):
                e.tensor_scalar(expb.ap(), expb.ap(), 1.0, None, ALU.add)
                e.reciprocal(expb.ap(), expb.ap())
                return e.tensor_tensor(zsf.ap(), PM.ap(), expb.ap(), ALU.mult)
            P.task("vector", siluz, reads=[PM, expb], writes=[zsf, expb])
            P.dma("gpsimd", zs_fm[et*128:(et+1)*128, t0:t0+TC], zsf.ap(), reads=[zsf], writes=[zs_fm])
        def mdbl(e):
            ins = None
            for et in range(16):
                ins = e.matmul(pdbl.ap()[0:96, :], wtile(xproj_bf, et, 96, 0, 96),
                               xc_bf[et].ap(), start=(et == 0), stop=(et == 15))
            return ins
        P.task("tensor", mdbl, reads=[xproj_bf] + xc_bf, writes=[pdbl])
        P.task("scalar", lambda e: e.copy(xdbl[0:96, :], pdbl.ap()[0:96, :]), reads=[pdbl], writes=[xdbl])
        P.dma("gpsimd", xdbl_fm[:, t0:t0+TC], xdbl[0:96, :], reads=[xdbl], writes=[xdbl_fm])
        P.dma("gpsimd", bc_fm[:, t0:t0+TC], xdbl[64:96, :], reads=[xdbl], writes=[bc_fm])
        P.task("scalar", lambda e: e.copy(xdbl_bf.ap(), xdbl[0:64, :]), reads=[xdbl], writes=[xdbl_bf])
        for dk in range(16):
            PM = pmm[dk % 2]
            P.task("tensor", lambda e, dk=dk, PM=PM: e.matmul(
                PM.ap(), dtproj_bf[:, dk*128:(dk+1)*128], xdbl_bf.ap(), start=True, stop=True),
                reads=[dtproj_bf, xdbl_bf], writes=[PM])
            def sp1(e, PM=PM, dk=dk):
                e.activation(dtt.ap(), PM.ap(), AF.Relu, bias=dtb[:, dk:dk+1])
                e.activation(dtt2.ap(), PM.ap(), AF.Abs, bias=dtb[:, dk:dk+1])
                e.activation(dtt2.ap(), dtt2.ap(), AF.Exp, scale=P.constf(-1.0))
                return e.activation(dtt2.ap(), dtt2.ap(), AF.Ln, bias=P.constf(1.0))
            P.task("scalar", sp1, reads=[PM, dtb], writes=[dtt, dtt2])
            P.task("vector", lambda e: e.tensor_tensor(dtt.ap(), dtt.ap(), dtt2.ap(), ALU.add),
                   reads=[dtt, dtt2], writes=[dtt])
            P.dma("gpsimd", dt_fm[dk*128:(dk+1)*128, t0:t0+TC], dtt.ap(), reads=[dtt], writes=[dt_fm])


def stage_scan_gen(P, c, W, dt_fm, xc_fm, zs_fm, bc_fm, y_fm_bf, name="sc"):
    TC = 512
    NB = 4
    oh_dram = P.dram_in(f"{name}_onehot", [32, 32*128], F32)
    oh = P.sbuf(f"{name}_oh", [32, 32*128], F32)
    P.dma("sync", oh.ap(), oh_dram.ap(), reads=[oh_dram], writes=[oh])
    A_all = P.sbuf(f"{name}_A", [128, 256], F32)
    Dcol = P.sbuf(f"{name}_D", [128, 16], F32)
    ast = P.sbuf(f"{name}_ast", [128, 16], F32)
    for k in range(16):
        P.dma("sync", ast.ap(), W["A_log"][k*128:(k+1)*128, :], reads=[W["A_log"]], writes=[ast])
        P.task("scalar", lambda e, k=k: e.activation(A_all[:, k*16:(k+1)*16], ast.ap(), AF.Exp),
               reads=[ast], writes=[A_all])
        P.dma("sync", Dcol[:, k:k+1], W["Dd"][k*128:(k+1)*128, :], reads=[W["Dd"]], writes=[Dcol])
    P.task("vector", lambda e: e.tensor_scalar(A_all.ap(), A_all.ap(), -1.0, None, ALU.mult),
           reads=[A_all], writes=[A_all])
    states = P.sbuf(f"{name}_st", [128, 256], F32)
    state_g = [Buf(states.ap()[:, g*4:(g+1)*4], f"stg{g}") for g in range(64)]
    bc_sb = P.sbuf(f"{name}_bc", [32, TC], F32)
    pB = P.pf(0, 0, TC)
    pC = P.pf(1, 0, TC)
    Bb = P.sbuf(f"{name}_Bb", [128, 16 * TC], F32)
    Cb = P.sbuf(f"{name}_Cb", [128, 16 * TC], F32)
    Bb_q = [Buf(Bb.ap()[:, q*NB*TC:(q+1)*NB*TC], f"{name}_Bbq{q}") for q in range(4)]
    Cb_q = [Buf(Cb.ap()[:, q*NB*TC:(q+1)*NB*TC], f"{name}_Cbq{q}") for q in range(4)]
    dtb_ = [P.sbuf(f"{name}_dt{i}", [128, TC], F32) for i in range(2)]
    xcb_ = [P.sbuf(f"{name}_xc{i}", [128, TC], F32) for i in range(2)]
    zsb_ = [P.sbuf(f"{name}_zs{i}", [128, TC], F32) for i in range(2)]
    wbuf = [P.sbuf(f"{name}_w{i}", [128, TC], F32) for i in range(2)]
    dA4 = [P.sbuf(f"{name}_dA{i}", [128, NB * TC], BF16) for i in range(2)]
    dB4 = [P.sbuf(f"{name}_dB{i}", [128, NB * TC], BF16) for i in range(2)]
    hh4 = [P.sbuf(f"{name}_hh{i}", [128, NB * TC], F32) for i in range(2)]
    tr4 = [P.sbuf(f"{name}_tr{i}", [128, NB * TC], F32) for i in range(2)]
    yacc_ = [P.sbuf(f"{name}_y{i}", [128, TC], F32) for i in range(2)]
    ybf_ = [P.sbuf(f"{name}_ybf{i}", [128, TC], BF16) for i in range(2)]
    nchunk = TLOC // TC
    seg = lambda ap_, w_=TC: ap_.rearrange("p (s c) -> p s c", s=NB)
    for ch in range(nchunk):
        t0 = ch * TC
        first_in_row = t0 % SROW == 0
        if first_in_row:
            P.task("vector", lambda e: e.memset(states.ap(), 0.0), writes=state_g)
        P.dma("sync", bc_sb.ap(), bc_fm[:, t0:t0+TC], reads=[bc_fm], writes=[bc_sb])
        for n in range(16):
            P.task("tensor", lambda e, n=n: e.matmul(pB.ap(), oh[:, n*128:(n+1)*128], bc_sb.ap(),
                                                     start=True, stop=True),
                   reads=[oh, bc_sb], writes=[pB])
            P.task("scalar", lambda e, n=n: e.copy(Bb[:, n*TC:(n+1)*TC], pB.ap()), reads=[pB], writes=[Bb_q[n//4]])
            P.task("tensor", lambda e, n=n: e.matmul(pC.ap(), oh[:, (16+n)*128:(17+n)*128], bc_sb.ap(),
                                                     start=True, stop=True),
                   reads=[oh, bc_sb], writes=[pC])
            P.task("scalar", lambda e, n=n: e.copy(Cb[:, n*TC:(n+1)*TC], pC.ap()), reads=[pC], writes=[Cb_q[n//4]])
        pend_gate = [None]
        pend_out = [None]
        for k in range(16):
            b = k % 2
            DT, XC, WW, ZS = dtb_[b], xcb_[b], zsb_[b], wbuf[b]
            P.dma("sync", DT.ap(), dt_fm[k*128:(k+1)*128, t0:t0+TC], reads=[dt_fm], writes=[DT])
            P.dma("sync", XC.ap(), xc_fm[k*128:(k+1)*128, t0:t0+TC], reads=[xc_fm], writes=[XC])
            P.dma("sync", ZS.ap(), zs_fm[k*128:(k+1)*128, t0:t0+TC], reads=[zs_fm], writes=[ZS])
            P.task("vector", lambda e, DT=DT, XC=XC, WW=WW: e.tensor_tensor(WW.ap(), DT.ap(), XC.ap(), ALU.mult),
                   reads=[DT, XC], writes=[WW])
            yacc, ybf = yacc_[k % 2], ybf_[k % 2]
            def emit_save_trt(nb, k=k):
                bb_ = nb % 2
                HH, TR = hh4[bb_], tr4[bb_]
                SG = state_g[k*4 + nb]
                P.task("gpsimd", lambda e, HH=HH, SG=SG: e.tensor_copy(
                    SG.ap().rearrange("p (s o) -> p s o", s=NB), seg(HH.ap())[:, :, TC-1:TC]),
                    reads=[HH], writes=[SG])
                def trt(e, HH=HH, nb=nb, TR=TR):
                    return e.tensor_tensor(TR.ap(), HH.ap(),
                                           Cb.ap()[:, nb*NB*TC:(nb+1)*NB*TC], ALU.mult)
                P.task("gpsimd", trt, reads=[HH, Cb_q[nb]], writes=[TR])
            def emit_yadd(nb, yacc=yacc):
                TR0 = tr4[nb % 2]
                def yadd(e, TR0=TR0, first=(nb == 0), yacc=yacc):
                    e.tensor_tensor(TR0.ap()[:, 0:2*TC], TR0.ap()[:, 0:2*TC], TR0.ap()[:, 2*TC:4*TC], ALU.add)
                    e.tensor_tensor(TR0.ap()[:, 0:TC], TR0.ap()[:, 0:TC], TR0.ap()[:, TC:2*TC], ALU.add)
                    if first:
                        return e.tensor_copy(yacc.ap(), TR0.ap()[:, 0:TC])
                    return e.tensor_tensor(yacc.ap(), yacc.ap(), TR0.ap()[:, 0:TC], ALU.add)
                P.task("gpsimd", yadd, reads=[TR0, yacc], writes=[yacc, TR0])
            for nb in range(4):
                bb_ = nb % 2
                DA, DB, HH = dA4[bb_], dB4[bb_], hh4[bb_]
                SG = state_g[k*4 + nb]
                def mk_da(e, DT=DT, k=k, nb=nb, DA=DA):
                    ins = None
                    for j in range(NB):
                        n = nb*NB + j
                        ins = e.activation(DA.ap()[:, j*TC:(j+1)*TC], DT.ap(), AF.Exp,
                                           scale=A_all[:, k*16+n:k*16+n+1])
                    return ins
                P.task("scalar", mk_da, reads=[DT, A_all], writes=[DA])
                def gpd(e, WW=WW, nb=nb, DA=DA, DB=DB, SG=SG):
                    e.memset(seg(DA.ap())[:, :, 0:1], 0.0)
                    e.tensor_tensor(
                        DB.ap().rearrange("p (s c) -> p s c", s=NB),
                        WW.ap().rearrange("p (o c) -> p o c", o=1).broadcast_to([128, NB, TC]),
                        Bb.ap()[:, nb*NB*TC:(nb+1)*NB*TC].rearrange("p (s c) -> p s c", s=NB), ALU.mult)
                    return e.tensor_copy(seg(DB.ap())[:, :, 0:1], SG.ap().rearrange("p (s o) -> p s o", s=NB))
                P.task("vector", gpd, reads=[WW, Bb_q[nb], SG], writes=[DA, DB])
                P.task("vector", lambda e, DA=DA, DB=DB, HH=HH: e.tensor_tensor_scan(
                    HH.ap(), DA.ap(), DB.ap(), 0.0, ALU.mult, ALU.add),
                    reads=[DA, DB], writes=[HH])
                if nb == 1 and pend_gate[0] is not None:
                    pend_gate[0](); pend_gate[0] = None
                if nb > 0:
                    emit_save_trt(nb - 1)
                if nb > 1:
                    emit_yadd(nb - 2)
            if pend_out[0] is not None:
                pend_out[0](); pend_out[0] = None
            emit_save_trt(3)
            emit_yadd(2)
            emit_yadd(3)
            def mk_gate(XC=XC, ZS=ZS, k=k, yacc=yacc):
                def gatet(e):
                    e.scalar_tensor_tensor(yacc.ap(), XC.ap(), Dcol[:, k:k+1], yacc.ap(), ALU.mult, ALU.add)
                    return e.tensor_tensor(yacc.ap(), yacc.ap(), ZS.ap(), ALU.mult)
                P.task("vector", gatet, reads=[XC, Dcol, yacc, ZS], writes=[yacc])
            def mk_out(k=k, ybf=ybf, yacc=yacc, t0=t0):
                P.task("scalar", lambda e: e.copy(ybf.ap(), yacc.ap()), reads=[yacc], writes=[ybf])
                P.dma("gpsimd", y_fm_bf[k*128:(k+1)*128, t0:t0+TC], ybf.ap(), reads=[ybf], writes=[y_fm_bf])
            pend_gate[0] = mk_gate
            pend_out[0] = mk_out
        pend_gate[0](); pend_gate[0] = None
        pend_out[0](); pend_out[0] = None
        yield ch

def host_onehot():
    oh = np.zeros((32, 32*128), dtype=np.float32)
    for n in range(32):
        oh[n, n*128:(n+1)*128] = 1.0
    return {"sc_onehot": oh}

def stage_precast(P, jobs, name="pc"):
    stg = [P.persist(f"{name}_s{i}", [128, 1024], F32) for i in range(1)]
    obf = [P.persist(f"{name}_o{i}", [128, 1024], BF16) for i in range(2)]
    gc = P.persist(f"{name}_g", [128, 16], F32)
    for (src, dst, gcol) in jobs:
        R, C = src.h.shape if hasattr(src.h, "shape") else (None, None)
        R, C = dst.h.shape
        ng = 0
        if gcol is not None:
            nd_ = R // 128
            for k in range(nd_):
                P.dma("scalar", gc[:, k:k+1], gcol[k*128:(k+1)*128, :], reads=[gcol], writes=[gc])
        i = 0
        for r0 in range(0, R, 128):
            for c0 in range(0, C, 1024):
                cw = min(1024, C - c0)
                b = i % 2; i += 1
                S_, O_ = stg[0], obf[b]
                P.dma("scalar", S_[0:128, 0:cw], src[r0:r0+128, c0:c0+cw], reads=[src], writes=[S_])
                if gcol is not None:
                    k = r0 // 128
                    P.task("scalar", lambda e, S_=S_, O_=O_, cw=cw, k=k: e.activation(
                        O_[0:128, 0:cw], S_[0:128, 0:cw], AF.Copy, scale=gc[:, k:k+1]),
                        reads=[S_, gc], writes=[O_])
                else:
                    P.task("scalar", lambda e, S_=S_, O_=O_, cw=cw: e.copy(O_[0:128, 0:cw], S_[0:128, 0:cw]),
                           reads=[S_], writes=[O_])
                P.dma("gpsimd", dst[r0:r0+128, c0:c0+cw], O_[0:128, 0:cw], reads=[O_], writes=[dst])

def bcast_row(P, c, name, row_dram_or_buf, N, f32=True):
    row = Buf(P.rowtmp.ap()[0:1, 0:N], f"{name}_row", state=P.rowtmp)
    src = row_dram_or_buf.ap() if isinstance(row_dram_or_buf, Buf) else row_dram_or_buf
    P.dma("sync", row[0:1, 0:N], src,
          reads=[row_dram_or_buf] if isinstance(row_dram_or_buf, Buf) else [], writes=[row])
    ps = P.pf(2, 0, min(N, 512), name=f"{name}_ps")
    out = P.sbuf(f"{name}_bc", [128, N], F32)
    for j0 in range(0, N, 512):
        j1 = min(j0 + 512, N)
        P.task("tensor", lambda e, j0=j0, j1=j1: e.matmul(ps.ap()[:, 0:j1-j0], c["ones_row"].ap(),
                                                          P.rowtmp.ap()[0:1, j0:j1], start=True, stop=True),
               reads=[row, c["ones_row"]], writes=[ps])
        P.task("scalar", lambda e, j0=j0, j1=j1: e.copy(out[:, j0:j1], ps.ap()[:, 0:j1-j0]),
               reads=[ps], writes=[out])
    return out

def rms_cols(P, name_bufs, X_ap, reads, eps, D_):
    sq, cols = name_bufs
    P.task("scalar", lambda e: e.activation(sq.ap(), X_ap, AF.Square, accum_out=cols[:, 0:1]),
           reads=reads, writes=[sq, cols])
    P.task("scalar", lambda e: e.activation(cols[:, 1:2], cols[:, 0:1], AF.Ln, bias=P.constf(eps), scale=P.constf(1.0/D_)),
           reads=[cols], writes=[cols], strict=True)
    P.task("scalar", lambda e: e.activation(cols[:, 2:3], cols[:, 1:2], AF.Exp, scale=P.constf(-0.5)),
           reads=[cols], writes=[cols], strict=True)

def stage_outproj_rms_gen(P, c, W, y_fm_bf, xn_tm, zmam_tm, zmam_fm_bf, name="s4a"):
    opj_bf = load_cast_bf(P, f"{name}_opj", W["out_projT"], 2048, 1024)
    n1w = bcast_row(P, c, f"{name}_n1w", W["norm1_w"], 1024)
    ybig = [P.sbuf(f"{name}_yb{i}", [128, 16 * 512], BF16) for i in range(1)]
    pm = [P.pf(0), P.pf(1)]
    pre = [P.sbuf(f"{name}_pre{i}", [128, 1024], F32) for i in range(2)]
    sq = P.sbuf(f"{name}_sq", [128, 1024], F32)
    cols = P.sbuf(f"{name}_cols", [128, 4], F32)
    zbf = [P.sbuf(f"{name}_zbf{i}", [128, 1024], BF16) for i in range(2)]
    tp = [Buf(P.psum_f[2].h.bitcast(BF16).ap()[:, 0:1024], f"{name}_tp0", state=P.psum_f[2]),
          Buf(P.psum_f[3].h.bitcast(BF16).ap()[:, 0:1024], f"{name}_tp1", state=P.psum_f[3])]
    tpo = [P.sbuf(f"{name}_tpo{i}", [128, 1024], BF16) for i in range(2)]
    for i in range(TLOC // 128):
        b = i % 2
        PM, PRE, ZBF = pm[b], pre[b], zbf[b]
        ch = i // 4
        sub = i % 4
        YB = ybig[0]
        if sub == 0:
            P.dma("sync", YB.ap().rearrange("p (k c) -> p k c", k=16),
                  y_fm_bf[0:2048, ch*512:(ch+1)*512].rearrange("(k p) c -> p k c", p=128),
                  reads=[y_fm_bf], writes=[YB])
        P.dma("sync", PRE.ap(), xn_tm[i*128:(i+1)*128, :], reads=[xn_tm], writes=[PRE])
        def mm(e, PM=PM, YB=YB, sub=sub):
            ins = None
            for half in range(2):
                for k in range(16):
                    ins = e.matmul(PM.ap()[:, half*512:(half+1)*512],
                                   YB[:, k*512 + sub*128 : k*512 + (sub+1)*128],
                                   wtile(opj_bf, k, 1024, half*512, (half+1)*512),
                                   start=(k == 0), stop=(k == 15))
            return ins
        P.task("tensor", mm, reads=[YB, opj_bf], writes=[PM])
        P.task("vector", lambda e, PM=PM, PRE=PRE: e.scalar_tensor_tensor(
            PRE.ap(), PM.ap(), 1.0, PRE.ap(), ALU.mult, ALU.add), reads=[PM, PRE], writes=[PRE])
        rms_cols(P, (sq, cols), PRE.ap(), [PRE], 1e-6, 1024)
        def rmsap(e, PRE=PRE):
            e.tensor_scalar(PRE.ap(), PRE.ap(), cols[:, 2:3], None, ALU.mult)
            return e.tensor_tensor(PRE.ap(), PRE.ap(), n1w.ap(), ALU.mult)
        P.task("vector", rmsap, reads=[PRE, cols, n1w], writes=[PRE])
        P.dma("gpsimd", zmam_tm[i*128:(i+1)*128, :], PRE.ap(), reads=[PRE], writes=[zmam_tm])
        P.task("scalar", lambda e, PRE=PRE, ZBF=ZBF: e.copy(ZBF.ap(), PRE.ap()), reads=[PRE], writes=[ZBF])
        TPB, TOB = tp[b], tpo[b]
        def trp(e, ZBF=ZBF, TPB=TPB):
            ins = None
            for j in range(8):
                ins = e.transpose(TPB[:, j*128:(j+1)*128], ZBF[:, j*128:(j+1)*128], c["ident_bf"].ap())
            return ins
        P.task("tensor", trp, reads=[ZBF, c["ident_bf"]], writes=[TPB])
        P.task("scalar", lambda e, TPB=TPB, TOB=TOB: e.copy(TOB.ap(), TPB.ap()), reads=[TPB], writes=[TOB])
        for j in range(8):
            P.dma("gpsimd", zmam_fm_bf[j*128:(j+1)*128, i*128:(i+1)*128], TOB[:, j*128:(j+1)*128],
                  reads=[TOB], writes=[zmam_fm_bf])
        if i % 4 == 3:
            yield i // 4

def ffn_fm(P, c, name, x_fm_bf, w1T_dram, b1_dram, w2T_dram, b2_row_dram, DIN, DHID, DOUT,
           out_cb, gamma_col_dram=None, act="gelu", TC=None, sc_prolog=None, pre_bf16=False):
    nd, nf = DIN // 128, DHID // 128
    SC = 512
    if gamma_col_dram is not None and not pre_bf16:
        gcol = P.sbuf(f"{name}_gcol", [128, nd], F32)
        for k in range(nd):
            P.dma("sync", gcol[:, k:k+1], gamma_col_dram[k*128:(k+1)*128, :],
                  reads=[gamma_col_dram], writes=[gcol])
    w1bf = P.sbuf(f"{name}_w1bf", [128, nd * DHID], BF16)
    if pre_bf16:
        for k in range(nd):
            P.dma("sync", w1bf[:, k*DHID:(k+1)*DHID], w1T_dram[k*128:(k+1)*128, :],
                  reads=[w1T_dram], writes=[w1bf])
        w2bf = P.sbuf(f"{name}_w2bf", [128, (DHID // 128) * DOUT], BF16)
        for fidx in range(DHID // 128):
            P.dma("sync", w2bf[:, fidx*DOUT:(fidx+1)*DOUT], w2T_dram[fidx*128:(fidx+1)*128, :],
                  reads=[w2T_dram], writes=[w2bf])
    else:
        CW = min(1024, DHID)
        st2_ = [P.sbuf(f"{name}_st{i}", [128, CW], F32) for i in range(2)]
        st = st2_[0]
        ii = 0
        for k in range(nd):
            for c0 in range(0, DHID, CW):
                S_ = st2_[ii % 2]; ii += 1
                P.dma("sync", S_[0:128, 0:CW], w1T_dram[k*128:(k+1)*128, c0:c0+CW],
                      reads=[w1T_dram], writes=[S_])
                if gamma_col_dram is not None:
                    P.task("scalar", lambda e, k=k, c0=c0, S_=S_: e.activation(
                        w1bf[:, k*DHID + c0 : k*DHID + c0 + CW], S_[0:128, 0:CW],
                        AF.Copy, scale=gcol[:, k:k+1]), reads=[S_, gcol], writes=[w1bf])
                else:
                    P.task("scalar", lambda e, k=k, c0=c0, S_=S_: e.copy(
                        w1bf[:, k*DHID + c0 : k*DHID + c0 + CW], S_[0:128, 0:CW]),
                        reads=[S_], writes=[w1bf])
        stw = 2048 if DHID >= 2048 else DHID
        w2bf = load_cast_bf(P, f"{name}_w2", w2T_dram, DHID, DOUT,
                            stage_bufs=[st] if stw >= DOUT else None)
    b1c = P.sbuf(f"{name}_b1c", [128, nf], F32)
    for f in range(nf):
        P.dma("sync", b1c[:, f:f+1], b1_dram[f*128:(f+1)*128, :], reads=[b1_dram], writes=[b1c])
    b2b = bcast_row(P, c, f"{name}_b2", b2_row_dram, DOUT)
    xta = P.sbuf(f"{name}_xta", [128, nd * SC], BF16)
    xt = [Buf(xta.ap()[:, k*SC:(k+1)*SC], f"{name}_xt{k}", state=xta) for k in range(nd)]
    hid = P.sbuf(f"{name}_hid", [128, nf * SC], BF16)
    ph8 = [P.pf(2, 0, 512), P.pf(2, 512, 1024), P.pf(3, 0, 512), P.pf(3, 512, 1024),
           P.pf(0, 0, 512), P.pf(0, 512, 1024), P.pf(1, 0, 512), P.pf(1, 512, 1024)]
    po = [P.pf(0, 0, DOUT), P.pf(1, 0, DOUT)]
    for sc in range(TLOC // SC):
        t0 = sc * SC
        if sc_prolog is not None:
            sc_prolog(sc)
        P.dma("scalar", xta.ap().rearrange("p (k c) -> p k c", k=nd),
              x_fm_bf[0:nd*128, t0:t0+SC].rearrange("(k p) c -> p k c", p=128),
              reads=[x_fm_bf], writes=[xta])
        for f0 in range(0, nf, 4):
            nfg = min(4, nf - f0)
            PH4 = ph8[((f0 // 4) % 2) * 4 : ((f0 // 4) % 2) * 4 + 4]
            def mm1(e, f0=f0, nfg=nfg, PH4=PH4):
                ins = None
                for fi in range(nfg):
                    f = f0 + fi
                    for k in range(nd):
                        ins = e.matmul(PH4[fi].ap(), w1bf[:, k*DHID + f*128 : k*DHID + (f+1)*128],
                                       xt[k].ap(), start=(k == 0), stop=(k == nd - 1))
                return ins
            P.task("tensor", mm1, reads=[w1bf] + xt, writes=PH4)
            def gel(e, f0=f0, nfg=nfg, PH4=PH4):
                ins = None
                for fi in range(nfg):
                    f = f0 + fi
                    ins = e.activation(hid[:, f*SC:(f+1)*SC], PH4[fi].ap(),
                                       AF.Gelu, bias=b1c[:, f:f+1])
                return ins
            P.task("scalar", gel, reads=PH4 + [b1c], writes=[hid])
        for s2 in range(SC // 128):
            i = (t0 + s2*128) // 128
            PO = po[s2 % 2]
            def mm2(e, s2=s2, PO=PO):
                ins = None
                for half in range(0, DOUT, 512):
                    for f in range(nf):
                        ins = e.matmul(PO.ap()[:, half:half+512], hid[:, f*SC + s2*128 : f*SC + (s2+1)*128],
                                       wtile(w2bf, f, DOUT, half, half+512), start=(f == 0), stop=(f == nf - 1))
                return ins
            P.task("tensor", mm2, reads=[hid, w2bf], writes=[PO])
            out_cb(i, PO, b2b)

def stage_mlp(P, c, W, zmam_fm_bf, zmam_tm, mlpz_tm, name="s4b"):
    zt = [P.sbuf(f"{name}_zt{i}", [128, 1024], F32) for i in range(2)]
    mo = [P.sbuf(f"{name}_mo{i}", [128, 1024], F32) for i in range(1)]
    def cb(i, PO, b2b):
        b = i % 2
        ZT, MO = zt[b], mo[0]
        P.dma("sync", ZT.ap(), zmam_tm[i*128:(i+1)*128, :], reads=[zmam_tm], writes=[ZT])
        P.task("vector", lambda e, PO=PO, MO=MO: e.scalar_tensor_tensor(
            MO.ap(), PO.ap(), 1.0, b2b.ap(), ALU.mult, ALU.add), reads=[PO, b2b], writes=[MO])
        P.task("vector", lambda e, MO=MO, ZT=ZT: e.tensor_tensor(MO.ap(), MO.ap(), ZT.ap(), ALU.add),
               reads=[MO, ZT], writes=[MO])
        P.dma("gpsimd", mlpz_tm[i*128:(i+1)*128, :], MO.ap(), reads=[MO], writes=[mlpz_tm])
    ffn_fm(P, c, name, zmam_fm_bf, W["mlp_w1T"], W["mlp_b1"], W["mlp_w2T"], W["mlp_b2"],
           1024, 4096, 1024, cb)

def stage_rms_out(P, c, W, mlpz_tm, attn0_tm, name="s4c"):
    n1w = bcast_row(P, c, f"{name}_n1w", W["norm1_w"], 1024)
    xt = [P.sbuf(f"{name}_x{i}", [128, 1024], F32) for i in range(2)]
    sq = P.sbuf(f"{name}_sq", [128, 1024], F32)
    cols = P.sbuf(f"{name}_cols", [128, 4], F32)
    for i in range(TLOC // 128):
        X = xt[i % 2]
        P.dma("sync", X.ap(), mlpz_tm[i*128:(i+1)*128, :], reads=[mlpz_tm], writes=[X])
        rms_cols(P, (sq, cols), X.ap(), [X], 1e-6, 1024)
        def rmsap(e, X=X):
            e.tensor_scalar(X.ap(), X.ap(), cols[:, 2:3], None, ALU.mult)
            return e.tensor_tensor(X.ap(), X.ap(), n1w.ap(), ALU.mult)
        P.task("vector", rmsap, reads=[X, cols, n1w], writes=[X])
        P.dma("gpsimd", attn0_tm[i*128:(i+1)*128, :], X.ap(), reads=[X], writes=[attn0_tm])

def stage_router(P, c, W, xn_tm, wgt_all, name="rt"):
    rwb = []
    for e8 in range(8):
        rwb.append(bcast_row(P, c, f"{name}_rw{e8}", W["router_w"].ap()[e8:e8+1, :], 1024))
    xt = [P.sbuf(f"{name}_x{i}", [128, 1024], F32) for i in range(2)]
    junk = P.sbuf(f"{name}_junk", [128, 1024], F32)
    L = P.sbuf(f"{name}_L", [128, 8], F32)
    E = P.sbuf(f"{name}_E", [128, 8], F32)
    CN = P.sbuf(f"{name}_CN", [128, 8], F32)
    TMPQ = P.sbuf(f"{name}_TQ", [128, 8], F32)
    cols = P.sbuf(f"{name}_cols", [128, 4], F32)
    for i in range(TLOC // 128):
        X = xt[i % 2]
        P.dma("sync", X.ap(), xn_tm[i*128:(i+1)*128, :], reads=[xn_tm], writes=[X])
        for e8 in range(8):
            P.task("vector", lambda e, X=X, e8=e8: e.scalar_tensor_tensor(
                junk.ap(), X.ap(), 1.0, rwb[e8].ap(), ALU.mult, ALU.mult, accum_out=L[:, e8:e8+1]),
                reads=[X, rwb[e8]], writes=[junk, L])
        P.task("vector", lambda e: e.tensor_reduce(cols[:, 0:1], L.ap(), AX.X, ALU.max), reads=[L], writes=[cols])
        P.task("vector", lambda e: e.tensor_scalar(cols[:, 1:2], cols[:, 0:1], -1.0, None, ALU.mult),
               reads=[cols], writes=[cols], strict=True)
        P.task("scalar", lambda e: e.activation(E.ap(), L.ap(), AF.Exp, bias=cols[:, 1:2],
                                                 accum_out=cols[:, 2:3]),
               reads=[L, cols], writes=[E, cols])
        P.task("vector", lambda e: e.reciprocal(cols[:, 3:4], cols[:, 2:3]), reads=[cols], writes=[cols], strict=True)
        P.task("vector", lambda e: e.tensor_scalar(E.ap(), E.ap(), cols[:, 3:4], None, ALU.mult),
               reads=[E, cols], writes=[E], strict=True)
        P.task("vector", lambda e: e.memset(CN.ap(), 0.0), writes=[CN])
        for dlt in range(1, 8):
            P.task("vector", lambda e, dlt=dlt: e.tensor_tensor(TMPQ[:, 0:8-dlt], E[:, dlt:8], E[:, 0:8-dlt], ALU.is_gt),
                   reads=[E], writes=[TMPQ], strict=True)
            P.task("vector", lambda e, dlt=dlt: e.tensor_tensor(CN[:, 0:8-dlt], CN[:, 0:8-dlt], TMPQ[:, 0:8-dlt], ALU.add),
                   reads=[CN, TMPQ], writes=[CN], strict=True)
            P.task("vector", lambda e, dlt=dlt: e.tensor_tensor(TMPQ[:, 8-dlt:8], E[:, 0:dlt], E[:, 8-dlt:8], ALU.is_ge),
                   reads=[E], writes=[TMPQ], strict=True)
            P.task("vector", lambda e, dlt=dlt: e.tensor_tensor(CN[:, 8-dlt:8], CN[:, 8-dlt:8], TMPQ[:, 8-dlt:8], ALU.add),
                   reads=[CN, TMPQ], writes=[CN], strict=True)
        P.task("vector", lambda e: e.tensor_scalar(CN.ap(), CN.ap(), 2.0, None, ALU.is_lt),
               reads=[CN], writes=[CN], strict=True)
        P.task("vector", lambda e, i=i: e.tensor_tensor(wgt_all[:, i*8:(i+1)*8], CN.ap(), E.ap(), ALU.mult),
               reads=[CN, E], writes=[wgt_all], strict=True)

def stage_moe(P, c, W, xn_fm_bf, wgt_all, name="moe"):
    moe_acc = P.dram_tmp(f"{name}_acc", [TLOC, 1024], F32)
    for ex in range(8):
        mk = P.arena_save()
        nta = P.sbuf(f"{name}_nta{ex}", [128, 2 * 1024], F32)
        nt_ = [Buf(nta.ap()[:, (s2 % 2)*1024:((s2 % 2)+1)*1024], f"{name}_n{ex}_{s2}", state=nta) for s2 in range(4)]
        def pre(sc, ex=ex, nta=nta, half=0):
            if ex == 0:
                return
            P.dma("gpsimd", nta.ap().rearrange("p (s c) -> p s c", s=2),
                  moe_acc.ap()[sc*512 + half*256 : sc*512 + half*256 + 256, :].rearrange("(s p) c -> p s c", p=128),
                  reads=[moe_acc], writes=[nta])
        def cb(i, PO, b2b, ex=ex, nt_=nt_):
            NT = nt_[i % 4]
            if ex > 0:
                P.task("vector", lambda e, PO=PO, NT=NT, i=i, ex=ex: e.scalar_tensor_tensor(
                    NT.ap(), PO.ap(), wgt_all[:, i*8+ex:i*8+ex+1], NT.ap(), ALU.mult, ALU.add),
                    reads=[PO, wgt_all, NT], writes=[NT])
            else:
                P.task("vector", lambda e, PO=PO, NT=NT, i=i, ex=ex: e.tensor_scalar(
                    NT.ap(), PO.ap(), wgt_all[:, i*8+ex:i*8+ex+1], None, ALU.mult),
                    reads=[PO, wgt_all], writes=[NT])
            P.task("vector", lambda e, NT=NT, i=i, ex=ex, b2b=b2b: e.scalar_tensor_tensor(
                NT.ap(), b2b.ap(), wgt_all[:, i*8+ex:i*8+ex+1], NT.ap(), ALU.mult, ALU.add),
                reads=[b2b, wgt_all, NT], writes=[NT])
            if i % 2 == 1:
                P.dma("gpsimd", moe_acc.ap()[(i-1)*128:(i+1)*128, :].rearrange("(s p) c -> p s c", p=128),
                      nta.ap().rearrange("p (s c) -> p s c", s=2),
                      reads=[nta], writes=[moe_acc])
                if i % 4 == 1 and ex > 0:
                    P.dma("gpsimd", nta.ap().rearrange("p (s c) -> p s c", s=2),
                          moe_acc.ap()[(i+1)*128:(i+3)*128, :].rearrange("(s p) c -> p s c", p=128),
                          reads=[moe_acc], writes=[nta])
        ffn_fm(P, c, f"{name}{ex}", xn_fm_bf,
               Buf(W["moe_w1"].ap()[ex*1024:(ex+1)*1024, :], "w1s", state=W["moe_w1"]),
               Buf(W["moe_b1"].ap()[ex*4096:(ex+1)*4096, :], "b1s", state=W["moe_b1"]),
               Buf(W["moe_w2"].ap()[ex*4096:(ex+1)*4096, :], "w2s", state=W["moe_w2"]),
               Buf(W["moe_b2"].ap()[ex:ex+1, :], "b2s", state=W["moe_b2"]),
               1024, 4096, 1024, cb, sc_prolog=pre)
        P.arena_restore(mk)
    return moe_acc

def stage_x1(P, c, moe_acc, x_tm, attn0_tm, x1_tm, name="x1f"):
    xt = [P.sbuf(f"{name}_fx{i}", [128, 1024], F32) for i in range(2)]
    a0 = [P.sbuf(f"{name}_fa{i}", [128, 1024], F32) for i in range(2)]
    mc = [P.sbuf(f"{name}_fm{i}", [128, 1024], F32) for i in range(2)]
    for i in range(TLOC // 128):
        b = i % 2
        XT, A0, MC = xt[b], a0[b], mc[b]
        P.dma("sync", XT.ap(), x_tm[i*128:(i+1)*128, :], reads=[x_tm], writes=[XT])
        P.dma("sync", A0.ap(), attn0_tm[i*128:(i+1)*128, :], reads=[attn0_tm], writes=[A0])
        P.dma("sync", MC.ap(), moe_acc.ap()[i*128:(i+1)*128, :], reads=[moe_acc], writes=[MC])
        P.task("vector", lambda e, XT=XT, A0=A0: e.tensor_tensor(XT.ap(), XT.ap(), A0.ap(), ALU.add),
               reads=[XT, A0], writes=[XT])
        P.task("vector", lambda e, XT=XT, MC=MC: e.tensor_tensor(XT.ap(), XT.ap(), MC.ap(), ALU.add),
               reads=[XT, MC], writes=[XT])
        P.dma("gpsimd", x1_tm[i*128:(i+1)*128, :], XT.ap(), reads=[XT], writes=[x1_tm])

def stage_pe_h(P, c, W, xn1_tm, h_tm, h_fm_bf, name="pe"):
    pet = [P.sbuf(f"{name}_pe{i}", [128, 1024], F32) for i in range(2)]
    xt = [P.sbuf(f"{name}_x{i}", [128, 1024], F32) for i in range(2)]
    hbf = [P.sbuf(f"{name}_hbf{i}", [128, 1024], BF16) for i in range(2)]
    tp = [Buf(P.psum_f[2].h.bitcast(BF16).ap()[:, 0:1024], f"{name}_tp0", state=P.psum_f[2]),
          Buf(P.psum_f[3].h.bitcast(BF16).ap()[:, 0:1024], f"{name}_tp1", state=P.psum_f[3])]
    tpo = [P.sbuf(f"{name}_tpo{i}", [128, 1024], BF16) for i in range(2)]
    for i in range(TLOC // 128):
        b = i % 2
        XT, PT, HB = xt[b], pet[b], hbf[b]
        P.dma("sync", XT.ap(), xn1_tm[i*128:(i+1)*128, :], reads=[xn1_tm], writes=[XT])
        P.dma("sync", PT.ap(), W["pe_tm"].ap()[(i % 16)*128:((i % 16)+1)*128, :], reads=[W["pe_tm"]], writes=[PT])
        P.task("vector", lambda e, XT=XT, PT=PT: e.tensor_tensor(XT.ap(), XT.ap(), PT.ap(), ALU.add),
               reads=[XT, PT], writes=[XT])
        P.dma("gpsimd", h_tm[i*128:(i+1)*128, :], XT.ap(), reads=[XT], writes=[h_tm])
        P.task("scalar", lambda e, XT=XT, HB=HB: e.copy(HB.ap(), XT.ap()), reads=[XT], writes=[HB])
        TPB, TOB = tp[b], tpo[b]
        def trp(e, HB=HB, TPB=TPB):
            ins = None
            for j in range(8):
                ins = e.transpose(TPB[:, j*128:(j+1)*128], HB[:, j*128:(j+1)*128], c["ident_bf"].ap())
            return ins
        P.task("tensor", trp, reads=[HB, c["ident_bf"]], writes=[TPB])
        P.task("scalar", lambda e, TPB=TPB, TOB=TOB: e.copy(TOB.ap(), TPB.ap()), reads=[TPB], writes=[TOB])
        for j in range(8):
            P.dma("gpsimd", h_fm_bf[j*128:(j+1)*128, i*128:(i+1)*128], TOB[:, j*128:(j+1)*128],
                  reads=[TOB], writes=[h_fm_bf])

def stage_qkv(P, c, W, h_fm_bf, q_fm_bf, k_fm_bf, v_tm_bf, name="qkv"):
    wq = load_cast_bf(P, f"{name}_wq", W["Wq"], 1024, 1024)
    wk = load_cast_bf(P, f"{name}_wk", W["Wk"], 1024, 1024)
    wv = load_cast_bf(P, f"{name}_wv", W["Wv"], 1024, 1024)
    bqc = P.sbuf(f"{name}_bq", [128, 8], F32)
    bkc = P.sbuf(f"{name}_bk", [128, 8], F32)
    for j in range(8):
        P.dma("sync", bqc[:, j:j+1], W["bq"].ap()[j*128:(j+1)*128, :], reads=[W["bq"]], writes=[bqc])
        P.dma("sync", bkc[:, j:j+1], W["bk"].ap()[j*128:(j+1)*128, :], reads=[W["bk"]], writes=[bkc])
    bvb = bcast_row(P, c, f"{name}_bv", W["bv_row"], 1024)
    TC = 512
    hta = P.sbuf(f"{name}_hta", [128, 8 * TC], BF16)
    ht = [Buf(hta.ap()[:, k*TC:(k+1)*TC], f"{name}_h{k}", state=hta) for k in range(8)]
    pm = [P.pf(0, 0, TC), P.pf(1, 0, TC)]
    ob = P.sbuf(f"{name}_ob", [128, TC], BF16)
    obf = P.sbuf(f"{name}_obf", [128, TC], F32)
    for ch in range(TLOC // TC):
        t0 = ch * TC
        P.dma("sync", hta.ap().rearrange("p (k c) -> p k c", k=8),
              h_fm_bf[0:1024, t0:t0+TC].rearrange("(k p) c -> p k c", p=128),
              reads=[h_fm_bf], writes=[hta])
        for (wbf, bc, dst) in ((wq, bqc, q_fm_bf), (wk, bkc, k_fm_bf)):
            for et in range(8):
                PM = pm[et % 2]
                def mm(e, wbf=wbf, et=et, PM=PM):
                    ins = None
                    for dk in range(8):
                        ins = e.matmul(PM.ap(), wtile(wbf, dk, 1024, et*128, (et+1)*128), ht[dk].ap(),
                                       start=(dk == 0), stop=(dk == 7))
                    return ins
                P.task("tensor", mm, reads=[wbf] + ht, writes=[PM])
                P.task("vector", lambda e, PM=PM, bc=bc, et=et: e.tensor_scalar(
                    obf.ap(), PM.ap(), bc[:, et:et+1], None, ALU.add), reads=[PM, bc], writes=[obf])
                P.task("scalar", lambda e: e.copy(ob.ap(), obf.ap()), reads=[obf], writes=[ob])
                P.dma("gpsimd", dst[et*128:(et+1)*128, t0:t0+TC], ob.ap(), reads=[ob], writes=[dst])
    vt = [P.sbuf(f"{name}_vt{i}", [128, 1024], F32) for i in range(2)]
    vb = [P.sbuf(f"{name}_vb{i}", [128, 1024], BF16) for i in range(2)]
    pmv = [P.pf(0), P.pf(1)]
    hlbig = [P.sbuf(f"{name}_hlb{i}", [128, 8 * 512], BF16) for i in range(2)]
    for i in range(TLOC // 128):
        b = i % 2
        PM, VT, VB = pmv[b], vt[b], vb[b]
        ch = i // 4
        sub = i % 4
        HL = hlbig[ch % 2]
        if sub == 0:
            P.dma("sync", HL.ap().rearrange("p (k c) -> p k c", k=8),
                  h_fm_bf[0:1024, ch*512:(ch+1)*512].rearrange("(k p) c -> p k c", p=128),
                  reads=[h_fm_bf], writes=[HL])
        def mmv(e, PM=PM, HL=HL, sub=sub):
            ins = None
            for half in range(2):
                for dk in range(8):
                    ins = e.matmul(PM.ap()[:, half*512:(half+1)*512],
                                   HL[:, dk*512 + sub*128 : dk*512 + (sub+1)*128],
                                   wtile(wv, dk, 1024, half*512, (half+1)*512),
                                   start=(dk == 0), stop=(dk == 7))
            return ins
        P.task("tensor", mmv, reads=[wv, HL], writes=[PM])
        P.task("vector", lambda e, PM=PM, VT=VT: e.scalar_tensor_tensor(
            VT.ap(), PM.ap(), 1.0, bvb.ap(), ALU.mult, ALU.add), reads=[PM, bvb], writes=[VT])
        P.task("scalar", lambda e, VT=VT, VB=VB: e.copy(VB.ap(), VT.ap()), reads=[VT], writes=[VB])
        P.dma("gpsimd", v_tm_bf[i*128:(i+1)*128, :], VB.ap(), reads=[VB], writes=[v_tm_bf])

def stage_attn_core_gen(P, c, q_fm_bf, k_fm_bf, v_tm_bf, attnP_fm_bf, name="att"):
    ksl = [P.sbuf(f"{name}_k{b}", [64, SROW], BF16) for b in range(2)]
    qsl = [P.sbuf(f"{name}_q{b}", [64, 512], BF16) for b in range(2)]
    vva = [P.sbuf(f"{name}_vva{b}", [128, 16 * 65], BF16) for b in range(2)]
    vv = [[Buf(vva[b].ap()[:, kt*65:(kt+1)*65], f"{name}_v{b}_{kt}", state=vva[b]) for kt in range(16)] for b in range(2)]
    Ebf = [[P.sbuf(f"{name}_E{b}_{kt}", [128, 512], BF16) for kt in range(16)] for b in range(2)]
    rr = [P.sbuf(f"{name}_rr{b}", [1, 512], F32) for b in range(2)]
    rb = [P.sbuf(f"{name}_rb{b}", [64, 512], F32) for b in range(2)]
    ab = [P.sbuf(f"{name}_ab{b}", [64, 512], BF16) for b in range(2)]
    ps4 = [P.pf(0, 0, 512), P.pf(0, 512, 1024), P.pf(1, 0, 512), P.pf(1, 512, 1024)]
    pav = [P.pf(2, 0, 512, 0, 65, name=f"{name}_pav0"), P.pf(3, 0, 512, 0, 65, name=f"{name}_pav1")]
    prb = [P.pf(2, 512, 1024, 0, 64, name=f"{name}_prb0"), P.pf(3, 512, 1024, 0, 64, name=f"{name}_prb1")]
    for b in range(2):
        for kt in range(16):
            P.task("vector", lambda e, b=b, kt=kt: e.memset(vv[b][kt].ap()[:, 64:65], 1.0), writes=[vv[b][kt]])
    for r in range(2):
        for h in range(16):
            hb = h % 2
            KS, E1, V1 = ksl[hb], Ebf[hb], vv[hb]
            P.dma("scalar", KS.ap(), k_fm_bf[h*64:(h+1)*64, r*SROW:(r+1)*SROW], reads=[k_fm_bf], writes=[KS])
            P.dma("scalar", vva[hb].ap().rearrange("p (k c) -> p k c", k=16)[:, :, 0:64],
                  v_tm_bf[r*SROW:(r+1)*SROW, h*64:(h+1)*64].rearrange("(k p) c -> p k c", p=128),
                  reads=[v_tm_bf], writes=[vva[hb]])
            for qc in range(4):
                q0 = r*SROW + qc*512
                QS = qsl[qc % 2]
                P.dma("sync", QS.ap(), q_fm_bf[h*64:(h+1)*64, q0:q0+512], reads=[q_fm_bf], writes=[QS])
                for k4 in range(0, 16, 4):
                    PSg = [ps4[(k4 // 4 + i) % 4] for i in range(1)]
                    PS4 = [ps4[0], ps4[1], ps4[2], ps4[3]] if k4 == 0 else None
                    def mmsc(e, k4=k4, KS=KS, QS=QS):
                        ins = None
                        for i in range(4):
                            kt = k4 + i
                            ins = e.matmul(ps4[i].ap(), KS[:, kt*128:(kt+1)*128], QS.ap(),
                                           start=True, stop=True)
                        return ins
                    P.task("tensor", mmsc, reads=[KS, QS], writes=ps4)
                    def expv(e, k4=k4, E1=E1):
                        ins = None
                        for i in range(4):
                            kt = k4 + i
                            ins = e.activation(E1[kt].ap(), ps4[i].ap(), AF.Exp, scale=P.constf(0.125))
                        return ins
                    P.task("scalar", expv, reads=ps4, writes=[E1[k4], E1[k4+1], E1[k4+2], E1[k4+3]])
                PAV, PRB = pav[hb], prb[hb]
                def mmav(e, E1=E1, V1=V1, PAV=PAV):
                    ins = None
                    for kt in range(16):
                        ins = e.matmul(PAV.ap(), V1[kt].ap(), E1[kt].ap(), start=(kt == 0), stop=(kt == 15))
                    return ins
                P.task("tensor", mmav, reads=V1 + E1, writes=[PAV])
                RR, RB, AB = rr[hb], rb[hb], ab[hb]
                P.task("vector", lambda e, PAV=PAV, RR=RR: e.reciprocal(RR.ap(), PAV.ap()[64:65, :]),
                       reads=[PAV], writes=[RR])
                P.task("tensor", lambda e, RR=RR, PRB=PRB: e.matmul(PRB.ap(), c["ones_row"][0:1, 0:64], RR.ap(),
                                                    start=True, stop=True),
                       reads=[c["ones_row"], RR], writes=[PRB])
                P.task("scalar", lambda e, PRB=PRB, RB=RB: e.copy(RB.ap(), PRB.ap()), reads=[PRB], writes=[RB])
                P.task("vector", lambda e, PAV=PAV, RB=RB, AB=AB: e.tensor_tensor(AB.ap(), PAV.ap()[0:64, :], RB.ap(), ALU.mult),
                       reads=[PAV, RB], writes=[AB])
                P.dma("gpsimd", attnP_fm_bf[h*64:(h+1)*64, q0:q0+512], AB.ap(), reads=[AB], writes=[attnP_fm_bf])
            yield (r, h)

def stage_wo_ln_gen(P, c, W, attnP_fm_bf, h_tm, attn1_tm, name="wo"):
    wo = load_cast_bf(P, f"{name}_wo", W["Wo"], 1024, 1024)
    bob = bcast_row(P, c, f"{name}_bo", W["bo_row"], 1024)
    gb = bcast_row(P, c, f"{name}_g", W["gamma_row"], 1024)
    bb = bcast_row(P, c, f"{name}_b", W["beta_row"], 1024)
    albig = [P.sbuf(f"{name}_alb{i}", [128, 8 * 512], BF16) for i in range(1)]
    pm = [P.pf(0), P.pf(1)]
    ht = [P.sbuf(f"{name}_h{i}", [128, 1024], F32) for i in range(2)]
    yt = [P.sbuf(f"{name}_y{i}", [128, 1024], F32) for i in range(1)]
    sq = P.sbuf(f"{name}_sq", [128, 1024], F32)
    cols = P.sbuf(f"{name}_cols", [128, 8], F32)
    for i in range(TLOC // 128):
        b = i % 2
        PM, HT, YT = pm[b], ht[b], yt[0]
        ch = i // 4
        sub = i % 4
        AL = albig[0]
        if sub == 0:
            P.dma("sync", AL.ap().rearrange("p (k c) -> p k c", k=8),
                  attnP_fm_bf[0:1024, ch*512:(ch+1)*512].rearrange("(k p) c -> p k c", p=128),
                  reads=[attnP_fm_bf], writes=[AL])
        P.dma("sync", HT.ap(), h_tm[i*128:(i+1)*128, :], reads=[h_tm], writes=[HT])
        def mm(e, PM=PM, AL=AL, sub=sub):
            ins = None
            for half in range(2):
                for k in range(8):
                    ins = e.matmul(PM.ap()[:, half*512:(half+1)*512],
                                   AL[:, k*512 + sub*128 : k*512 + (sub+1)*128],
                                   wtile(wo, k, 1024, half*512, (half+1)*512),
                                   start=(k == 0), stop=(k == 7))
            return ins
        P.task("tensor", mm, reads=[AL, wo], writes=[PM])
        def boh(e, PM=PM, YT=YT, HT=HT):
            e.scalar_tensor_tensor(YT.ap(), PM.ap(), 1.0, bob.ap(), ALU.mult, ALU.add)
            return e.tensor_tensor(YT.ap(), YT.ap(), HT.ap(), ALU.add)
        P.task("vector", boh, reads=[PM, bob, HT], writes=[YT])
        P.task("vector", lambda e, YT=YT: e.tensor_reduce(cols[:, 0:1], YT.ap(), AX.X, ALU.add),
               reads=[YT], writes=[cols])
        P.task("vector", lambda e: e.tensor_scalar(cols[:, 1:2], cols[:, 0:1], -1.0/1024, None, ALU.mult),
               reads=[cols], writes=[cols], strict=True)
        P.task("vector", lambda e, YT=YT: e.tensor_scalar(YT.ap(), YT.ap(), cols[:, 1:2], None, ALU.add),
               reads=[YT, cols], writes=[YT], strict=True)
        P.task("scalar", lambda e, YT=YT: e.activation(sq.ap(), YT.ap(), AF.Square, accum_out=cols[:, 2:3]),
               reads=[YT], writes=[sq, cols])
        P.task("scalar", lambda e: e.activation(cols[:, 3:4], cols[:, 2:3], AF.Ln, bias=P.constf(1e-5), scale=P.constf(1.0/1024)),
               reads=[cols], writes=[cols], strict=True)
        P.task("scalar", lambda e: e.activation(cols[:, 4:5], cols[:, 3:4], AF.Exp, scale=P.constf(-0.5)),
               reads=[cols], writes=[cols], strict=True)
        def lnap(e, YT=YT):
            e.tensor_scalar(YT.ap(), YT.ap(), cols[:, 4:5], None, ALU.mult)
            e.tensor_tensor(YT.ap(), YT.ap(), gb.ap(), ALU.mult)
            return e.tensor_tensor(YT.ap(), YT.ap(), bb.ap(), ALU.add)
        P.task("vector", lnap, reads=[YT, cols, gb, bb], writes=[YT])
        P.dma("gpsimd", attn1_tm[i*128:(i+1)*128, :], YT.ap(), reads=[YT], writes=[attn1_tm])
        yield i

def stage_route_gen(P, c, W, xn1_tm, normed_fm_bf, mask_cols, name="rte"):
    SQ1024 = float(np.float32(np.sqrt(np.float32(1024.0))))
    rtb = bcast_row(P, c, f"{name}_rt", W["rt_row"], 1024)
    iota_bc_ps = [P.pf(0, 0, 512), P.pf(1, 0, 512)]
    iota_bc = P.sbuf(f"{name}_iota", [128, SROW], F32)
    iota_row = Buf(P.rowtmp.ap()[0:1, 0:SROW], f"{name}_ior", state=P.rowtmp)
    idx_col = P.sbuf(f"{name}_idx", [128, 2], F32)
    P.dma("sync", iota_row[0:1, 0:SROW], W["iota_row"].ap(), reads=[W["iota_row"]], writes=[iota_row])
    P.dma("sync", idx_col[:, 0:1], W["idx_col"].ap(), reads=[W["idx_col"]], writes=[idx_col])
    for j0 in range(0, SROW, 512):
        PSI = iota_bc_ps[(j0 // 512) % 2]
        P.task("tensor", lambda e, j0=j0, PSI=PSI: e.matmul(PSI.ap(), c["ones_row"].ap(),
               P.rowtmp.ap()[0:1, j0:j0+512], start=True, stop=True), reads=[c["ones_row"], iota_row], writes=[PSI])
        P.task("scalar", lambda e, j0=j0, PSI=PSI: e.copy(iota_bc[:, j0:j0+512], PSI.ap()),
               reads=[PSI], writes=[iota_bc])
    xt = [P.sbuf(f"{name}_x{i}", [128, 1024], F32) for i in range(2)]
    nb = [P.sbuf(f"{name}_nb{i}", [128, 1024], BF16) for i in range(2)]
    sq = P.sbuf(f"{name}_sq", [128, 1024], F32)
    cols = P.sbuf(f"{name}_cols", [128, 8], F32)
    s_cols = P.sbuf(f"{name}_scols", [128, 32], F32)
    tp = [Buf(P.psum_f[2].h.bitcast(BF16).ap()[:, 0:1024], f"{name}_tp0", state=P.psum_f[2]),
          Buf(P.psum_f[3].h.bitcast(BF16).ap()[:, 0:1024], f"{name}_tp1", state=P.psum_f[3])]
    tpo = [P.sbuf(f"{name}_tpo{i}", [128, 1024], BF16) for i in range(2)]
    for i in range(TLOC // 128):
        b = i % 2
        X, NB = xt[b], nb[b]
        P.dma("sync", X.ap(), xn1_tm[i*128:(i+1)*128, :], reads=[xn1_tm], writes=[X])
        P.task("scalar", lambda e, X=X: e.activation(sq.ap(), X.ap(), AF.Square, accum_out=cols[:, 0:1]),
               reads=[X], writes=[sq, cols])
        P.task("scalar", lambda e: e.activation(cols[:, 1:2], cols[:, 0:1], AF.Ln),
               reads=[cols], writes=[cols], strict=True)
        P.task("scalar", lambda e: e.activation(cols[:, 2:3], cols[:, 1:2], AF.Exp, scale=P.constf(0.5)),
               reads=[cols], writes=[cols], strict=True)
        P.task("vector", lambda e: e.tensor_scalar(cols[:, 3:4], cols[:, 2:3], 1e-12, None, ALU.max),
               reads=[cols], writes=[cols], strict=True)
        P.task("vector", lambda e: e.reciprocal(cols[:, 4:5], cols[:, 3:4]), reads=[cols], writes=[cols], strict=True)
        P.task("vector", lambda e, X=X, i=i: e.scalar_tensor_tensor(
            sq.ap(), X.ap(), 1.0, rtb.ap(), ALU.mult, ALU.mult, accum_out=s_cols[:, i:i+1]),
            reads=[X, rtb], writes=[sq, s_cols])
        P.task("vector", lambda e, X=X: e.tensor_scalar(X.ap(), X.ap(), cols[:, 4:5], SQ1024, ALU.mult, ALU.mult),
               reads=[X, cols], writes=[X], strict=True)
        P.task("scalar", lambda e, X=X, NB=NB: e.copy(NB.ap(), X.ap()), reads=[X], writes=[NB])
        TPB, TOB = tp[b], tpo[b]
        def trp(e, NB=NB, TPB=TPB):
            ins = None
            for j in range(8):
                ins = e.transpose(TPB[:, j*128:(j+1)*128], NB[:, j*128:(j+1)*128], c["ident_bf"].ap())
            return ins
        P.task("tensor", trp, reads=[NB, c["ident_bf"]], writes=[TPB])
        P.task("scalar", lambda e, TPB=TPB, TOB=TOB: e.copy(TOB.ap(), TPB.ap()), reads=[TPB], writes=[TOB])
        for j in range(8):
            P.dma("gpsimd", normed_fm_bf[j*128:(j+1)*128, i*128:(i+1)*128], TOB[:, j*128:(j+1)*128],
                  reads=[TOB], writes=[normed_fm_bf])
        yield ("n", i)
    s_lin = P.dram_tmp(f"{name}_slin", [32, 128], F32)
    pt = P.pf(2, 0, 128, 0, 32, name=f"{name}_pt")
    sro = P.sbuf(f"{name}_sro", [32, 128], F32)
    P.task("tensor", lambda e: e.transpose(pt.ap(), s_cols.ap(), c["ident_f"].ap()), reads=[s_cols, c["ident_f"]], writes=[pt])
    P.task("scalar", lambda e: e.copy(sro.ap(), pt.ap()), reads=[pt], writes=[sro])
    P.dma("gpsimd", s_lin.ap(), sro.ap(), reads=[sro], writes=[s_lin])
    CT = P.sbuf(f"{name}_ct", [64, SROW], F32)
    NBv = P.sbuf(f"{name}_nbv", [64, SROW], F32)
    SB = P.sbuf(f"{name}_sb", [64, SROW], F32)
    EX = P.sbuf(f"{name}_ex", [64, SROW], F32)
    CC = P.sbuf(f"{name}_cc", [64, 8], F32)
    P.task("vector", lambda e: e.memset(CT.ap(), 0.0), writes=[CT])
    P.dma("sync", CT.ap()[0:1, :], s_lin.ap()[0:16, :], reads=[s_lin], writes=[CT])
    P.dma("sync", CT.ap()[32:33, :], s_lin.ap()[16:32, :], reads=[s_lin], writes=[CT])
    P.task("vector", lambda e: e.tensor_scalar(NBv.ap(), CT.ap(), 0.0, None, ALU.max),
           reads=[CT], writes=[NBv])
    INV = INV01
    CONSTLN = float(np.float32(0.1) * np.float32(np.log(np.float32(8.0))))
    yield ("coor0", 0)
    for it in range(50):
        P.task("vector", lambda e: e.tensor_tensor(SB.ap(), CT.ap(), NBv.ap(), ALU.subtract),
               reads=[CT, NBv], writes=[SB])
        P.task("vector", lambda e: e.tensor_reduce(CC[0:64, 0:1], SB.ap(), AX.X, ALU.max),
               reads=[SB], writes=[CC])
        P.task("vector", lambda e: e.tensor_scalar(CC[0:64, 1:2], CC[0:64, 0:1], -INV, None, ALU.mult),
               reads=[CC], writes=[CC], strict=True)
        P.task("scalar", lambda e: e.activation(EX.ap(), SB.ap(), AF.Exp, bias=CC[0:64, 1:2],
                                                scale=P.constf(INV, 64), accum_out=CC[0:64, 2:3]),
               reads=[SB, CC], writes=[EX, CC])
        P.task("scalar", lambda e: e.activation(CC[0:64, 3:4], CC[0:64, 2:3], AF.Ln), reads=[CC], writes=[CC], strict=True)
        P.task("vector", lambda e: e.tensor_scalar(CC[0:64, 4:5], CC[0:64, 0:1], -1.0, CONSTLN, ALU.mult, ALU.add),
               reads=[CC], writes=[CC], strict=True)
        P.task("vector", lambda e: e.tensor_scalar(CC[0:64, 5:6], CC[0:64, 3:4], -float(np.float32(0.1)), None, ALU.mult),
               reads=[CC], writes=[CC], strict=True)
        P.task("vector", lambda e: e.tensor_tensor(CC[0:64, 6:7], CC[0:64, 5:6], CC[0:64, 4:5], ALU.add),
               reads=[CC], writes=[CC], strict=True)
        P.task("vector", lambda e: e.tensor_scalar(NBv.ap(), CT.ap(), CC[0:64, 6:7], 0.0, ALU.add, ALU.max),
               reads=[CT, CC], writes=[NBv], strict=True)
        if it % 5 == 4:
            yield ("coor", it)
    P.task("vector", lambda e: e.tensor_scalar(SB.ap(), CT.ap(), CC[0:64, 6:7], 0.0, ALU.add, ALU.min),
           reads=[CT, CC], writes=[SB], strict=True)
    P.task("scalar", lambda e: e.activation(EX.ap(), SB.ap(), AF.Exp, scale=P.constf(INV, 64)),
           reads=[SB], writes=[EX])
    FLTMIN = float(np.float32(1.17549435e-38))
    P.task("vector", lambda e: e.tensor_scalar(SB.ap(), EX.ap(), FLTMIN, None, ALU.is_ge),
           reads=[EX], writes=[SB])
    P.task("vector", lambda e: e.tensor_tensor(EX.ap(), EX.ap(), SB.ap(), ALU.mult),
           reads=[EX, SB], writes=[EX])
    Sfree = P.sbuf(f"{name}_sf", [128, SROW], F32)
    sccol = P.sbuf(f"{name}_sc", [128, 16], F32)
    gsum = P.sbuf(f"{name}_gs", [128, 8], F32)
    mlt = P.sbuf(f"{name}_ml", [128, SROW], F32)
    junk2 = mlt
    pr0 = P.pf(0, 0, 512); pr1 = P.pf(1, 0, 512)
    prt = P.pf(2, 0, 64, 0, 128, name=f"{name}_prt")
    for r in range(2):
        base = 32 * r
        for j0 in range(0, SROW, 512):
            PS = pr0 if (j0 // 512) % 2 == 0 else pr1
            P.task("tensor", lambda e, j0=j0, PS=PS, base=base: e.matmul(
                PS.ap(), c["ones_all"][base:base+1, :], EX[base:base+1, j0:j0+512], start=True, stop=True),
                reads=[c["ones_all"], EX], writes=[PS])
            P.task("scalar", lambda e, j0=j0, PS=PS: e.copy(Sfree[:, j0:j0+512], PS.ap()),
                   reads=[PS], writes=[Sfree])
        for ic in range(16):
            P.task("tensor", lambda e, ic=ic, base=base: e.matmul(
                prt.ap()[:, 0:1], EX[base:base+1, ic*128:(ic+1)*128], c["ones_all"][base:base+1, 0:1],
                start=True, stop=True), reads=[EX, c["ones_all"]], writes=[prt])
            P.task("scalar", lambda e, ic=ic: e.copy(sccol[:, ic:ic+1], prt.ap()[:, 0:1]),
                   reads=[prt], writes=[sccol])
        for ic in range(16):
            P.task("vector", lambda e, ic=ic, r=r: e.tensor_scalar(idx_col[:, 1:2], idx_col[:, 0:1],
                   float(ic*128), None, ALU.add), reads=[idx_col], writes=[idx_col])
            P.task("vector", lambda e: e.tensor_scalar(mlt.ap(), iota_bc.ap(), idx_col[:, 1:2], None, ALU.is_lt),
                   reads=[iota_bc, idx_col], writes=[mlt], strict=True)
            P.task("vector", lambda e, ic=ic: e.scalar_tensor_tensor(junk2.ap(), Sfree.ap(), sccol[:, ic:ic+1],
                   mlt.ap(), ALU.is_equal, ALU.mult, accum_out=gsum[:, 0:1]),
                   reads=[Sfree, sccol, mlt], writes=[junk2, gsum])
            P.task("vector", lambda e, ic=ic: e.tensor_scalar(junk2.ap(), Sfree.ap(), sccol[:, ic:ic+1],
                   0.0, ALU.is_gt, ALU.add, accum_out=gsum[:, 1:2]),
                   reads=[Sfree, sccol], writes=[junk2, gsum])
            P.task("vector", lambda e: e.tensor_tensor(gsum[:, 2:3], gsum[:, 0:1], gsum[:, 1:2], ALU.add),
                   reads=[gsum], writes=[gsum], strict=True)
            P.task("vector", lambda e, r=r, ic=ic: e.tensor_scalar(mask_cols[:, r*16+ic:r*16+ic+1],
                   gsum[:, 2:3], 1024.0, None, ALU.is_lt), reads=[gsum], writes=[mask_cols], strict=True)
            yield ("rank", r*16+ic)

def stage_ffn1(P, c, W, normed_fm_bf, mask_cols, x1_tm, attn1_tm, out_tm, name="f1"):
    light_tm = P.dram_tmp(f"{name}_lt", [TLOC, 1024], F32)
    mk = P.arena_save()
    lt = [P.sbuf(f"{name}_l{i}", [128, 1024], F32) for i in range(2)]
    def cbl(i, PO, b2b):
        LT = lt[i % 2]
        P.task("vector", lambda e, PO=PO, LT=LT, b2b=b2b: e.scalar_tensor_tensor(
            LT.ap(), PO.ap(), 1.0, b2b.ap(), ALU.mult, ALU.add), reads=[PO, b2b], writes=[LT])
        P.dma("gpsimd", light_tm.ap()[i*128:(i+1)*128, :], LT.ap(), reads=[LT], writes=[light_tm])
    ffn_fm(P, c, f"{name}l", normed_fm_bf, W["w1_light"], W["b1_light"], W["w2_light"], W["b2_light_row"],
           1024, 512, 1024, cbl, gamma_col_dram=W["gamma_light_col"])
    P.arena_restore(mk)
    xt = [P.sbuf(f"{name}_x{i}", [128, 1024], F32) for i in range(2)]
    a1 = [P.sbuf(f"{name}_a{i}", [128, 1024], F32) for i in range(1)]
    def cbh(i, PO, b2b):
        b = i % 2
        XT, A1 = xt[b], a1[0]
        P.dma("scalar", XT.ap(), x1_tm[i*128:(i+1)*128, :], reads=[x1_tm], writes=[XT])
        P.dma("scalar", A1.ap(), attn1_tm[i*128:(i+1)*128, :], reads=[attn1_tm], writes=[A1])
        P.task("vector", lambda e, XT=XT, A1=A1: e.tensor_tensor(XT.ap(), XT.ap(), A1.ap(), ALU.add),
               reads=[XT, A1], writes=[XT])
        P.dma("scalar", A1.ap(), light_tm.ap()[i*128:(i+1)*128, :], reads=[light_tm], writes=[A1])
        P.task("vector", lambda e, XT=XT, A1=A1: e.tensor_tensor(XT.ap(), XT.ap(), A1.ap(), ALU.add),
               reads=[XT, A1], writes=[XT])
        P.task("vector", lambda e, PO=PO, XT=XT, i=i: e.scalar_tensor_tensor(
            XT.ap(), PO.ap(), mask_cols[:, i:i+1], XT.ap(), ALU.mult, ALU.add),
            reads=[PO, mask_cols, XT], writes=[XT])
        P.task("vector", lambda e, XT=XT, i=i, b2b=b2b: e.scalar_tensor_tensor(
            XT.ap(), b2b.ap(), mask_cols[:, i:i+1], XT.ap(), ALU.mult, ALU.add),
            reads=[b2b, mask_cols, XT], writes=[XT])
        P.dma("gpsimd", out_tm[i*128:(i+1)*128, :], XT.ap(), reads=[XT], writes=[out_tm])
    ffn_fm(P, c, f"{name}h", normed_fm_bf, W["w1_heavy"], W["b1_heavy"], W["w2_heavy"], W["b2_heavy_row"],
           1024, 4096, 1024, cbh, gamma_col_dram=W["gamma_heavy_col"])


def stage_x1_ln1_pe(P, c, W, mlpz_tm, moe_acc, x_tm, x1_tm, xn1_tm, h_tm, h_fm_bf, name="xp"):
    n1w = bcast_row(P, c, f"{name}_n1w", W["norm1_w"], 1024)
    xt = [P.sbuf(f"{name}_x{i}", [128, 1024], F32) for i in range(2)]
    mz = [P.sbuf(f"{name}_mz{i}", [128, 1024], F32) for i in range(2)]
    mc = [P.sbuf(f"{name}_mc{i}", [128, 1024], F32) for i in range(2)]
    pet = [P.sbuf(f"{name}_pe{i}", [128, 1024], F32) for i in range(2)]
    hbf = [P.sbuf(f"{name}_hbf{i}", [128, 1024], BF16) for i in range(2)]
    sq = P.sbuf(f"{name}_sq", [128, 1024], F32)
    cols = [P.sbuf(f"{name}_cols{i}", [128, 8], F32) for i in range(2)]
    tp = [Buf(P.psum_f[2].h.bitcast(BF16).ap()[:, 0:1024], f"{name}_tp0", state=P.psum_f[2]),
          Buf(P.psum_f[3].h.bitcast(BF16).ap()[:, 0:1024], f"{name}_tp1", state=P.psum_f[3])]
    tpo = [P.sbuf(f"{name}_tpo{i}", [128, 1024], BF16) for i in range(2)]
    for i in range(TLOC // 128):
        b = i % 2
        X, MZ, MC, PT, HB, CL = xt[b], mz[b], mc[b], pet[b], hbf[b], cols[b]
        P.dma("sync", MZ.ap(), mlpz_tm[i*128:(i+1)*128, :], reads=[mlpz_tm], writes=[MZ])
        P.dma("sync", X.ap(), x_tm[i*128:(i+1)*128, :], reads=[x_tm], writes=[X])
        P.dma("gpsimd", MC.ap(), moe_acc.ap()[i*128:(i+1)*128, :], reads=[moe_acc], writes=[MC])
        P.dma("scalar", PT.ap(), W["pe_tm"].ap()[(i % 16)*128:((i % 16)+1)*128, :], reads=[W["pe_tm"]], writes=[PT])
        rms_cols(P, (sq, CL), MZ.ap(), [MZ], 1e-6, 1024)
        def a0x1(e, X=X, MZ=MZ, MC=MC, CL=CL):
            e.scalar_tensor_tensor(MZ.ap(), MZ.ap(), CL[:, 2:3], n1w.ap(), ALU.mult, ALU.mult)
            e.tensor_tensor(X.ap(), X.ap(), MZ.ap(), ALU.add)
            return e.tensor_tensor(X.ap(), X.ap(), MC.ap(), ALU.add)
        P.task("vector", a0x1, reads=[X, MZ, MC, CL, n1w], writes=[X, MZ], strict=True)
        P.dma("gpsimd", x1_tm[i*128:(i+1)*128, :], X.ap(), reads=[X], writes=[x1_tm])
        P.task("vector", lambda e, X=X, CL=CL: e.tensor_reduce(CL[:, 0:1], X.ap(), AX.X, ALU.add),
               reads=[X], writes=[CL])
        P.task("vector", lambda e, CL=CL: e.tensor_scalar(CL[:, 1:2], CL[:, 0:1], -1.0/1024, None, ALU.mult),
               reads=[CL], writes=[CL], strict=True)
        P.task("vector", lambda e, X=X, CL=CL: e.tensor_scalar(X.ap(), X.ap(), CL[:, 1:2], None, ALU.add),
               reads=[X, CL], writes=[X], strict=True)
        P.task("scalar", lambda e, X=X, CL=CL: e.activation(sq.ap(), X.ap(), AF.Square, accum_out=CL[:, 2:3]),
               reads=[X], writes=[sq, CL])
        P.task("scalar", lambda e, CL=CL: e.activation(CL[:, 3:4], CL[:, 2:3], AF.Ln, bias=P.constf(1e-5), scale=P.constf(1.0/1024)),
               reads=[CL], writes=[CL], strict=True)
        P.task("scalar", lambda e, CL=CL: e.activation(CL[:, 4:5], CL[:, 3:4], AF.Exp, scale=P.constf(-0.5)),
               reads=[CL], writes=[CL], strict=True)
        def xnh(e, X=X, CL=CL, PT=PT):
            e.tensor_scalar(X.ap(), X.ap(), CL[:, 4:5], None, ALU.mult)
            return e.tensor_tensor(PT.ap(), X.ap(), PT.ap(), ALU.add)
        P.task("vector", xnh, reads=[X, CL, PT], writes=[X, PT], strict=True)
        P.dma("gpsimd", xn1_tm[i*128:(i+1)*128, :], X.ap(), reads=[X], writes=[xn1_tm])
        P.dma("gpsimd", h_tm[i*128:(i+1)*128, :], PT.ap(), reads=[PT], writes=[h_tm])
        P.task("scalar", lambda e, PT=PT, HB=HB: e.copy(HB.ap(), PT.ap()), reads=[PT], writes=[HB])
        TPB, TOB = tp[b], tpo[b]
        def trp(e, HB=HB, TPB=TPB):
            ins = None
            for j in range(8):
                ins = e.transpose(TPB[:, j*128:(j+1)*128], HB[:, j*128:(j+1)*128], c["ident_bf"].ap())
            return ins
        P.task("tensor", trp, reads=[HB, c["ident_bf"]], writes=[TPB])
        P.task("scalar", lambda e, TPB=TPB, TOB=TOB: e.copy(TOB.ap(), TPB.ap()), reads=[TPB], writes=[TOB])
        for j in range(8):
            P.dma("gpsimd", h_fm_bf[j*128:(j+1)*128, i*128:(i+1)*128], TOB[:, j*128:(j+1)*128],
                  reads=[TOB], writes=[h_fm_bf])

def build_program():
    TL = TLOC
    P = Prog()
    c = emit_consts(P)
    wgt_all = P.persist("wgt_all", [128, 256], F32)
    mask_cols = P.persist("mask_cols", [128, 32], F32)
    D_in = lambda n, sh: P.dram_in(n, sh, F32)
    W = {k: D_in(k, sh) for k, sh in [
        ("W_inT", [1024, 4096]), ("x_projT", [2048, 96]), ("dt_projT", [64, 2048]),
        ("conv_w", [2048, 4]), ("conv_b", [2048, 1]), ("dt_b", [2048, 1]),
        ("A_log", [2048, 16]), ("Dd", [2048, 1]),
        ("out_projT", [2048, 1024]), ("norm1_w", [1, 1024]),
        ("mlp_w1T", [1024, 4096]), ("mlp_b1", [4096, 1]), ("mlp_w2T", [4096, 1024]), ("mlp_b2", [1, 1024]),
        ("router_w", [8, 1024]),
        ("moe_w1", [8192, 4096]), ("moe_b1", [32768, 1]), ("moe_w2", [32768, 1024]), ("moe_b2", [8, 1024]),
        ("pe_tm", [2048, 1024]),
        ("Wq", [1024, 1024]), ("Wk", [1024, 1024]), ("Wv", [1024, 1024]), ("Wo", [1024, 1024]),
        ("bq", [1024, 1]), ("bk", [1024, 1]), ("bv_row", [1, 1024]), ("bo_row", [1, 1024]),
        ("gamma_row", [1, 1024]), ("beta_row", [1, 1024]),
        ("rt_row", [1, 1024]), ("iota_row", [1, 2048]), ("idx_col", [128, 1]),
        ("w1_light", [1024, 512]), ("b1_light", [512, 1]), ("w2_light", [512, 1024]),
        ("b2_light_row", [1, 1024]), ("gamma_light_col", [1024, 1]),
        ("w1_heavy", [1024, 4096]), ("b1_heavy", [4096, 1]), ("w2_heavy", [4096, 1024]),
        ("b2_heavy_row", [1, 1024]), ("gamma_heavy_col", [1024, 1]),
    ]}
    oh_dram = None
    x_in = P.dram_in("x", [TL, 1024], F32)
    out_tm = P.dram_out("out", [TL, 1024], F32)
    t = lambda n, sh, dt=F32: P.dram_tmp(n, sh, dt)
    import os
    dbg = os.environ.get("FULL_DBG")
    if dbg:
        t = lambda n, sh, dt=F32: (P.dram_out(n, sh, dt) if n in ("attn0_tm", "x1_tm", "attn1_tm", "zmam_tm", "mlpz_tm") else P.dram_tmp(n, sh, dt))
    xn_tm = t("xn_tm", [TL, 1024]); xn_fm_bf = t("xn_fm", [1024, TL], BF16)
    xc_fm = t("xc_fm", [2048, TL]); zs_fm = t("zs_fm", [2048, TL]); dt_fm = t("dt_fm", [2048, TL])
    xdbl_fm = t("xdbl_fm", [96, TL]); bc_fm = t("bc_fm", [32, TL])
    y_fm_bf = t("y_fm", [2048, TL], BF16)
    zmam_tm = t("zmam_tm", [TL, 1024]); zmam_fm_bf = t("zmam_fm", [1024, TL], BF16)
    mlpz_tm = t("mlpz_tm", [TL, 1024]); attn0_tm = t("attn0_tm", [TL, 1024])
    x1_tm = t("x1_tm", [TL, 1024]); xn1_tm = t("xn1_tm", [TL, 1024])
    h_tm = t("h_tm", [TL, 1024]); h_fm_bf = t("h_fm", [1024, TL], BF16)
    q_fm_bf = t("q_fm", [1024, TL], BF16); k_fm_bf = t("k_fm", [1024, TL], BF16)
    v_tm_bf = t("v_tm", [TL, 1024], BF16)
    attnP_fm_bf = t("attnP_fm", [1024, TL], BF16); attn1_tm = t("attn1_tm", [TL, 1024])
    normed_fm_bf = t("normed_fm", [1024, TL], BF16)

    m0 = P.arena_save()
    P.stage_marks = []
    def MARK(nm):
        P.stage_marks.append((nm, dict(P.count)))
    P.MARK = MARK
    stage_ln(P, c, x_in, xn_tm, xn_fm_bf, name="ln0"); MARK("ln0"); P.arena_restore(m0)
    stage_mamba_front(P, c, W, xn_fm_bf, xc_fm, zs_fm, dt_fm, xdbl_fm, bc_fm, name="mf"); MARK("mf"); P.arena_restore(m0)
    for _ in stage_scan_gen(P, c, W, dt_fm, xc_fm, zs_fm, bc_fm, y_fm_bf, name="sc"):
        pass
    MARK("sc"); P.arena_restore(m0)
    for _ in stage_outproj_rms_gen(P, c, W, y_fm_bf, xn_tm, zmam_tm, zmam_fm_bf, name="s4a"):
        pass
    MARK("s4a"); P.arena_restore(m0)
    stage_router(P, c, W, xn_tm, wgt_all, name="rt"); MARK("rt"); P.arena_restore(m0)
    stage_mlp(P, c, W, zmam_fm_bf, zmam_tm, mlpz_tm, name="s4b"); MARK("s4b"); P.arena_restore(m0)
    moe_acc = stage_moe(P, c, W, xn_fm_bf, wgt_all, name="moe"); MARK("moe"); P.arena_restore(m0)
    stage_x1_ln1_pe(P, c, W, mlpz_tm, moe_acc, x_in, x1_tm, xn1_tm, h_tm, h_fm_bf, name="xp"); MARK("xp"); P.arena_restore(m0)
    stage_qkv(P, c, W, h_fm_bf, q_fm_bf, k_fm_bf, v_tm_bf, name="qkv"); MARK("qkv"); P.arena_restore(m0)
    g_att = stage_attn_core_gen(P, c, q_fm_bf, k_fm_bf, v_tm_bf, attnP_fm_bf, name="att")
    g_wo = stage_wo_ln_gen(P, c, W, attnP_fm_bf, h_tm, attn1_tm, name="wo")
    g_rte = stage_route_gen(P, c, W, xn1_tm, normed_fm_bf, mask_cols, name="rte")
    def pump(g, n):
        for _ in range(n):
            try:
                next(g)
            except StopIteration:
                return False
        return True
    for step in range(32):
        pump(g_att, 1)
        pump(g_rte, 3)
        if step >= 16:
            pump(g_wo, 1)
    while pump(g_rte, 1):
        pass
    while pump(g_wo, 1):
        pass
    MARK("att+wo+rte"); P.arena_restore(m0)
    stage_ffn1(P, c, W, normed_fm_bf, mask_cols, x1_tm, attn1_tm, out_tm, name="f1"); MARK("f1")
    P.finish_all()
    return P

def host_weights(z):
    asc = np.ascontiguousarray
    S_ = 2048; D_ = 1024
    pos = np.arange(S_, dtype=np.float64)[:, None]
    div = np.exp(np.arange(0, D_, 2, dtype=np.float64) * (-np.log(10000.0) / D_))
    pe = np.zeros((S_, D_), dtype=np.float32)
    pe[:, 0::2] = np.sin(pos * div).astype(np.float32)
    pe[:, 1::2] = np.cos(pos * div).astype(np.float32)
    w = {
        "W_inT": asc(z["L0_attn_in_proj_w"].T), "x_projT": asc(z["L0_attn_x_proj_w"].T),
        "dt_projT": asc(z["L0_attn_dt_proj_w"].T), "conv_w": asc(z["L0_attn_conv_w"]),
        "conv_b": z["L0_attn_conv_b"].reshape(-1, 1), "dt_b": z["L0_attn_dt_proj_b"].reshape(-1, 1),
        "A_log": asc(z["L0_attn_A_log"]), "Dd": z["L0_attn_D"].reshape(-1, 1),
        "out_projT": asc(z["L0_attn_out_proj_w"].T), "norm1_w": z["L0_attn_norm1_w"].reshape(1, -1),
        "mlp_w1T": asc(z["L0_attn_mlp_w1"].T), "mlp_b1": z["L0_attn_mlp_b1"].reshape(-1, 1),
        "mlp_w2T": asc(z["L0_attn_mlp_w2"].T), "mlp_b2": z["L0_attn_mlp_b2"].reshape(1, -1),
        "router_w": asc(z["L0_ffn_router_w"]),
        "moe_w1": asc(z["L0_ffn_w1"].reshape(8 * 1024, 4096)),
        "moe_b1": z["L0_ffn_b1"].reshape(-1, 1),
        "moe_w2": asc(z["L0_ffn_w2"].reshape(8 * 4096, 1024)),
        "moe_b2": asc(z["L0_ffn_b2"]),
        "pe_tm": pe,
        "Wq": asc(z["L1_attn_Wq"]), "Wk": asc(z["L1_attn_Wk"]),
        "Wv": asc(z["L1_attn_Wv"]), "Wo": asc(z["L1_attn_Wo"]),
        "bq": z["L1_attn_bq"].reshape(-1, 1), "bk": z["L1_attn_bk"].reshape(-1, 1),
        "bv_row": z["L1_attn_bv"].reshape(1, -1), "bo_row": z["L1_attn_bo"].reshape(1, -1),
        "gamma_row": z["L1_attn_gamma"].reshape(1, -1), "beta_row": z["L1_attn_beta"].reshape(1, -1),
        "rt_row": z["L1_ffn_routing_token"].reshape(1, -1),
        "iota_row": np.arange(2048, dtype=np.float32).reshape(1, -1),
        "idx_col": np.arange(128, dtype=np.float32).reshape(-1, 1),
        "w1_light": asc(z["L1_ffn_w1_light"]), "b1_light": z["L1_ffn_b1_light"].reshape(-1, 1),
        "w2_light": asc(z["L1_ffn_w2_light"]), "b2_light_row": z["L1_ffn_b2_light"].reshape(1, -1),
        "gamma_light_col": z["L1_ffn_gamma_light"].reshape(-1, 1),
        "w1_heavy": asc(z["L1_ffn_w1_heavy"]), "b1_heavy": z["L1_ffn_b1_heavy"].reshape(-1, 1),
        "w2_heavy": asc(z["L1_ffn_w2_heavy"]), "b2_heavy_row": z["L1_ffn_b2_heavy"].reshape(1, -1),
        "gamma_heavy_col": z["L1_ffn_gamma_heavy"].reshape(-1, 1),
    }
    w.update(host_consts())
    w.update(host_onehot())
    return w



_CACHE = {}

def kernel(**inputs):
    import numpy as np
    from concourse.bass_utils import run_bass_kernel_spmd
    enable_ldw_opt()
    if "prog" not in _CACHE:
        P = build_program()
        nc = P.build()
        _CACHE["prog"] = (P, nc)
    P, nc = _CACHE["prog"]
    w = host_weights(inputs)
    w.update(P.host_const_pool())
    x = np.ascontiguousarray(inputs["L0_attn_Z"].astype(np.float32, copy=False))
    in_maps = []
    for cid in range(8):
        m = dict(w)
        m["x"] = np.ascontiguousarray(x[2*cid:2*cid+2].reshape(TLOC, 1024))
        in_maps.append(m)
    res = run_bass_kernel_spmd(nc, in_maps, core_ids=list(range(8)))
    out = np.empty((16, 2048, 1024), dtype=np.float32)
    for cid in range(8):
        out[2*cid:2*cid+2] = res.results[cid]["out"].reshape(2, 2048, 1024)
    return out


# version 09 (final)
# speedup vs baseline: 1.0378x; 1.0099x over previous
import numpy as np
import contextlib
import concourse.bass as bass
import concourse.mybir as mybir

F32 = mybir.dt.float32
BF16 = mybir.dt.bfloat16
AF = mybir.ActivationFunctionType
ALU = mybir.AluOpType
AX = mybir.AxisListType

ENGINES = ("sync", "tensor", "vector", "scalar", "gpsimd")

class Buf:
    __slots__ = ("h", "name", "last_writer", "readers", "state")
    def __init__(self, h, name, state=None):
        self.h = h; self.name = name
        self.last_writer = None
        self.readers = []
        self.state = state if state is not None else self
    def ap(self):
        a = getattr(self.h, "ap", None)
        return self.h.ap() if callable(a) else self.h
    def __getitem__(self, idx):
        return self.ap()[idx]

class Prog:
    def __init__(self, name="prog"):
        self.nc = bass.Bass("TRN2", target_bir_lowering=False)
        self.stack = contextlib.ExitStack()
        self.sems = {}
        self.NDMAGRP = 6
        self._dma_rr = {e: 0 for e in ENGINES}
        for e in ENGINES:
            self.sems[e] = self.stack.enter_context(self.nc.semaphore(f"s_{e}"))
            for g in range(self.NDMAGRP):
                self.sems[f"{e}_dma{g}"] = self.stack.enter_context(self.nc.semaphore(f"s_{e}d{g}"))
        self.count = {k: 0 for k in self.sems}
        self.tasks = {e: [] for e in ENGINES}
        self.final_deps = []
        self.nbuf = 0
        self.current_barrier = ()
        self.NCONST = 64
        self.const_vals = {}
        self.const_pool = Buf(self.nc.alloc_sbuf_tensor("c_pool_sb", [128, self.NCONST], F32), "c_pool_sb")
        self.const_dram = Buf(self.nc.dram_tensor("c_pool", [128, self.NCONST], F32, kind="ExternalInput"), "c_pool")
        self.dma("sync", self.const_pool.ap(), self.const_dram.ap(),
                 reads=[self.const_dram], writes=[self.const_pool])
        self.ARENA_BYTES = 195 * 1024
        self._arena_h = self.nc.alloc_sbuf_tensor("arena", [128, self.ARENA_BYTES], mybir.dt.uint8)
        self._arena_views = {}
        self._arena_off = 0
        self.current_barrier = ()
        self.rowtmp = Buf(self.nc.alloc_sbuf_tensor("rowtmp", [1, 2048], F32), "rowtmp")
        self.psum_f = [Buf(self.nc.alloc_psum_tensor(f"ppf{i}", [128, 1024], F32), f"ppf{i}") for i in range(4)]
        self._psum_bf_view = self.psum_f[3].h.bitcast(BF16).ap()

    def persist(self, name, shape, dtype):
        return Buf(self.nc.alloc_sbuf_tensor(name, list(shape), dtype), name)
    def sbuf(self, name, shape, dtype):
        assert len(shape) == 2 and shape[0] <= 128, shape
        esz = {F32: 4, BF16: 2, mybir.dt.int32: 4}[dtype]
        off = (self._arena_off + 31) & ~31
        nbytes = shape[1] * esz
        assert off + nbytes <= self.ARENA_BYTES, f"arena overflow at {name}: {off+nbytes}"
        self._arena_off = off + nbytes
        if dtype not in self._arena_views:
            self._arena_views[dtype] = self._arena_h.bitcast(dtype).ap()
        v = self._arena_views[dtype]
        ap = v[0:shape[0], off // esz : off // esz + shape[1]]
        return Buf(ap, name)
    def arena_save(self):
        return self._arena_off
    def arena_restore(self, mark):
        self._arena_off = mark
        self.stage_barrier()
    def stage_barrier(self):
        toks = []
        for e in ENGINES:
            if self.count[e] > 0:
                toks.append((e, self.count[e]))
            for g in range(self.NDMAGRP):
                sname = f"{e}_dma{g}"
                if self.count[sname] > 0:
                    toks.append((sname, self.count[sname]))
        self.current_barrier = tuple(toks)
    def finish_all(self):
        toks = []
        for e in ENGINES:
            if self.count[e] > 0:
                toks.append((e, self.count[e]))
            for g in range(self.NDMAGRP):
                sname = f"{e}_dma{g}"
                if self.count[sname] > 0:
                    toks.append((sname, self.count[sname]))
        self.finish(toks)
    def psum(self, name, shape, dtype=F32):
        return Buf(self.nc.alloc_psum_tensor(name, list(shape), dtype), name)
    def pf(self, idx, c0=0, c1=1024, p0=0, p1=128, name=None):
        return Buf(self.psum_f[idx].h.ap()[p0:p1, c0:c1], name or f"pf{idx}_{c0}_{p0}",
                   state=self.psum_f[idx])
    def pb(self, idx, name=None):
        return Buf(self._psum_bf_view[:, idx*128:(idx+1)*128], name or f"pb{idx}", state=self.psum_f[3])
    def dram_in(self, name, shape, dtype):
        return Buf(self.nc.dram_tensor(name, list(shape), dtype, kind="ExternalInput"), name)
    def dram_out(self, name, shape, dtype):
        return Buf(self.nc.dram_tensor(name, list(shape), dtype, kind="ExternalOutput"), name)
    def dram_tmp(self, name, shape, dtype):
        return Buf(self.nc.dram_tensor(name, list(shape), dtype, kind="Internal"), name)

    def _deps_for(self, engine, reads, writes, extra):
        deps = list(self.current_barrier)
        for b in reads:
            b = b.state
            if b.last_writer is not None:
                deps.append(b.last_writer)
        for b in writes:
            b = b.state
            deps.extend(b.readers)
            if b.last_writer is not None:
                deps.append(b.last_writer)
        deps.extend(t for t in extra if t is not None)
        best = {}
        for (s, v) in deps:
            if "_dma" in s:
                v = self.count[s]
            if best.get(s, -1) < v:
                best[s] = v
        return tuple(best.items())

    def _update(self, tok, reads, writes):
        for b in reads:
            b.state.readers.append(tok)
        for b in writes:
            b.state.last_writer = tok
            b.state.readers = []

    def constf(self, v, p=128):
        import numpy as _np
        v = float(_np.float32(v))
        if v not in self.const_vals:
            assert len(self.const_vals) < self.NCONST, "const pool full"
            self.const_vals[v] = len(self.const_vals)
        i = self.const_vals[v]
        return self.const_pool.ap()[0:p, i:i+1]

    def host_const_pool(self):
        import numpy as _np
        arr = _np.zeros((128, self.NCONST), dtype=_np.float32)
        for v, i in self.const_vals.items():
            arr[:, i] = v
        return {"c_pool": arr}

    def task(self, engine, body, reads=(), writes=(), deps=(), strict=False):
        reads = list(reads) + [self.const_pool]
        dd = self._deps_for(engine, reads, writes, deps)
        self.tasks[engine].append((dd, "c", body, strict) if strict else (dd, "c", body))
        self.count[engine] += 1
        tok = (engine, self.count[engine])
        self._update(tok, reads, writes)
        return tok

    def dma(self, engine, out_ap, in_ap, reads=(), writes=(), deps=(), slot=None, **kw):
        dd = self._deps_for(engine, reads, writes, deps)
        if slot is None:
            slot = self._dma_rr[engine]
            self._dma_rr[engine] = (slot + 1) % self.NDMAGRP
        sname = f"{engine}_dma{slot}"
        if self.count[sname] > 0:
            dd = tuple(list(dd) + [(sname, self.count[sname])])
        self.tasks[engine].append((dd, "d", (out_ap, in_ap, sname, kw)))
        self.count[sname] += 16
        tok = (sname, self.count[sname])
        self._update(tok, reads, writes)
        return tok

    def finish(self, deps):
        self.final_deps.extend(d for d in deps if d is not None)

    def build(self):
        waited = {e: {} for e in ENGINES}
        def emit_stream(eng_obj, ename):
            for entry in self.tasks[ename]:
                if len(entry) == 4:
                    deps, kind, payload, strict = entry
                else:
                    (deps, kind, payload), strict = entry, False
                for (sname, val) in deps:
                    if sname == ename and ename == "tensor" and kind == "c":
                        continue
                    if waited[ename].get(sname, 0) < val:
                        eng_obj.wait_ge(self.sems[sname], val)
                        waited[ename][sname] = val
                if kind == "d":
                    out_ap, in_ap, sname_d, kw = payload
                    ins = eng_obj.dma_start(out=out_ap, in_=in_ap, **kw)
                    ins.then_inc(self.sems[sname_d], 16)
                else:
                    last = payload(eng_obj)
                    if isinstance(last, (list, tuple)):
                        last = last[-1]
                    assert last is not None, f"task body on {ename} returned None"
                    last.then_inc(self.sems[ename], 1)
            if ename == "sync":
                best = {}
                for (s, v) in self.final_deps:
                    if best.get(s, -1) < v:
                        best[s] = v
                for s, v in best.items():
                    if s == ename:
                        continue
                    if waited[ename].get(s, 0) < v:
                        eng_obj.wait_ge(self.sems[s], v)
                        waited[ename][s] = v
        with self.nc.Block() as block:
            @block.sync
            def _(e): emit_stream(e, "sync")
            if self.tasks["tensor"]:
                @block.tensor
                def _(e): emit_stream(e, "tensor")
            if self.tasks["vector"]:
                @block.vector
                def _(e): emit_stream(e, "vector")
            if self.tasks["scalar"]:
                @block.scalar
                def _(e): emit_stream(e, "scalar")
            if self.tasks["gpsimd"]:
                @block.gpsimd
                def _(e): emit_stream(e, "gpsimd")
        return self.nc


def enable_ldw_opt():
    import concourse.bass_utils as _bu
    if getattr(_bu, "_ldw_patched", False):
        return
    _orig = _bu.run_command
    def _patched(cmd, *a, **kw):
        if isinstance(cmd, list):
            cmd = ["--enable-ldw-opt=true" if c == "--enable-ldw-opt=false" else c for c in cmd]
        return _orig(cmd, *a, **kw)
    _bu.run_command = _patched
    _bu._ldw_patched = True


TLOC, SROW, D, DI = 4096, 2048, 1024, 2048
INV01 = float(np.float32(1.0) / np.float32(0.1))

def emit_consts(P):
    c = {}
    ident_f = P.persist("ident_f", [128, 128], F32)
    c["ident_f"] = ident_f
    c["ident_bf"] = P.persist("ident_bf", [128, 128], BF16)
    c["ones_f"] = P.persist("ones_f", [128, 1], F32)
    c["ones_row"] = P.persist("ones_row", [1, 128], F32)
    c["ones_all"] = P.persist("ones_all", [64, 128], F32)
    id_dram = P.dram_in("c_ident", [128, 128], F32)
    ones_dram = P.dram_in("c_ones", [128, 1], F32)
    onesr_dram = P.dram_in("c_ones_row", [1, 128], F32)
    onesa_dram = P.dram_in("c_ones_all", [64, 128], F32)
    P.dma("sync", ident_f.ap(), id_dram.ap(), reads=[id_dram], writes=[ident_f])
    P.dma("sync", c["ones_f"].ap(), ones_dram.ap(), reads=[ones_dram], writes=[c["ones_f"]])
    P.dma("sync", c["ones_row"].ap(), onesr_dram.ap(), reads=[onesr_dram], writes=[c["ones_row"]])
    P.dma("sync", c["ones_all"].ap(), onesa_dram.ap(), reads=[onesa_dram], writes=[c["ones_all"]])
    P.task("scalar", lambda e: e.copy(c["ident_bf"].ap(), ident_f.ap()),
           reads=[ident_f], writes=[c["ident_bf"]])
    return c

def host_consts():
    return {"c_ident": np.eye(128, dtype=np.float32),
            "c_ones": np.ones((128, 1), dtype=np.float32),
            "c_ones_row": np.ones((1, 128), dtype=np.float32),
            "c_ones_all": np.ones((64, 128), dtype=np.float32)}

def stage_ln(P, c, x_tm, xn_tm, xn_fm_bf, name="ln0", eps=1e-5):
    nt = TLOC // 128
    sx = [P.sbuf(f"{name}_x{i}", [128, D], F32) for i in range(2)]
    sq = P.sbuf(f"{name}_sq", [128, D], F32)
    cols = [P.sbuf(f"{name}_cols{i}", [128, 8], F32) for i in range(2)]
    xnbf = [P.sbuf(f"{name}_xnbf{i}", [128, D], BF16) for i in range(2)]
    tp = [Buf(P.psum_f[2].h.bitcast(BF16).ap()[:, 0:1024], f"{name}_tp0", state=P.psum_f[2]),
          Buf(P.psum_f[3].h.bitcast(BF16).ap()[:, 0:1024], f"{name}_tp1", state=P.psum_f[3])]
    tpo = [P.sbuf(f"{name}_tpo{i}", [128, 1024], BF16) for i in range(2)]
    for i in range(nt):
        b = i % 2
        X, CL = sx[b], cols[b]
        P.dma("sync", X.ap(), x_tm[i*128:(i+1)*128, :], reads=[x_tm], writes=[X])
        P.task("vector", lambda e, X=X, CL=CL: e.tensor_reduce(CL[:, 0:1], X.ap(), AX.X, ALU.add),
               reads=[X], writes=[CL])
        P.task("vector", lambda e, CL=CL: e.tensor_scalar(CL[:, 1:2], CL[:, 0:1], -1.0/D, None, ALU.mult),
               reads=[CL], writes=[CL], strict=True)
        P.task("vector", lambda e, X=X, CL=CL: e.tensor_scalar(X.ap(), X.ap(), CL[:, 1:2], None, ALU.add),
               reads=[X, CL], writes=[X], strict=True)
        P.task("scalar", lambda e, X=X, CL=CL: e.activation(sq.ap(), X.ap(), AF.Square, accum_out=CL[:, 2:3]),
               reads=[X], writes=[sq, CL])
        P.task("scalar", lambda e, CL=CL: e.activation(CL[:, 3:4], CL[:, 2:3], AF.Ln, bias=P.constf(eps), scale=P.constf(1.0/D)),
               reads=[CL], writes=[CL], strict=True)
        P.task("scalar", lambda e, CL=CL: e.activation(CL[:, 4:5], CL[:, 3:4], AF.Exp, scale=P.constf(-0.5)),
               reads=[CL], writes=[CL], strict=True)
        P.task("vector", lambda e, X=X, CL=CL: e.tensor_scalar(X.ap(), X.ap(), CL[:, 4:5], None, ALU.mult),
               reads=[X, CL], writes=[X])
        P.dma("gpsimd", xn_tm[i*128:(i+1)*128, :], X.ap(), reads=[X], writes=[xn_tm])
        if xn_fm_bf is None:
            continue
        XB = xnbf[b]
        P.task("scalar", lambda e, X=X, XB=XB: e.copy(XB.ap(), X.ap()), reads=[X], writes=[XB])
        TPB, TOB = tp[b], tpo[b]
        def trp(e, XB=XB, TPB=TPB):
            ins = None
            for j in range(8):
                ins = e.transpose(TPB[:, j*128:(j+1)*128], XB[:, j*128:(j+1)*128], c["ident_bf"].ap())
            return ins
        P.task("tensor", trp, reads=[XB, c["ident_bf"]], writes=[TPB])
        P.task("scalar", lambda e, TPB=TPB, TOB=TOB: e.copy(TOB.ap(), TPB.ap()), reads=[TPB], writes=[TOB])
        for j in range(8):
            P.dma("gpsimd", xn_fm_bf[j*128:(j+1)*128, i*128:(i+1)*128], TOB[:, j*128:(j+1)*128],
                  reads=[TOB], writes=[xn_fm_bf])

def load_cast_bf(P, name, w_dram, R, Ccols, stage_bufs=None):
    wbf = P.sbuf(f"{name}_bf", [128, (R // 128) * Ccols], BF16)
    if stage_bufs is None:
        stage_bufs = [P.sbuf(f"{name}_st0", [128, Ccols], F32)]
    for i in range(R // 128):
        S = stage_bufs[i % len(stage_bufs)]
        P.dma("sync", S[0:128, 0:Ccols], w_dram[i*128:(i+1)*128, :], reads=[w_dram], writes=[S])
        P.task("scalar", lambda e, S=S, i=i: e.copy(wbf[:, i*Ccols:(i+1)*Ccols], S[0:128, 0:Ccols]),
               reads=[S], writes=[wbf])
    return wbf

def wtile(wbf, i, Ccols, j0, j1):
    return wbf[:, i*Ccols + j0 : i*Ccols + j1]

def stage_mamba_front(P, c, W, xn_fm_bf, xc_fm, zs_fm, dt_fm, xdbl_fm, bc_fm, name="mf"):
    TC = 512
    nchunk = TLOC // TC
    w_in_bf = load_cast_bf(P, f"{name}_win", W["W_inT"], 1024, 4096)
    xproj_bf = load_cast_bf(P, f"{name}_xp", W["x_projT"], 2048, 96)
    dtproj_bf = P.sbuf(f"{name}_dtp_bf", [64, 2048], BF16)
    dtp_st = P.sbuf(f"{name}_dtp_st", [64, 2048], F32)
    P.dma("sync", dtp_st.ap(), W["dt_projT"].ap(), reads=[W["dt_projT"]], writes=[dtp_st])
    P.task("scalar", lambda e: e.copy(dtproj_bf.ap(), dtp_st.ap()), reads=[dtp_st], writes=[dtproj_bf])
    convw = P.sbuf(f"{name}_convw", [128, 16*4], F32)
    convb = P.sbuf(f"{name}_convb", [128, 16], F32)
    dtb = P.sbuf(f"{name}_dtb", [128, 16], F32)
    for i in range(16):
        P.dma("sync", convw[:, i*4:(i+1)*4], W["conv_w"][i*128:(i+1)*128, :], reads=[W["conv_w"]], writes=[convw])
        P.dma("sync", convb[:, i:i+1], W["conv_b"][i*128:(i+1)*128, :], reads=[W["conv_b"]], writes=[convb])
        P.dma("sync", dtb[:, i:i+1], W["dt_b"][i*128:(i+1)*128, :], reads=[W["dt_b"]], writes=[dtb])
    xn_a = P.sbuf(f"{name}_xna", [128, 8 * TC], BF16)
    xn_t = [Buf(xn_a.ap()[:, i*TC:(i+1)*TC], f"{name}_xn{i}", state=xn_a) for i in range(8)]
    pmm = [P.pf(0, 0, TC), P.pf(1, 0, TC)]
    xzx2 = [P.sbuf(f"{name}_xzx{b}", [128, TC+3], F32) for b in range(2)]
    carry = [P.sbuf(f"{name}_carry{i}", [128, 3*16], F32) for i in range(2)]
    conv_u2 = [P.sbuf(f"{name}_convu{b}", [128, TC], F32) for b in range(2)]
    expb2 = [P.sbuf(f"{name}_expb{b}", [128, TC], F32) for b in range(2)]
    zsf2 = [P.sbuf(f"{name}_zsf{b}", [128, TC], F32) for b in range(2)]
    xc_f = [P.sbuf(f"{name}_xcf{i}", [128, TC], F32) for i in range(2)]
    xc_bf = [P.sbuf(f"{name}_xcbf{i}", [128, TC], BF16) for i in range(16)]
    xdbl = P.sbuf(f"{name}_xdbl", [128, TC], F32)
    xdbl_bf = P.sbuf(f"{name}_xdblbf", [64, TC], BF16)
    pdbl = P.pf(2, 0, TC)
    dtt = P.sbuf(f"{name}_dtt", [128, TC], F32)
    dtt2 = P.sbuf(f"{name}_dtt2", [128, TC], F32)

    for ch in range(nchunk):
        row = (ch * TC) // SROW
        first_in_row = (ch * TC) % SROW == 0
        t0 = ch * TC
        CARRY = carry[0]
        P.dma("sync", xn_a.ap().rearrange("p (k c) -> p k c", k=8),
              xn_fm_bf[0:1024, t0:t0+TC].rearrange("(k p) c -> p k c", p=128),
              reads=[xn_fm_bf], writes=[xn_a])
        XN = xn_t
        for et in range(16):
            PM = pmm[et % 2]
            eb = et % 2
            xzx, conv_u, expb = xzx2[eb], conv_u2[eb], expb2[eb]
            if et % 2 == 0:
                def mm(e, et=et, XN=XN):
                    ins = None
                    for ei in range(2):
                        PMx = pmm[ei]
                        for dk in range(8):
                            ins = e.matmul(PMx.ap(), wtile(w_in_bf, dk, 4096, (et+ei)*128, (et+ei+1)*128),
                                           XN[dk].ap(), start=(dk == 0), stop=(dk == 7))
                    return ins
                P.task("tensor", mm, reads=[w_in_bf] + XN, writes=[pmm[0], pmm[1]])
                for ei in range(2):
                    P.task("scalar", lambda e, PM=pmm[ei], xzx=xzx2[ei]: e.copy(xzx[:, 3:TC+3], PM.ap()),
                           reads=[pmm[ei]], writes=[xzx2[ei]])
            def convt(e, et=et, xzx=xzx, conv_u=conv_u, fir=first_in_row, CARRY=CARRY):
                if fir:
                    e.memset(xzx[:, 0:3], 0.0)
                else:
                    e.tensor_copy(xzx[:, 0:3], CARRY[:, et*3:et*3+3])
                e.tensor_copy(CARRY[:, et*3:et*3+3], xzx[:, TC:TC+3])
                e.tensor_scalar(conv_u.ap(), xzx[:, 0:TC], convw[:, et*4:et*4+1], convb[:, et:et+1], ALU.mult, ALU.add)
                ins = None
                for k2 in (1, 2, 3):
                    ins = e.scalar_tensor_tensor(conv_u.ap(), xzx[:, k2:TC+k2], convw[:, et*4+k2:et*4+k2+1],
                                                 conv_u.ap(), ALU.mult, ALU.add)
                return ins
            P.task("vector", convt, reads=[xzx, convw, convb, CARRY], writes=[conv_u, xzx, CARRY])
            XCF, XCB = xc_f[et % 2], xc_bf[et]
            P.task("scalar", lambda e, conv_u=conv_u, XCF=XCF: e.activation(XCF.ap(), conv_u.ap(), AF.Silu),
                   reads=[conv_u], writes=[XCF])
            P.task("scalar", lambda e, XCF=XCF, XCB=XCB: e.copy(XCB.ap(), XCF.ap()), reads=[XCF], writes=[XCB])
            P.dma("gpsimd", xc_fm[et*128:(et+1)*128, t0:t0+TC], XCF.ap(), reads=[XCF], writes=[xc_fm])
        for et in range(16):
            PM = pmm[et % 2]
            eb = et % 2
            expb, zsf = expb2[eb], zsf2[eb]
            if et % 2 == 0:
                def mmz(e, et=et, XN=XN):
                    ins = None
                    for ei in range(2):
                        PMx = pmm[ei]
                        for dk in range(8):
                            ins = e.matmul(PMx.ap(), wtile(w_in_bf, dk, 4096, 2048+(et+ei)*128, 2048+(et+ei+1)*128),
                                           XN[dk].ap(), start=(dk == 0), stop=(dk == 7))
                    return ins
                P.task("tensor", mmz, reads=[w_in_bf] + XN, writes=[pmm[0], pmm[1]])
            P.task("scalar", lambda e, PM=PM, zsf=zsf: e.activation(zsf.ap(), PM.ap(), AF.Silu),
                   reads=[PM], writes=[zsf])
            P.dma("gpsimd", zs_fm[et*128:(et+1)*128, t0:t0+TC], zsf.ap(), reads=[zsf], writes=[zs_fm])
        def mdbl(e):
            ins = None
            for et in range(16):
                ins = e.matmul(pdbl.ap()[0:96, :], wtile(xproj_bf, et, 96, 0, 96),
                               xc_bf[et].ap(), start=(et == 0), stop=(et == 15))
            return ins
        P.task("tensor", mdbl, reads=[xproj_bf] + xc_bf, writes=[pdbl])
        P.task("scalar", lambda e: e.copy(xdbl[0:96, :], pdbl.ap()[0:96, :]), reads=[pdbl], writes=[xdbl])
        P.dma("gpsimd", xdbl_fm[:, t0:t0+TC], xdbl[0:96, :], reads=[xdbl], writes=[xdbl_fm])
        P.dma("gpsimd", bc_fm[:, t0:t0+TC], xdbl[64:96, :], reads=[xdbl], writes=[bc_fm])
        P.task("scalar", lambda e: e.copy(xdbl_bf.ap(), xdbl[0:64, :]), reads=[xdbl], writes=[xdbl_bf])
        for dk in range(16):
            PM = pmm[dk % 2]
            P.task("tensor", lambda e, dk=dk, PM=PM: e.matmul(
                PM.ap(), dtproj_bf[:, dk*128:(dk+1)*128], xdbl_bf.ap(), start=True, stop=True),
                reads=[dtproj_bf, xdbl_bf], writes=[PM])
            def sp1(e, PM=PM, dk=dk):
                e.activation(dtt.ap(), PM.ap(), AF.Relu, bias=dtb[:, dk:dk+1])
                e.activation(dtt2.ap(), PM.ap(), AF.Abs, bias=dtb[:, dk:dk+1])
                e.activation(dtt2.ap(), dtt2.ap(), AF.Exp, scale=P.constf(-1.0))
                return e.activation(dtt2.ap(), dtt2.ap(), AF.Ln, bias=P.constf(1.0))
            P.task("scalar", sp1, reads=[PM, dtb], writes=[dtt, dtt2])
            P.task("vector", lambda e: e.tensor_tensor(dtt.ap(), dtt.ap(), dtt2.ap(), ALU.add),
                   reads=[dtt, dtt2], writes=[dtt])
            P.dma("gpsimd", dt_fm[dk*128:(dk+1)*128, t0:t0+TC], dtt.ap(), reads=[dtt], writes=[dt_fm])


def stage_scan_gen(P, c, W, dt_fm, xc_fm, zs_fm, bc_fm, y_fm_bf, name="sc"):
    TC = 512
    NB = 4
    oh_dram = P.dram_in(f"{name}_onehot", [32, 32*128], F32)
    oh = P.sbuf(f"{name}_oh", [32, 32*128], F32)
    P.dma("sync", oh.ap(), oh_dram.ap(), reads=[oh_dram], writes=[oh])
    A_all = P.sbuf(f"{name}_A", [128, 256], F32)
    Dcol = P.sbuf(f"{name}_D", [128, 16], F32)
    ast = P.sbuf(f"{name}_ast", [128, 16], F32)
    for k in range(16):
        P.dma("sync", ast.ap(), W["A_log"][k*128:(k+1)*128, :], reads=[W["A_log"]], writes=[ast])
        P.task("scalar", lambda e, k=k: e.activation(A_all[:, k*16:(k+1)*16], ast.ap(), AF.Exp),
               reads=[ast], writes=[A_all])
        P.dma("sync", Dcol[:, k:k+1], W["Dd"][k*128:(k+1)*128, :], reads=[W["Dd"]], writes=[Dcol])
    P.task("vector", lambda e: e.tensor_scalar(A_all.ap(), A_all.ap(), -1.0, None, ALU.mult),
           reads=[A_all], writes=[A_all])
    states = P.sbuf(f"{name}_st", [128, 256], F32)
    state_g = [Buf(states.ap()[:, g*4:(g+1)*4], f"stg{g}") for g in range(64)]
    bc_sb = P.sbuf(f"{name}_bc", [32, TC], F32)
    pB = P.pf(0, 0, TC)
    pC = P.pf(1, 0, TC)
    Bb = P.sbuf(f"{name}_Bb", [128, 16 * TC], F32)
    Cb = P.sbuf(f"{name}_Cb", [128, 16 * TC], F32)
    Bb_q = [Buf(Bb.ap()[:, q*NB*TC:(q+1)*NB*TC], f"{name}_Bbq{q}") for q in range(4)]
    Cb_q = [Buf(Cb.ap()[:, q*NB*TC:(q+1)*NB*TC], f"{name}_Cbq{q}") for q in range(4)]
    dtb_ = [P.sbuf(f"{name}_dt{i}", [128, TC], F32) for i in range(2)]
    xcb_ = [P.sbuf(f"{name}_xc{i}", [128, TC], F32) for i in range(2)]
    zsb_ = [P.sbuf(f"{name}_zs{i}", [128, TC], F32) for i in range(2)]
    wbuf = [P.sbuf(f"{name}_w{i}", [128, TC], F32) for i in range(2)]
    dA4 = [P.sbuf(f"{name}_dA{i}", [128, NB * TC], BF16) for i in range(2)]
    dB4 = [P.sbuf(f"{name}_dB{i}", [128, NB * TC], BF16) for i in range(2)]
    hh4 = [P.sbuf(f"{name}_hh{i}", [128, NB * TC], F32) for i in range(2)]
    tr4 = [P.sbuf(f"{name}_tr{i}", [128, NB * TC], F32) for i in range(2)]
    yacc_ = [P.sbuf(f"{name}_y{i}", [128, TC], F32) for i in range(2)]
    ybf_ = [P.sbuf(f"{name}_ybf{i}", [128, TC], BF16) for i in range(2)]
    nchunk = TLOC // TC
    seg = lambda ap_, w_=TC: ap_.rearrange("p (s c) -> p s c", s=NB)
    for ch in range(nchunk):
        t0 = ch * TC
        first_in_row = t0 % SROW == 0
        if first_in_row:
            P.task("vector", lambda e: e.memset(states.ap(), 0.0), writes=state_g)
        P.dma("sync", bc_sb.ap(), bc_fm[:, t0:t0+TC], reads=[bc_fm], writes=[bc_sb])
        for n in range(16):
            P.task("tensor", lambda e, n=n: e.matmul(pB.ap(), oh[:, n*128:(n+1)*128], bc_sb.ap(),
                                                     start=True, stop=True),
                   reads=[oh, bc_sb], writes=[pB])
            P.task("scalar", lambda e, n=n: e.copy(Bb[:, n*TC:(n+1)*TC], pB.ap()), reads=[pB], writes=[Bb_q[n//4]])
            P.task("tensor", lambda e, n=n: e.matmul(pC.ap(), oh[:, (16+n)*128:(17+n)*128], bc_sb.ap(),
                                                     start=True, stop=True),
                   reads=[oh, bc_sb], writes=[pC])
            P.task("scalar", lambda e, n=n: e.copy(Cb[:, n*TC:(n+1)*TC], pC.ap()), reads=[pC], writes=[Cb_q[n//4]])
        pend_gate = [None]
        pend_out = [None]
        for k in range(16):
            b = k % 2
            DT, XC, WW, ZS = dtb_[b], xcb_[b], zsb_[b], wbuf[b]
            P.dma("sync", DT.ap(), dt_fm[k*128:(k+1)*128, t0:t0+TC], reads=[dt_fm], writes=[DT])
            P.dma("sync", XC.ap(), xc_fm[k*128:(k+1)*128, t0:t0+TC], reads=[xc_fm], writes=[XC])
            P.dma("sync", ZS.ap(), zs_fm[k*128:(k+1)*128, t0:t0+TC], reads=[zs_fm], writes=[ZS])
            P.task("vector", lambda e, DT=DT, XC=XC, WW=WW: e.tensor_tensor(WW.ap(), DT.ap(), XC.ap(), ALU.mult),
                   reads=[DT, XC], writes=[WW])
            yacc, ybf = yacc_[k % 2], ybf_[k % 2]
            def emit_trt(nb, k=k):
                bb_ = nb % 2
                HH, TR = hh4[bb_], tr4[bb_]
                def trt(e, HH=HH, nb=nb, TR=TR):
                    return e.tensor_tensor(TR.ap(), HH.ap(),
                                           Cb.ap()[:, nb*NB*TC:(nb+1)*NB*TC], ALU.mult)
                P.task("gpsimd", trt, reads=[HH, Cb_q[nb]], writes=[TR])
            def emit_yadd(nb, yacc=yacc):
                TR0 = tr4[nb % 2]
                def yadd(e, TR0=TR0, first=(nb == 0), yacc=yacc):
                    e.tensor_tensor(TR0.ap()[:, 0:2*TC], TR0.ap()[:, 0:2*TC], TR0.ap()[:, 2*TC:4*TC], ALU.add)
                    e.tensor_tensor(TR0.ap()[:, 0:TC], TR0.ap()[:, 0:TC], TR0.ap()[:, TC:2*TC], ALU.add)
                    if first:
                        return e.tensor_copy(yacc.ap(), TR0.ap()[:, 0:TC])
                    return e.tensor_tensor(yacc.ap(), yacc.ap(), TR0.ap()[:, 0:TC], ALU.add)
                P.task("gpsimd", yadd, reads=[TR0, yacc], writes=[yacc, TR0])
            for nb in range(4):
                bb_ = nb % 2
                DA, DB, HH = dA4[bb_], dB4[bb_], hh4[bb_]
                SG = state_g[k*4 + nb]
                def mk_da(e, DT=DT, k=k, nb=nb, DA=DA):
                    ins = None
                    for j in range(NB):
                        n = nb*NB + j
                        ins = e.activation(DA.ap()[:, j*TC:(j+1)*TC], DT.ap(), AF.Exp,
                                           scale=A_all[:, k*16+n:k*16+n+1])
                    return ins
                P.task("scalar", mk_da, reads=[DT, A_all], writes=[DA])
                def gpd(e, WW=WW, nb=nb, DA=DA, DB=DB, SG=SG):
                    e.memset(seg(DA.ap())[:, :, 0:1], 0.0)
                    e.tensor_tensor(
                        DB.ap().rearrange("p (s c) -> p s c", s=NB),
                        WW.ap().rearrange("p (o c) -> p o c", o=1).broadcast_to([128, NB, TC]),
                        Bb.ap()[:, nb*NB*TC:(nb+1)*NB*TC].rearrange("p (s c) -> p s c", s=NB), ALU.mult)
                    return e.tensor_copy(seg(DB.ap())[:, :, 0:1], SG.ap().rearrange("p (s o) -> p s o", s=NB))
                P.task("vector", gpd, reads=[WW, Bb_q[nb], SG], writes=[DA, DB])
                P.task("vector", lambda e, DA=DA, DB=DB, HH=HH: e.tensor_tensor_scan(
                    HH.ap(), DA.ap(), DB.ap(), 0.0, ALU.mult, ALU.add),
                    reads=[DA, DB], writes=[HH])
                P.task("vector", lambda e, HH=HH, SG=SG: e.tensor_copy(
                    SG.ap().rearrange("p (s o) -> p s o", s=NB), seg(HH.ap())[:, :, TC-1:TC]),
                    reads=[HH], writes=[SG])
                if nb == 1 and pend_gate[0] is not None:
                    pend_gate[0](); pend_gate[0] = None
                if nb > 0:
                    emit_trt(nb - 1)
                if nb > 1:
                    emit_yadd(nb - 2)
            if pend_out[0] is not None:
                pend_out[0](); pend_out[0] = None
            emit_trt(3)
            emit_yadd(2)
            emit_yadd(3)
            def mk_gate(XC=XC, ZS=ZS, k=k, yacc=yacc):
                def gatet(e):
                    e.scalar_tensor_tensor(yacc.ap(), XC.ap(), Dcol[:, k:k+1], yacc.ap(), ALU.mult, ALU.add)
                    return e.tensor_tensor(yacc.ap(), yacc.ap(), ZS.ap(), ALU.mult)
                P.task("vector", gatet, reads=[XC, Dcol, yacc, ZS], writes=[yacc])
            def mk_out(k=k, ybf=ybf, yacc=yacc, t0=t0):
                P.task("scalar", lambda e: e.copy(ybf.ap(), yacc.ap()), reads=[yacc], writes=[ybf])
                P.dma("gpsimd", y_fm_bf[k*128:(k+1)*128, t0:t0+TC], ybf.ap(), reads=[ybf], writes=[y_fm_bf])
            pend_gate[0] = mk_gate
            pend_out[0] = mk_out
        pend_gate[0](); pend_gate[0] = None
        pend_out[0](); pend_out[0] = None
        yield ch

def host_onehot():
    oh = np.zeros((32, 32*128), dtype=np.float32)
    for n in range(32):
        oh[n, n*128:(n+1)*128] = 1.0
    return {"sc_onehot": oh}

def stage_precast(P, jobs, name="pc"):
    stg = [P.persist(f"{name}_s{i}", [128, 1024], F32) for i in range(1)]
    obf = [P.persist(f"{name}_o{i}", [128, 1024], BF16) for i in range(2)]
    gc = P.persist(f"{name}_g", [128, 16], F32)
    for (src, dst, gcol) in jobs:
        R, C = src.h.shape if hasattr(src.h, "shape") else (None, None)
        R, C = dst.h.shape
        ng = 0
        if gcol is not None:
            nd_ = R // 128
            for k in range(nd_):
                P.dma("scalar", gc[:, k:k+1], gcol[k*128:(k+1)*128, :], reads=[gcol], writes=[gc])
        i = 0
        for r0 in range(0, R, 128):
            for c0 in range(0, C, 1024):
                cw = min(1024, C - c0)
                b = i % 2; i += 1
                S_, O_ = stg[0], obf[b]
                P.dma("scalar", S_[0:128, 0:cw], src[r0:r0+128, c0:c0+cw], reads=[src], writes=[S_])
                if gcol is not None:
                    k = r0 // 128
                    P.task("scalar", lambda e, S_=S_, O_=O_, cw=cw, k=k: e.activation(
                        O_[0:128, 0:cw], S_[0:128, 0:cw], AF.Copy, scale=gc[:, k:k+1]),
                        reads=[S_, gc], writes=[O_])
                else:
                    P.task("scalar", lambda e, S_=S_, O_=O_, cw=cw: e.copy(O_[0:128, 0:cw], S_[0:128, 0:cw]),
                           reads=[S_], writes=[O_])
                P.dma("gpsimd", dst[r0:r0+128, c0:c0+cw], O_[0:128, 0:cw], reads=[O_], writes=[dst])

def bcast_row(P, c, name, row_dram_or_buf, N, f32=True):
    row = Buf(P.rowtmp.ap()[0:1, 0:N], f"{name}_row", state=P.rowtmp)
    src = row_dram_or_buf.ap() if isinstance(row_dram_or_buf, Buf) else row_dram_or_buf
    P.dma("sync", row[0:1, 0:N], src,
          reads=[row_dram_or_buf] if isinstance(row_dram_or_buf, Buf) else [], writes=[row])
    ps = P.pf(2, 0, min(N, 512), name=f"{name}_ps")
    out = P.sbuf(f"{name}_bc", [128, N], F32)
    for j0 in range(0, N, 512):
        j1 = min(j0 + 512, N)
        P.task("tensor", lambda e, j0=j0, j1=j1: e.matmul(ps.ap()[:, 0:j1-j0], c["ones_row"].ap(),
                                                          P.rowtmp.ap()[0:1, j0:j1], start=True, stop=True),
               reads=[row, c["ones_row"]], writes=[ps])
        P.task("scalar", lambda e, j0=j0, j1=j1: e.copy(out[:, j0:j1], ps.ap()[:, 0:j1-j0]),
               reads=[ps], writes=[out])
    return out

def rms_cols(P, name_bufs, X_ap, reads, eps, D_):
    sq, cols = name_bufs
    P.task("scalar", lambda e: e.activation(sq.ap(), X_ap, AF.Square, accum_out=cols[:, 0:1]),
           reads=reads, writes=[sq, cols])
    P.task("scalar", lambda e: e.activation(cols[:, 1:2], cols[:, 0:1], AF.Ln, bias=P.constf(eps), scale=P.constf(1.0/D_)),
           reads=[cols], writes=[cols], strict=True)
    P.task("scalar", lambda e: e.activation(cols[:, 2:3], cols[:, 1:2], AF.Exp, scale=P.constf(-0.5)),
           reads=[cols], writes=[cols], strict=True)

def stage_outproj_rms_gen(P, c, W, y_fm_bf, xn_tm, zmam_tm, zmam_fm_bf, name="s4a"):
    opj_bf = load_cast_bf(P, f"{name}_opj", W["out_projT"], 2048, 1024)
    n1w = bcast_row(P, c, f"{name}_n1w", W["norm1_w"], 1024)
    ybig = [P.sbuf(f"{name}_yb{i}", [128, 16 * 512], BF16) for i in range(1)]
    pm = [P.pf(0), P.pf(1)]
    pre = [P.sbuf(f"{name}_pre{i}", [128, 1024], F32) for i in range(2)]
    sq = P.sbuf(f"{name}_sq", [128, 1024], F32)
    cols = P.sbuf(f"{name}_cols", [128, 4], F32)
    zbf = [P.sbuf(f"{name}_zbf{i}", [128, 1024], BF16) for i in range(2)]
    tp = [Buf(P.psum_f[2].h.bitcast(BF16).ap()[:, 0:1024], f"{name}_tp0", state=P.psum_f[2]),
          Buf(P.psum_f[3].h.bitcast(BF16).ap()[:, 0:1024], f"{name}_tp1", state=P.psum_f[3])]
    tpo = [P.sbuf(f"{name}_tpo{i}", [128, 1024], BF16) for i in range(2)]
    for i in range(TLOC // 128):
        b = i % 2
        PM, PRE, ZBF = pm[b], pre[b], zbf[b]
        ch = i // 4
        sub = i % 4
        YB = ybig[0]
        if sub == 0:
            P.dma("sync", YB.ap().rearrange("p (k c) -> p k c", k=16),
                  y_fm_bf[0:2048, ch*512:(ch+1)*512].rearrange("(k p) c -> p k c", p=128),
                  reads=[y_fm_bf], writes=[YB])
        P.dma("sync", PRE.ap(), xn_tm[i*128:(i+1)*128, :], reads=[xn_tm], writes=[PRE])
        def mm(e, PM=PM, YB=YB, sub=sub):
            ins = None
            for half in range(2):
                for k in range(16):
                    ins = e.matmul(PM.ap()[:, half*512:(half+1)*512],
                                   YB[:, k*512 + sub*128 : k*512 + (sub+1)*128],
                                   wtile(opj_bf, k, 1024, half*512, (half+1)*512),
                                   start=(k == 0), stop=(k == 15))
            return ins
        P.task("tensor", mm, reads=[YB, opj_bf], writes=[PM])
        P.task("vector", lambda e, PM=PM, PRE=PRE: e.scalar_tensor_tensor(
            PRE.ap(), PM.ap(), 1.0, PRE.ap(), ALU.mult, ALU.add), reads=[PM, PRE], writes=[PRE])
        rms_cols(P, (sq, cols), PRE.ap(), [PRE], 1e-6, 1024)
        def rmsap(e, PRE=PRE):
            e.tensor_scalar(PRE.ap(), PRE.ap(), cols[:, 2:3], None, ALU.mult)
            return e.tensor_tensor(PRE.ap(), PRE.ap(), n1w.ap(), ALU.mult)
        P.task("vector", rmsap, reads=[PRE, cols, n1w], writes=[PRE])
        P.dma("gpsimd", zmam_tm[i*128:(i+1)*128, :], PRE.ap(), reads=[PRE], writes=[zmam_tm])
        P.task("scalar", lambda e, PRE=PRE, ZBF=ZBF: e.copy(ZBF.ap(), PRE.ap()), reads=[PRE], writes=[ZBF])
        TPB, TOB = tp[b], tpo[b]
        def trp(e, ZBF=ZBF, TPB=TPB):
            ins = None
            for j in range(8):
                ins = e.transpose(TPB[:, j*128:(j+1)*128], ZBF[:, j*128:(j+1)*128], c["ident_bf"].ap())
            return ins
        P.task("tensor", trp, reads=[ZBF, c["ident_bf"]], writes=[TPB])
        P.task("scalar", lambda e, TPB=TPB, TOB=TOB: e.copy(TOB.ap(), TPB.ap()), reads=[TPB], writes=[TOB])
        for j in range(8):
            P.dma("gpsimd", zmam_fm_bf[j*128:(j+1)*128, i*128:(i+1)*128], TOB[:, j*128:(j+1)*128],
                  reads=[TOB], writes=[zmam_fm_bf])
        if i % 4 == 3:
            yield i // 4

def ffn_fm(P, c, name, x_fm_bf, w1T_dram, b1_dram, w2T_dram, b2_row_dram, DIN, DHID, DOUT,
           out_cb, gamma_col_dram=None, act="gelu", TC=None, sc_prolog=None, pre_bf16=False):
    nd, nf = DIN // 128, DHID // 128
    SC = 512
    if gamma_col_dram is not None and not pre_bf16:
        gcol = P.sbuf(f"{name}_gcol", [128, nd], F32)
        for k in range(nd):
            P.dma("sync", gcol[:, k:k+1], gamma_col_dram[k*128:(k+1)*128, :],
                  reads=[gamma_col_dram], writes=[gcol])
    w1bf = P.sbuf(f"{name}_w1bf", [128, nd * DHID], BF16)
    if pre_bf16:
        for k in range(nd):
            P.dma("sync", w1bf[:, k*DHID:(k+1)*DHID], w1T_dram[k*128:(k+1)*128, :],
                  reads=[w1T_dram], writes=[w1bf])
        w2bf = P.sbuf(f"{name}_w2bf", [128, (DHID // 128) * DOUT], BF16)
        for fidx in range(DHID // 128):
            P.dma("sync", w2bf[:, fidx*DOUT:(fidx+1)*DOUT], w2T_dram[fidx*128:(fidx+1)*128, :],
                  reads=[w2T_dram], writes=[w2bf])
    else:
        CW = min(1024, DHID)
        st2_ = [P.sbuf(f"{name}_st{i}", [128, CW], F32) for i in range(2)]
        st = st2_[0]
        ii = 0
        for k in range(nd):
            for c0 in range(0, DHID, CW):
                S_ = st2_[ii % 2]; ii += 1
                P.dma("sync", S_[0:128, 0:CW], w1T_dram[k*128:(k+1)*128, c0:c0+CW],
                      reads=[w1T_dram], writes=[S_])
                if gamma_col_dram is not None:
                    P.task("scalar", lambda e, k=k, c0=c0, S_=S_: e.activation(
                        w1bf[:, k*DHID + c0 : k*DHID + c0 + CW], S_[0:128, 0:CW],
                        AF.Copy, scale=gcol[:, k:k+1]), reads=[S_, gcol], writes=[w1bf])
                else:
                    P.task("scalar", lambda e, k=k, c0=c0, S_=S_: e.copy(
                        w1bf[:, k*DHID + c0 : k*DHID + c0 + CW], S_[0:128, 0:CW]),
                        reads=[S_], writes=[w1bf])
        stw = 2048 if DHID >= 2048 else DHID
        w2bf = load_cast_bf(P, f"{name}_w2", w2T_dram, DHID, DOUT,
                            stage_bufs=[st] if stw >= DOUT else None)
    b1c = P.sbuf(f"{name}_b1c", [128, nf], F32)
    for f in range(nf):
        P.dma("sync", b1c[:, f:f+1], b1_dram[f*128:(f+1)*128, :], reads=[b1_dram], writes=[b1c])
    b2b = bcast_row(P, c, f"{name}_b2", b2_row_dram, DOUT)
    xta = P.sbuf(f"{name}_xta", [128, nd * SC], BF16)
    xt = [Buf(xta.ap()[:, k*SC:(k+1)*SC], f"{name}_xt{k}", state=xta) for k in range(nd)]
    hid = P.sbuf(f"{name}_hid", [128, nf * SC], BF16)
    ph8 = [P.pf(2, 0, 512), P.pf(2, 512, 1024), P.pf(3, 0, 512), P.pf(3, 512, 1024),
           P.pf(0, 0, 512), P.pf(0, 512, 1024), P.pf(1, 0, 512), P.pf(1, 512, 1024)]
    po = [P.pf(0, 0, DOUT), P.pf(1, 0, DOUT)]
    for sc in range(TLOC // SC):
        t0 = sc * SC
        if sc_prolog is not None:
            sc_prolog(sc)
        P.dma("scalar", xta.ap().rearrange("p (k c) -> p k c", k=nd),
              x_fm_bf[0:nd*128, t0:t0+SC].rearrange("(k p) c -> p k c", p=128),
              reads=[x_fm_bf], writes=[xta])
        for f0 in range(0, nf, 4):
            nfg = min(4, nf - f0)
            PH4 = ph8[((f0 // 4) % 2) * 4 : ((f0 // 4) % 2) * 4 + 4]
            def mm1(e, f0=f0, nfg=nfg, PH4=PH4):
                ins = None
                for fi in range(nfg):
                    f = f0 + fi
                    for k in range(nd):
                        ins = e.matmul(PH4[fi].ap(), w1bf[:, k*DHID + f*128 : k*DHID + (f+1)*128],
                                       xt[k].ap(), start=(k == 0), stop=(k == nd - 1))
                return ins
            P.task("tensor", mm1, reads=[w1bf] + xt, writes=PH4)
            def gel(e, f0=f0, nfg=nfg, PH4=PH4):
                ins = None
                for fi in range(nfg):
                    f = f0 + fi
                    ins = e.activation(hid[:, f*SC:(f+1)*SC], PH4[fi].ap(),
                                       AF.Gelu, bias=b1c[:, f:f+1])
                return ins
            P.task("scalar", gel, reads=PH4 + [b1c], writes=[hid])
        for s2 in range(SC // 128):
            i = (t0 + s2*128) // 128
            PO = po[s2 % 2]
            def mm2(e, s2=s2, PO=PO):
                ins = None
                for half in range(0, DOUT, 512):
                    for f in range(nf):
                        ins = e.matmul(PO.ap()[:, half:half+512], hid[:, f*SC + s2*128 : f*SC + (s2+1)*128],
                                       wtile(w2bf, f, DOUT, half, half+512), start=(f == 0), stop=(f == nf - 1))
                return ins
            P.task("tensor", mm2, reads=[hid, w2bf], writes=[PO])
            out_cb(i, PO, b2b)

def stage_mlp(P, c, W, zmam_fm_bf, zmam_tm, mlpz_tm, name="s4b"):
    zt = [P.sbuf(f"{name}_zt{i}", [128, 1024], F32) for i in range(2)]
    mo = [P.sbuf(f"{name}_mo{i}", [128, 1024], F32) for i in range(1)]
    def cb(i, PO, b2b):
        b = i % 2
        ZT, MO = zt[b], mo[0]
        P.dma("sync", ZT.ap(), zmam_tm[i*128:(i+1)*128, :], reads=[zmam_tm], writes=[ZT])
        P.task("vector", lambda e, PO=PO, MO=MO: e.scalar_tensor_tensor(
            MO.ap(), PO.ap(), 1.0, b2b.ap(), ALU.mult, ALU.add), reads=[PO, b2b], writes=[MO])
        P.task("vector", lambda e, MO=MO, ZT=ZT: e.tensor_tensor(MO.ap(), MO.ap(), ZT.ap(), ALU.add),
               reads=[MO, ZT], writes=[MO])
        P.dma("gpsimd", mlpz_tm[i*128:(i+1)*128, :], MO.ap(), reads=[MO], writes=[mlpz_tm])
    ffn_fm(P, c, name, zmam_fm_bf, W["mlp_w1T"], W["mlp_b1"], W["mlp_w2T"], W["mlp_b2"],
           1024, 4096, 1024, cb)

def stage_rms_out(P, c, W, mlpz_tm, attn0_tm, name="s4c"):
    n1w = bcast_row(P, c, f"{name}_n1w", W["norm1_w"], 1024)
    xt = [P.sbuf(f"{name}_x{i}", [128, 1024], F32) for i in range(2)]
    sq = P.sbuf(f"{name}_sq", [128, 1024], F32)
    cols = P.sbuf(f"{name}_cols", [128, 4], F32)
    for i in range(TLOC // 128):
        X = xt[i % 2]
        P.dma("sync", X.ap(), mlpz_tm[i*128:(i+1)*128, :], reads=[mlpz_tm], writes=[X])
        rms_cols(P, (sq, cols), X.ap(), [X], 1e-6, 1024)
        def rmsap(e, X=X):
            e.tensor_scalar(X.ap(), X.ap(), cols[:, 2:3], None, ALU.mult)
            return e.tensor_tensor(X.ap(), X.ap(), n1w.ap(), ALU.mult)
        P.task("vector", rmsap, reads=[X, cols, n1w], writes=[X])
        P.dma("gpsimd", attn0_tm[i*128:(i+1)*128, :], X.ap(), reads=[X], writes=[attn0_tm])

def stage_router(P, c, W, xn_tm, wgt_all, name="rt"):
    rwb = []
    for e8 in range(8):
        rwb.append(bcast_row(P, c, f"{name}_rw{e8}", W["router_w"].ap()[e8:e8+1, :], 1024))
    xt = [P.sbuf(f"{name}_x{i}", [128, 1024], F32) for i in range(2)]
    junk = P.sbuf(f"{name}_junk", [128, 1024], F32)
    L = P.sbuf(f"{name}_L", [128, 8], F32)
    E = P.sbuf(f"{name}_E", [128, 8], F32)
    CN = P.sbuf(f"{name}_CN", [128, 8], F32)
    TMPQ = P.sbuf(f"{name}_TQ", [128, 8], F32)
    cols = P.sbuf(f"{name}_cols", [128, 4], F32)
    for i in range(TLOC // 128):
        X = xt[i % 2]
        P.dma("sync", X.ap(), xn_tm[i*128:(i+1)*128, :], reads=[xn_tm], writes=[X])
        for e8 in range(8):
            P.task("vector", lambda e, X=X, e8=e8: e.scalar_tensor_tensor(
                junk.ap(), X.ap(), 1.0, rwb[e8].ap(), ALU.mult, ALU.mult, accum_out=L[:, e8:e8+1]),
                reads=[X, rwb[e8]], writes=[junk, L])
        P.task("vector", lambda e: e.tensor_reduce(cols[:, 0:1], L.ap(), AX.X, ALU.max), reads=[L], writes=[cols])
        P.task("vector", lambda e: e.tensor_scalar(cols[:, 1:2], cols[:, 0:1], -1.0, None, ALU.mult),
               reads=[cols], writes=[cols], strict=True)
        P.task("scalar", lambda e: e.activation(E.ap(), L.ap(), AF.Exp, bias=cols[:, 1:2],
                                                 accum_out=cols[:, 2:3]),
               reads=[L, cols], writes=[E, cols])
        P.task("vector", lambda e: e.reciprocal(cols[:, 3:4], cols[:, 2:3]), reads=[cols], writes=[cols], strict=True)
        P.task("vector", lambda e: e.tensor_scalar(E.ap(), E.ap(), cols[:, 3:4], None, ALU.mult),
               reads=[E, cols], writes=[E], strict=True)
        P.task("vector", lambda e: e.memset(CN.ap(), 0.0), writes=[CN])
        for dlt in range(1, 8):
            P.task("vector", lambda e, dlt=dlt: e.tensor_tensor(TMPQ[:, 0:8-dlt], E[:, dlt:8], E[:, 0:8-dlt], ALU.is_gt),
                   reads=[E], writes=[TMPQ], strict=True)
            P.task("vector", lambda e, dlt=dlt: e.tensor_tensor(CN[:, 0:8-dlt], CN[:, 0:8-dlt], TMPQ[:, 0:8-dlt], ALU.add),
                   reads=[CN, TMPQ], writes=[CN], strict=True)
            P.task("vector", lambda e, dlt=dlt: e.tensor_tensor(TMPQ[:, 8-dlt:8], E[:, 0:dlt], E[:, 8-dlt:8], ALU.is_ge),
                   reads=[E], writes=[TMPQ], strict=True)
            P.task("vector", lambda e, dlt=dlt: e.tensor_tensor(CN[:, 8-dlt:8], CN[:, 8-dlt:8], TMPQ[:, 8-dlt:8], ALU.add),
                   reads=[CN, TMPQ], writes=[CN], strict=True)
        P.task("vector", lambda e: e.tensor_scalar(CN.ap(), CN.ap(), 2.0, None, ALU.is_lt),
               reads=[CN], writes=[CN], strict=True)
        P.task("vector", lambda e, i=i: e.tensor_tensor(wgt_all[:, i*8:(i+1)*8], CN.ap(), E.ap(), ALU.mult),
               reads=[CN, E], writes=[wgt_all], strict=True)

def stage_moe(P, c, W, xn_fm_bf, wgt_all, name="moe"):
    moe_acc = P.dram_tmp(f"{name}_acc", [TLOC, 1024], F32)
    for ex in range(8):
        mk = P.arena_save()
        nta = P.sbuf(f"{name}_nta{ex}", [128, 2 * 1024], F32)
        nt_ = [Buf(nta.ap()[:, (s2 % 2)*1024:((s2 % 2)+1)*1024], f"{name}_n{ex}_{s2}", state=nta) for s2 in range(4)]
        def pre(sc, ex=ex, nta=nta, half=0):
            if ex == 0:
                return
            P.dma("gpsimd", nta.ap().rearrange("p (s c) -> p s c", s=2),
                  moe_acc.ap()[sc*512 + half*256 : sc*512 + half*256 + 256, :].rearrange("(s p) c -> p s c", p=128),
                  reads=[moe_acc], writes=[nta])
        def cb(i, PO, b2b, ex=ex, nt_=nt_):
            NT = nt_[i % 4]
            if ex > 0:
                P.task("vector", lambda e, PO=PO, NT=NT, i=i, ex=ex: e.scalar_tensor_tensor(
                    NT.ap(), PO.ap(), wgt_all[:, i*8+ex:i*8+ex+1], NT.ap(), ALU.mult, ALU.add),
                    reads=[PO, wgt_all, NT], writes=[NT])
            else:
                P.task("vector", lambda e, PO=PO, NT=NT, i=i, ex=ex: e.tensor_scalar(
                    NT.ap(), PO.ap(), wgt_all[:, i*8+ex:i*8+ex+1], None, ALU.mult),
                    reads=[PO, wgt_all], writes=[NT])
            P.task("vector", lambda e, NT=NT, i=i, ex=ex, b2b=b2b: e.scalar_tensor_tensor(
                NT.ap(), b2b.ap(), wgt_all[:, i*8+ex:i*8+ex+1], NT.ap(), ALU.mult, ALU.add),
                reads=[b2b, wgt_all, NT], writes=[NT])
            if i % 2 == 1:
                P.dma("gpsimd", moe_acc.ap()[(i-1)*128:(i+1)*128, :].rearrange("(s p) c -> p s c", p=128),
                      nta.ap().rearrange("p (s c) -> p s c", s=2),
                      reads=[nta], writes=[moe_acc])
                if i % 4 == 1 and ex > 0:
                    P.dma("gpsimd", nta.ap().rearrange("p (s c) -> p s c", s=2),
                          moe_acc.ap()[(i+1)*128:(i+3)*128, :].rearrange("(s p) c -> p s c", p=128),
                          reads=[moe_acc], writes=[nta])
        ffn_fm(P, c, f"{name}{ex}", xn_fm_bf,
               Buf(W["moe_w1"].ap()[ex*1024:(ex+1)*1024, :], "w1s", state=W["moe_w1"]),
               Buf(W["moe_b1"].ap()[ex*4096:(ex+1)*4096, :], "b1s", state=W["moe_b1"]),
               Buf(W["moe_w2"].ap()[ex*4096:(ex+1)*4096, :], "w2s", state=W["moe_w2"]),
               Buf(W["moe_b2"].ap()[ex:ex+1, :], "b2s", state=W["moe_b2"]),
               1024, 4096, 1024, cb, sc_prolog=pre)
        P.arena_restore(mk)
    return moe_acc

def stage_x1(P, c, moe_acc, x_tm, attn0_tm, x1_tm, name="x1f"):
    xt = [P.sbuf(f"{name}_fx{i}", [128, 1024], F32) for i in range(2)]
    a0 = [P.sbuf(f"{name}_fa{i}", [128, 1024], F32) for i in range(2)]
    mc = [P.sbuf(f"{name}_fm{i}", [128, 1024], F32) for i in range(2)]
    for i in range(TLOC // 128):
        b = i % 2
        XT, A0, MC = xt[b], a0[b], mc[b]
        P.dma("sync", XT.ap(), x_tm[i*128:(i+1)*128, :], reads=[x_tm], writes=[XT])
        P.dma("sync", A0.ap(), attn0_tm[i*128:(i+1)*128, :], reads=[attn0_tm], writes=[A0])
        P.dma("sync", MC.ap(), moe_acc.ap()[i*128:(i+1)*128, :], reads=[moe_acc], writes=[MC])
        P.task("vector", lambda e, XT=XT, A0=A0: e.tensor_tensor(XT.ap(), XT.ap(), A0.ap(), ALU.add),
               reads=[XT, A0], writes=[XT])
        P.task("vector", lambda e, XT=XT, MC=MC: e.tensor_tensor(XT.ap(), XT.ap(), MC.ap(), ALU.add),
               reads=[XT, MC], writes=[XT])
        P.dma("gpsimd", x1_tm[i*128:(i+1)*128, :], XT.ap(), reads=[XT], writes=[x1_tm])

def stage_pe_h(P, c, W, xn1_tm, h_tm, h_fm_bf, name="pe"):
    pet = [P.sbuf(f"{name}_pe{i}", [128, 1024], F32) for i in range(2)]
    xt = [P.sbuf(f"{name}_x{i}", [128, 1024], F32) for i in range(2)]
    hbf = [P.sbuf(f"{name}_hbf{i}", [128, 1024], BF16) for i in range(2)]
    tp = [Buf(P.psum_f[2].h.bitcast(BF16).ap()[:, 0:1024], f"{name}_tp0", state=P.psum_f[2]),
          Buf(P.psum_f[3].h.bitcast(BF16).ap()[:, 0:1024], f"{name}_tp1", state=P.psum_f[3])]
    tpo = [P.sbuf(f"{name}_tpo{i}", [128, 1024], BF16) for i in range(2)]
    for i in range(TLOC // 128):
        b = i % 2
        XT, PT, HB = xt[b], pet[b], hbf[b]
        P.dma("sync", XT.ap(), xn1_tm[i*128:(i+1)*128, :], reads=[xn1_tm], writes=[XT])
        P.dma("sync", PT.ap(), W["pe_tm"].ap()[(i % 16)*128:((i % 16)+1)*128, :], reads=[W["pe_tm"]], writes=[PT])
        P.task("vector", lambda e, XT=XT, PT=PT: e.tensor_tensor(XT.ap(), XT.ap(), PT.ap(), ALU.add),
               reads=[XT, PT], writes=[XT])
        P.dma("gpsimd", h_tm[i*128:(i+1)*128, :], XT.ap(), reads=[XT], writes=[h_tm])
        P.task("scalar", lambda e, XT=XT, HB=HB: e.copy(HB.ap(), XT.ap()), reads=[XT], writes=[HB])
        TPB, TOB = tp[b], tpo[b]
        def trp(e, HB=HB, TPB=TPB):
            ins = None
            for j in range(8):
                ins = e.transpose(TPB[:, j*128:(j+1)*128], HB[:, j*128:(j+1)*128], c["ident_bf"].ap())
            return ins
        P.task("tensor", trp, reads=[HB, c["ident_bf"]], writes=[TPB])
        P.task("scalar", lambda e, TPB=TPB, TOB=TOB: e.copy(TOB.ap(), TPB.ap()), reads=[TPB], writes=[TOB])
        for j in range(8):
            P.dma("gpsimd", h_fm_bf[j*128:(j+1)*128, i*128:(i+1)*128], TOB[:, j*128:(j+1)*128],
                  reads=[TOB], writes=[h_fm_bf])

def stage_qkv(P, c, W, h_fm_bf, q_fm_bf, k_fm_bf, v_tm_bf, name="qkv"):
    wq = load_cast_bf(P, f"{name}_wq", W["Wq"], 1024, 1024)
    wk = load_cast_bf(P, f"{name}_wk", W["Wk"], 1024, 1024)
    wv = load_cast_bf(P, f"{name}_wv", W["Wv"], 1024, 1024)
    bqc = P.sbuf(f"{name}_bq", [128, 8], F32)
    bkc = P.sbuf(f"{name}_bk", [128, 8], F32)
    for j in range(8):
        P.dma("sync", bqc[:, j:j+1], W["bq"].ap()[j*128:(j+1)*128, :], reads=[W["bq"]], writes=[bqc])
        P.dma("sync", bkc[:, j:j+1], W["bk"].ap()[j*128:(j+1)*128, :], reads=[W["bk"]], writes=[bkc])
    bvb = bcast_row(P, c, f"{name}_bv", W["bv_row"], 1024)
    TC = 512
    hta = P.sbuf(f"{name}_hta", [128, 8 * TC], BF16)
    ht = [Buf(hta.ap()[:, k*TC:(k+1)*TC], f"{name}_h{k}", state=hta) for k in range(8)]
    pm = [P.pf(0, 0, TC), P.pf(1, 0, TC)]
    ob = P.sbuf(f"{name}_ob", [128, TC], BF16)
    obf = P.sbuf(f"{name}_obf", [128, TC], F32)
    for ch in range(TLOC // TC):
        t0 = ch * TC
        P.dma("sync", hta.ap().rearrange("p (k c) -> p k c", k=8),
              h_fm_bf[0:1024, t0:t0+TC].rearrange("(k p) c -> p k c", p=128),
              reads=[h_fm_bf], writes=[hta])
        for (wbf, bc, dst) in ((wq, bqc, q_fm_bf), (wk, bkc, k_fm_bf)):
            for et in range(8):
                PM = pm[et % 2]
                def mm(e, wbf=wbf, et=et, PM=PM):
                    ins = None
                    for dk in range(8):
                        ins = e.matmul(PM.ap(), wtile(wbf, dk, 1024, et*128, (et+1)*128), ht[dk].ap(),
                                       start=(dk == 0), stop=(dk == 7))
                    return ins
                P.task("tensor", mm, reads=[wbf] + ht, writes=[PM])
                P.task("vector", lambda e, PM=PM, bc=bc, et=et: e.tensor_scalar(
                    obf.ap(), PM.ap(), bc[:, et:et+1], None, ALU.add), reads=[PM, bc], writes=[obf])
                P.task("scalar", lambda e: e.copy(ob.ap(), obf.ap()), reads=[obf], writes=[ob])
                P.dma("gpsimd", dst[et*128:(et+1)*128, t0:t0+TC], ob.ap(), reads=[ob], writes=[dst])
    vt = [P.sbuf(f"{name}_vt{i}", [128, 1024], F32) for i in range(2)]
    vb = [P.sbuf(f"{name}_vb{i}", [128, 1024], BF16) for i in range(2)]
    pmv = [P.pf(0), P.pf(1)]
    hlbig = [P.sbuf(f"{name}_hlb{i}", [128, 8 * 512], BF16) for i in range(2)]
    for i in range(TLOC // 128):
        b = i % 2
        PM, VT, VB = pmv[b], vt[b], vb[b]
        ch = i // 4
        sub = i % 4
        HL = hlbig[ch % 2]
        if sub == 0:
            P.dma("sync", HL.ap().rearrange("p (k c) -> p k c", k=8),
                  h_fm_bf[0:1024, ch*512:(ch+1)*512].rearrange("(k p) c -> p k c", p=128),
                  reads=[h_fm_bf], writes=[HL])
        def mmv(e, PM=PM, HL=HL, sub=sub):
            ins = None
            for half in range(2):
                for dk in range(8):
                    ins = e.matmul(PM.ap()[:, half*512:(half+1)*512],
                                   HL[:, dk*512 + sub*128 : dk*512 + (sub+1)*128],
                                   wtile(wv, dk, 1024, half*512, (half+1)*512),
                                   start=(dk == 0), stop=(dk == 7))
            return ins
        P.task("tensor", mmv, reads=[wv, HL], writes=[PM])
        P.task("vector", lambda e, PM=PM, VT=VT: e.scalar_tensor_tensor(
            VT.ap(), PM.ap(), 1.0, bvb.ap(), ALU.mult, ALU.add), reads=[PM, bvb], writes=[VT])
        P.task("scalar", lambda e, VT=VT, VB=VB: e.copy(VB.ap(), VT.ap()), reads=[VT], writes=[VB])
        P.dma("gpsimd", v_tm_bf[i*128:(i+1)*128, :], VB.ap(), reads=[VB], writes=[v_tm_bf])

def stage_attn_core_gen(P, c, q_fm_bf, k_fm_bf, v_tm_bf, attnP_fm_bf, name="att"):
    ksl = [P.sbuf(f"{name}_k{b}", [64, SROW], BF16) for b in range(2)]
    qsl = [P.sbuf(f"{name}_q{b}", [64, 512], BF16) for b in range(2)]
    vva = [P.sbuf(f"{name}_vva{b}", [128, 16 * 65], BF16) for b in range(2)]
    vv = [[Buf(vva[b].ap()[:, kt*65:(kt+1)*65], f"{name}_v{b}_{kt}", state=vva[b]) for kt in range(16)] for b in range(2)]
    Ebf = [[P.sbuf(f"{name}_E{b}_{kt}", [128, 512], BF16) for kt in range(16)] for b in range(2)]
    rr = [P.sbuf(f"{name}_rr{b}", [1, 512], F32) for b in range(2)]
    rb = [P.sbuf(f"{name}_rb{b}", [64, 512], F32) for b in range(2)]
    ab = [P.sbuf(f"{name}_ab{b}", [64, 512], BF16) for b in range(2)]
    ps4 = [P.pf(0, 0, 512), P.pf(0, 512, 1024), P.pf(1, 0, 512), P.pf(1, 512, 1024)]
    pav = [P.pf(2, 0, 512, 0, 65, name=f"{name}_pav0"), P.pf(3, 0, 512, 0, 65, name=f"{name}_pav1")]
    prb = [P.pf(2, 512, 1024, 0, 64, name=f"{name}_prb0"), P.pf(3, 512, 1024, 0, 64, name=f"{name}_prb1")]
    for b in range(2):
        for kt in range(16):
            P.task("vector", lambda e, b=b, kt=kt: e.memset(vv[b][kt].ap()[:, 64:65], 1.0), writes=[vv[b][kt]])
    for r in range(2):
        for h in range(16):
            hb = h % 2
            KS, E1, V1 = ksl[hb], Ebf[hb], vv[hb]
            P.dma("scalar", KS.ap(), k_fm_bf[h*64:(h+1)*64, r*SROW:(r+1)*SROW], reads=[k_fm_bf], writes=[KS])
            P.dma("scalar", vva[hb].ap().rearrange("p (k c) -> p k c", k=16)[:, :, 0:64],
                  v_tm_bf[r*SROW:(r+1)*SROW, h*64:(h+1)*64].rearrange("(k p) c -> p k c", p=128),
                  reads=[v_tm_bf], writes=[vva[hb]])
            for qc in range(4):
                q0 = r*SROW + qc*512
                QS = qsl[qc % 2]
                P.dma("sync", QS.ap(), q_fm_bf[h*64:(h+1)*64, q0:q0+512], reads=[q_fm_bf], writes=[QS])
                for k4 in range(0, 16, 4):
                    PSg = [ps4[(k4 // 4 + i) % 4] for i in range(1)]
                    PS4 = [ps4[0], ps4[1], ps4[2], ps4[3]] if k4 == 0 else None
                    def mmsc(e, k4=k4, KS=KS, QS=QS):
                        ins = None
                        for i in range(4):
                            kt = k4 + i
                            ins = e.matmul(ps4[i].ap(), KS[:, kt*128:(kt+1)*128], QS.ap(),
                                           start=True, stop=True)
                        return ins
                    P.task("tensor", mmsc, reads=[KS, QS], writes=ps4)
                    def expv(e, k4=k4, E1=E1):
                        ins = None
                        for i in range(4):
                            kt = k4 + i
                            ins = e.activation(E1[kt].ap(), ps4[i].ap(), AF.Exp, scale=P.constf(0.125))
                        return ins
                    P.task("scalar", expv, reads=ps4, writes=[E1[k4], E1[k4+1], E1[k4+2], E1[k4+3]])
                PAV, PRB = pav[hb], prb[hb]
                def mmav(e, E1=E1, V1=V1, PAV=PAV):
                    ins = None
                    for kt in range(16):
                        ins = e.matmul(PAV.ap(), V1[kt].ap(), E1[kt].ap(), start=(kt == 0), stop=(kt == 15))
                    return ins
                P.task("tensor", mmav, reads=V1 + E1, writes=[PAV])
                RR, RB, AB = rr[hb], rb[hb], ab[hb]
                P.task("vector", lambda e, PAV=PAV, RR=RR: e.reciprocal(RR.ap(), PAV.ap()[64:65, :]),
                       reads=[PAV], writes=[RR])
                P.task("tensor", lambda e, RR=RR, PRB=PRB: e.matmul(PRB.ap(), c["ones_row"][0:1, 0:64], RR.ap(),
                                                    start=True, stop=True),
                       reads=[c["ones_row"], RR], writes=[PRB])
                P.task("scalar", lambda e, PRB=PRB, RB=RB: e.copy(RB.ap(), PRB.ap()), reads=[PRB], writes=[RB])
                P.task("vector", lambda e, PAV=PAV, RB=RB, AB=AB: e.tensor_tensor(AB.ap(), PAV.ap()[0:64, :], RB.ap(), ALU.mult),
                       reads=[PAV, RB], writes=[AB])
                P.dma("gpsimd", attnP_fm_bf[h*64:(h+1)*64, q0:q0+512], AB.ap(), reads=[AB], writes=[attnP_fm_bf])
            yield (r, h)

def stage_wo_ln_gen(P, c, W, attnP_fm_bf, h_tm, attn1_tm, name="wo"):
    wo = load_cast_bf(P, f"{name}_wo", W["Wo"], 1024, 1024)
    bob = bcast_row(P, c, f"{name}_bo", W["bo_row"], 1024)
    gb = bcast_row(P, c, f"{name}_g", W["gamma_row"], 1024)
    bb = bcast_row(P, c, f"{name}_b", W["beta_row"], 1024)
    albig = [P.sbuf(f"{name}_alb{i}", [128, 8 * 512], BF16) for i in range(1)]
    pm = [P.pf(0), P.pf(1)]
    ht = [P.sbuf(f"{name}_h{i}", [128, 1024], F32) for i in range(2)]
    yt = [P.sbuf(f"{name}_y{i}", [128, 1024], F32) for i in range(1)]
    sq = P.sbuf(f"{name}_sq", [128, 1024], F32)
    cols = P.sbuf(f"{name}_cols", [128, 8], F32)
    for i in range(TLOC // 128):
        b = i % 2
        PM, HT, YT = pm[b], ht[b], yt[0]
        ch = i // 4
        sub = i % 4
        AL = albig[0]
        if sub == 0:
            P.dma("sync", AL.ap().rearrange("p (k c) -> p k c", k=8),
                  attnP_fm_bf[0:1024, ch*512:(ch+1)*512].rearrange("(k p) c -> p k c", p=128),
                  reads=[attnP_fm_bf], writes=[AL])
        P.dma("sync", HT.ap(), h_tm[i*128:(i+1)*128, :], reads=[h_tm], writes=[HT])
        def mm(e, PM=PM, AL=AL, sub=sub):
            ins = None
            for half in range(2):
                for k in range(8):
                    ins = e.matmul(PM.ap()[:, half*512:(half+1)*512],
                                   AL[:, k*512 + sub*128 : k*512 + (sub+1)*128],
                                   wtile(wo, k, 1024, half*512, (half+1)*512),
                                   start=(k == 0), stop=(k == 7))
            return ins
        P.task("tensor", mm, reads=[AL, wo], writes=[PM])
        def boh(e, PM=PM, YT=YT, HT=HT):
            e.scalar_tensor_tensor(YT.ap(), PM.ap(), 1.0, bob.ap(), ALU.mult, ALU.add)
            return e.tensor_tensor(YT.ap(), YT.ap(), HT.ap(), ALU.add)
        P.task("vector", boh, reads=[PM, bob, HT], writes=[YT])
        P.task("vector", lambda e, YT=YT: e.tensor_reduce(cols[:, 0:1], YT.ap(), AX.X, ALU.add),
               reads=[YT], writes=[cols])
        P.task("vector", lambda e: e.tensor_scalar(cols[:, 1:2], cols[:, 0:1], -1.0/1024, None, ALU.mult),
               reads=[cols], writes=[cols], strict=True)
        P.task("vector", lambda e, YT=YT: e.tensor_scalar(YT.ap(), YT.ap(), cols[:, 1:2], None, ALU.add),
               reads=[YT, cols], writes=[YT], strict=True)
        P.task("scalar", lambda e, YT=YT: e.activation(sq.ap(), YT.ap(), AF.Square, accum_out=cols[:, 2:3]),
               reads=[YT], writes=[sq, cols])
        P.task("scalar", lambda e: e.activation(cols[:, 3:4], cols[:, 2:3], AF.Ln, bias=P.constf(1e-5), scale=P.constf(1.0/1024)),
               reads=[cols], writes=[cols], strict=True)
        P.task("scalar", lambda e: e.activation(cols[:, 4:5], cols[:, 3:4], AF.Exp, scale=P.constf(-0.5)),
               reads=[cols], writes=[cols], strict=True)
        def lnap(e, YT=YT):
            e.tensor_scalar(YT.ap(), YT.ap(), cols[:, 4:5], None, ALU.mult)
            e.tensor_tensor(YT.ap(), YT.ap(), gb.ap(), ALU.mult)
            return e.tensor_tensor(YT.ap(), YT.ap(), bb.ap(), ALU.add)
        P.task("vector", lnap, reads=[YT, cols, gb, bb], writes=[YT])
        P.dma("gpsimd", attn1_tm[i*128:(i+1)*128, :], YT.ap(), reads=[YT], writes=[attn1_tm])
        yield i

def stage_route_gen(P, c, W, xn1_tm, normed_fm_bf, mask_cols, name="rte"):
    SQ1024 = float(np.float32(np.sqrt(np.float32(1024.0))))
    rtb = bcast_row(P, c, f"{name}_rt", W["rt_row"], 1024)
    iota_bc_ps = [P.pf(0, 0, 512), P.pf(1, 0, 512)]
    iota_bc = P.sbuf(f"{name}_iota", [128, SROW], F32)
    iota_row = Buf(P.rowtmp.ap()[0:1, 0:SROW], f"{name}_ior", state=P.rowtmp)
    idx_col = P.sbuf(f"{name}_idx", [128, 2], F32)
    P.dma("sync", iota_row[0:1, 0:SROW], W["iota_row"].ap(), reads=[W["iota_row"]], writes=[iota_row])
    P.dma("sync", idx_col[:, 0:1], W["idx_col"].ap(), reads=[W["idx_col"]], writes=[idx_col])
    for j0 in range(0, SROW, 512):
        PSI = iota_bc_ps[(j0 // 512) % 2]
        P.task("tensor", lambda e, j0=j0, PSI=PSI: e.matmul(PSI.ap(), c["ones_row"].ap(),
               P.rowtmp.ap()[0:1, j0:j0+512], start=True, stop=True), reads=[c["ones_row"], iota_row], writes=[PSI])
        P.task("scalar", lambda e, j0=j0, PSI=PSI: e.copy(iota_bc[:, j0:j0+512], PSI.ap()),
               reads=[PSI], writes=[iota_bc])
    xt = [P.sbuf(f"{name}_x{i}", [128, 1024], F32) for i in range(2)]
    nb = [P.sbuf(f"{name}_nb{i}", [128, 1024], BF16) for i in range(2)]
    sq = P.sbuf(f"{name}_sq", [128, 1024], F32)
    cols = P.sbuf(f"{name}_cols", [128, 8], F32)
    s_cols = P.sbuf(f"{name}_scols", [128, 32], F32)
    tp = [Buf(P.psum_f[2].h.bitcast(BF16).ap()[:, 0:1024], f"{name}_tp0", state=P.psum_f[2]),
          Buf(P.psum_f[3].h.bitcast(BF16).ap()[:, 0:1024], f"{name}_tp1", state=P.psum_f[3])]
    tpo = [P.sbuf(f"{name}_tpo{i}", [128, 1024], BF16) for i in range(2)]
    for i in range(TLOC // 128):
        b = i % 2
        X, NB = xt[b], nb[b]
        P.dma("sync", X.ap(), xn1_tm[i*128:(i+1)*128, :], reads=[xn1_tm], writes=[X])
        P.task("scalar", lambda e, X=X: e.activation(sq.ap(), X.ap(), AF.Square, accum_out=cols[:, 0:1]),
               reads=[X], writes=[sq, cols])
        P.task("scalar", lambda e: e.activation(cols[:, 1:2], cols[:, 0:1], AF.Ln),
               reads=[cols], writes=[cols], strict=True)
        P.task("scalar", lambda e: e.activation(cols[:, 2:3], cols[:, 1:2], AF.Exp, scale=P.constf(0.5)),
               reads=[cols], writes=[cols], strict=True)
        P.task("vector", lambda e: e.tensor_scalar(cols[:, 3:4], cols[:, 2:3], 1e-12, None, ALU.max),
               reads=[cols], writes=[cols], strict=True)
        P.task("vector", lambda e: e.reciprocal(cols[:, 4:5], cols[:, 3:4]), reads=[cols], writes=[cols], strict=True)
        P.task("vector", lambda e, X=X, i=i: e.scalar_tensor_tensor(
            sq.ap(), X.ap(), 1.0, rtb.ap(), ALU.mult, ALU.mult, accum_out=s_cols[:, i:i+1]),
            reads=[X, rtb], writes=[sq, s_cols])
        P.task("vector", lambda e, X=X: e.tensor_scalar(X.ap(), X.ap(), cols[:, 4:5], SQ1024, ALU.mult, ALU.mult),
               reads=[X, cols], writes=[X], strict=True)
        P.task("scalar", lambda e, X=X, NB=NB: e.copy(NB.ap(), X.ap()), reads=[X], writes=[NB])
        TPB, TOB = tp[b], tpo[b]
        def trp(e, NB=NB, TPB=TPB):
            ins = None
            for j in range(8):
                ins = e.transpose(TPB[:, j*128:(j+1)*128], NB[:, j*128:(j+1)*128], c["ident_bf"].ap())
            return ins
        P.task("tensor", trp, reads=[NB, c["ident_bf"]], writes=[TPB])
        P.task("scalar", lambda e, TPB=TPB, TOB=TOB: e.copy(TOB.ap(), TPB.ap()), reads=[TPB], writes=[TOB])
        for j in range(8):
            P.dma("gpsimd", normed_fm_bf[j*128:(j+1)*128, i*128:(i+1)*128], TOB[:, j*128:(j+1)*128],
                  reads=[TOB], writes=[normed_fm_bf])
        yield ("n", i)
    s_lin = P.dram_tmp(f"{name}_slin", [32, 128], F32)
    pt = P.pf(2, 0, 128, 0, 32, name=f"{name}_pt")
    sro = P.sbuf(f"{name}_sro", [32, 128], F32)
    P.task("tensor", lambda e: e.transpose(pt.ap(), s_cols.ap(), c["ident_f"].ap()), reads=[s_cols, c["ident_f"]], writes=[pt])
    P.task("scalar", lambda e: e.copy(sro.ap(), pt.ap()), reads=[pt], writes=[sro])
    P.dma("gpsimd", s_lin.ap(), sro.ap(), reads=[sro], writes=[s_lin])
    CT = P.sbuf(f"{name}_ct", [64, SROW], F32)
    NBv = P.sbuf(f"{name}_nbv", [64, SROW], F32)
    SB = P.sbuf(f"{name}_sb", [64, SROW], F32)
    EX = P.sbuf(f"{name}_ex", [64, SROW], F32)
    CC = P.sbuf(f"{name}_cc", [64, 8], F32)
    P.task("vector", lambda e: e.memset(CT.ap(), 0.0), writes=[CT])
    P.dma("sync", CT.ap()[0:1, :], s_lin.ap()[0:16, :], reads=[s_lin], writes=[CT])
    P.dma("sync", CT.ap()[32:33, :], s_lin.ap()[16:32, :], reads=[s_lin], writes=[CT])
    P.task("vector", lambda e: e.tensor_scalar(NBv.ap(), CT.ap(), 0.0, None, ALU.max),
           reads=[CT], writes=[NBv])
    INV = INV01
    CONSTLN = float(np.float32(0.1) * np.float32(np.log(np.float32(8.0))))
    yield ("coor0", 0)
    for it in range(50):
        P.task("vector", lambda e: e.tensor_tensor(SB.ap(), CT.ap(), NBv.ap(), ALU.subtract),
               reads=[CT, NBv], writes=[SB])
        P.task("vector", lambda e: e.tensor_reduce(CC[0:64, 0:1], SB.ap(), AX.X, ALU.max),
               reads=[SB], writes=[CC])
        P.task("vector", lambda e: e.tensor_scalar(CC[0:64, 1:2], CC[0:64, 0:1], -INV, None, ALU.mult),
               reads=[CC], writes=[CC], strict=True)
        P.task("scalar", lambda e: e.activation(EX.ap(), SB.ap(), AF.Exp, bias=CC[0:64, 1:2],
                                                scale=P.constf(INV, 64), accum_out=CC[0:64, 2:3]),
               reads=[SB, CC], writes=[EX, CC])
        P.task("scalar", lambda e: e.activation(CC[0:64, 3:4], CC[0:64, 2:3], AF.Ln), reads=[CC], writes=[CC], strict=True)
        P.task("vector", lambda e: e.tensor_scalar(CC[0:64, 4:5], CC[0:64, 0:1], -1.0, CONSTLN, ALU.mult, ALU.add),
               reads=[CC], writes=[CC], strict=True)
        P.task("vector", lambda e: e.tensor_scalar(CC[0:64, 5:6], CC[0:64, 3:4], -float(np.float32(0.1)), None, ALU.mult),
               reads=[CC], writes=[CC], strict=True)
        P.task("vector", lambda e: e.tensor_tensor(CC[0:64, 6:7], CC[0:64, 5:6], CC[0:64, 4:5], ALU.add),
               reads=[CC], writes=[CC], strict=True)
        P.task("vector", lambda e: e.tensor_scalar(NBv.ap(), CT.ap(), CC[0:64, 6:7], 0.0, ALU.add, ALU.max),
               reads=[CT, CC], writes=[NBv], strict=True)
        if it % 5 == 4:
            yield ("coor", it)
    P.task("vector", lambda e: e.tensor_scalar(SB.ap(), CT.ap(), CC[0:64, 6:7], 0.0, ALU.add, ALU.min),
           reads=[CT, CC], writes=[SB], strict=True)
    P.task("scalar", lambda e: e.activation(EX.ap(), SB.ap(), AF.Exp, scale=P.constf(INV, 64)),
           reads=[SB], writes=[EX])
    FLTMIN = float(np.float32(1.17549435e-38))
    P.task("vector", lambda e: e.tensor_scalar(SB.ap(), EX.ap(), FLTMIN, None, ALU.is_ge),
           reads=[EX], writes=[SB])
    P.task("vector", lambda e: e.tensor_tensor(EX.ap(), EX.ap(), SB.ap(), ALU.mult),
           reads=[EX, SB], writes=[EX])
    Sfree = P.sbuf(f"{name}_sf", [128, SROW], F32)
    sccol = P.sbuf(f"{name}_sc", [128, 16], F32)
    gsum = P.sbuf(f"{name}_gs", [128, 8], F32)
    mlt = P.sbuf(f"{name}_ml", [128, SROW], F32)
    junk2 = mlt
    pr0 = P.pf(0, 0, 512); pr1 = P.pf(1, 0, 512)
    prt = P.pf(2, 0, 64, 0, 128, name=f"{name}_prt")
    for r in range(2):
        base = 32 * r
        for j0 in range(0, SROW, 512):
            PS = pr0 if (j0 // 512) % 2 == 0 else pr1
            P.task("tensor", lambda e, j0=j0, PS=PS, base=base: e.matmul(
                PS.ap(), c["ones_all"][base:base+1, :], EX[base:base+1, j0:j0+512], start=True, stop=True),
                reads=[c["ones_all"], EX], writes=[PS])
            P.task("scalar", lambda e, j0=j0, PS=PS: e.copy(Sfree[:, j0:j0+512], PS.ap()),
                   reads=[PS], writes=[Sfree])
        for ic in range(16):
            P.task("tensor", lambda e, ic=ic, base=base: e.matmul(
                prt.ap()[:, 0:1], EX[base:base+1, ic*128:(ic+1)*128], c["ones_all"][base:base+1, 0:1],
                start=True, stop=True), reads=[EX, c["ones_all"]], writes=[prt])
            P.task("scalar", lambda e, ic=ic: e.copy(sccol[:, ic:ic+1], prt.ap()[:, 0:1]),
                   reads=[prt], writes=[sccol])
        for ic in range(16):
            P.task("vector", lambda e, ic=ic, r=r: e.tensor_scalar(idx_col[:, 1:2], idx_col[:, 0:1],
                   float(ic*128), None, ALU.add), reads=[idx_col], writes=[idx_col])
            P.task("vector", lambda e: e.tensor_scalar(mlt.ap(), iota_bc.ap(), idx_col[:, 1:2], None, ALU.is_lt),
                   reads=[iota_bc, idx_col], writes=[mlt], strict=True)
            P.task("vector", lambda e, ic=ic: e.scalar_tensor_tensor(junk2.ap(), Sfree.ap(), sccol[:, ic:ic+1],
                   mlt.ap(), ALU.is_equal, ALU.mult, accum_out=gsum[:, 0:1]),
                   reads=[Sfree, sccol, mlt], writes=[junk2, gsum])
            P.task("vector", lambda e, ic=ic: e.tensor_scalar(junk2.ap(), Sfree.ap(), sccol[:, ic:ic+1],
                   0.0, ALU.is_gt, ALU.add, accum_out=gsum[:, 1:2]),
                   reads=[Sfree, sccol], writes=[junk2, gsum])
            P.task("vector", lambda e: e.tensor_tensor(gsum[:, 2:3], gsum[:, 0:1], gsum[:, 1:2], ALU.add),
                   reads=[gsum], writes=[gsum], strict=True)
            P.task("vector", lambda e, r=r, ic=ic: e.tensor_scalar(mask_cols[:, r*16+ic:r*16+ic+1],
                   gsum[:, 2:3], 1024.0, None, ALU.is_lt), reads=[gsum], writes=[mask_cols], strict=True)
            yield ("rank", r*16+ic)

def stage_ffn1(P, c, W, normed_fm_bf, mask_cols, x1_tm, attn1_tm, out_tm, name="f1"):
    light_tm = P.dram_tmp(f"{name}_lt", [TLOC, 1024], F32)
    mk = P.arena_save()
    lt = [P.sbuf(f"{name}_l{i}", [128, 1024], F32) for i in range(2)]
    def cbl(i, PO, b2b):
        LT = lt[i % 2]
        P.task("vector", lambda e, PO=PO, LT=LT, b2b=b2b: e.scalar_tensor_tensor(
            LT.ap(), PO.ap(), 1.0, b2b.ap(), ALU.mult, ALU.add), reads=[PO, b2b], writes=[LT])
        P.dma("gpsimd", light_tm.ap()[i*128:(i+1)*128, :], LT.ap(), reads=[LT], writes=[light_tm])
    ffn_fm(P, c, f"{name}l", normed_fm_bf, W["w1_light"], W["b1_light"], W["w2_light"], W["b2_light_row"],
           1024, 512, 1024, cbl, gamma_col_dram=W["gamma_light_col"])
    P.arena_restore(mk)
    xt = [P.sbuf(f"{name}_x{i}", [128, 1024], F32) for i in range(2)]
    a1 = [P.sbuf(f"{name}_a{i}", [128, 1024], F32) for i in range(1)]
    def cbh(i, PO, b2b):
        b = i % 2
        XT, A1 = xt[b], a1[0]
        P.dma("scalar", XT.ap(), x1_tm[i*128:(i+1)*128, :], reads=[x1_tm], writes=[XT])
        P.dma("scalar", A1.ap(), attn1_tm[i*128:(i+1)*128, :], reads=[attn1_tm], writes=[A1])
        P.task("vector", lambda e, XT=XT, A1=A1: e.tensor_tensor(XT.ap(), XT.ap(), A1.ap(), ALU.add),
               reads=[XT, A1], writes=[XT])
        P.dma("scalar", A1.ap(), light_tm.ap()[i*128:(i+1)*128, :], reads=[light_tm], writes=[A1])
        P.task("vector", lambda e, XT=XT, A1=A1: e.tensor_tensor(XT.ap(), XT.ap(), A1.ap(), ALU.add),
               reads=[XT, A1], writes=[XT])
        P.task("vector", lambda e, PO=PO, XT=XT, i=i: e.scalar_tensor_tensor(
            XT.ap(), PO.ap(), mask_cols[:, i:i+1], XT.ap(), ALU.mult, ALU.add),
            reads=[PO, mask_cols, XT], writes=[XT])
        P.task("vector", lambda e, XT=XT, i=i, b2b=b2b: e.scalar_tensor_tensor(
            XT.ap(), b2b.ap(), mask_cols[:, i:i+1], XT.ap(), ALU.mult, ALU.add),
            reads=[b2b, mask_cols, XT], writes=[XT])
        P.dma("gpsimd", out_tm[i*128:(i+1)*128, :], XT.ap(), reads=[XT], writes=[out_tm])
    ffn_fm(P, c, f"{name}h", normed_fm_bf, W["w1_heavy"], W["b1_heavy"], W["w2_heavy"], W["b2_heavy_row"],
           1024, 4096, 1024, cbh, gamma_col_dram=W["gamma_heavy_col"])


def stage_x1_ln1_pe(P, c, W, mlpz_tm, moe_acc, x_tm, x1_tm, xn1_tm, h_tm, h_fm_bf, name="xp"):
    n1w = bcast_row(P, c, f"{name}_n1w", W["norm1_w"], 1024)
    xt = [P.sbuf(f"{name}_x{i}", [128, 1024], F32) for i in range(2)]
    mz = [P.sbuf(f"{name}_mz{i}", [128, 1024], F32) for i in range(2)]
    mc = [P.sbuf(f"{name}_mc{i}", [128, 1024], F32) for i in range(2)]
    pet = [P.sbuf(f"{name}_pe{i}", [128, 1024], F32) for i in range(2)]
    hbf = [P.sbuf(f"{name}_hbf{i}", [128, 1024], BF16) for i in range(2)]
    sq = P.sbuf(f"{name}_sq", [128, 1024], F32)
    cols = [P.sbuf(f"{name}_cols{i}", [128, 8], F32) for i in range(2)]
    tp = [Buf(P.psum_f[2].h.bitcast(BF16).ap()[:, 0:1024], f"{name}_tp0", state=P.psum_f[2]),
          Buf(P.psum_f[3].h.bitcast(BF16).ap()[:, 0:1024], f"{name}_tp1", state=P.psum_f[3])]
    tpo = [P.sbuf(f"{name}_tpo{i}", [128, 1024], BF16) for i in range(2)]
    for i in range(TLOC // 128):
        b = i % 2
        X, MZ, MC, PT, HB, CL = xt[b], mz[b], mc[b], pet[b], hbf[b], cols[b]
        P.dma("sync", MZ.ap(), mlpz_tm[i*128:(i+1)*128, :], reads=[mlpz_tm], writes=[MZ])
        P.dma("sync", X.ap(), x_tm[i*128:(i+1)*128, :], reads=[x_tm], writes=[X])
        P.dma("gpsimd", MC.ap(), moe_acc.ap()[i*128:(i+1)*128, :], reads=[moe_acc], writes=[MC])
        P.dma("scalar", PT.ap(), W["pe_tm"].ap()[(i % 16)*128:((i % 16)+1)*128, :], reads=[W["pe_tm"]], writes=[PT])
        rms_cols(P, (sq, CL), MZ.ap(), [MZ], 1e-6, 1024)
        def a0x1(e, X=X, MZ=MZ, MC=MC, CL=CL):
            e.scalar_tensor_tensor(MZ.ap(), MZ.ap(), CL[:, 2:3], n1w.ap(), ALU.mult, ALU.mult)
            e.tensor_tensor(X.ap(), X.ap(), MZ.ap(), ALU.add)
            return e.tensor_tensor(X.ap(), X.ap(), MC.ap(), ALU.add)
        P.task("vector", a0x1, reads=[X, MZ, MC, CL, n1w], writes=[X, MZ], strict=True)
        P.dma("gpsimd", x1_tm[i*128:(i+1)*128, :], X.ap(), reads=[X], writes=[x1_tm])
        P.task("vector", lambda e, X=X, CL=CL: e.tensor_reduce(CL[:, 0:1], X.ap(), AX.X, ALU.add),
               reads=[X], writes=[CL])
        P.task("vector", lambda e, CL=CL: e.tensor_scalar(CL[:, 1:2], CL[:, 0:1], -1.0/1024, None, ALU.mult),
               reads=[CL], writes=[CL], strict=True)
        P.task("vector", lambda e, X=X, CL=CL: e.tensor_scalar(X.ap(), X.ap(), CL[:, 1:2], None, ALU.add),
               reads=[X, CL], writes=[X], strict=True)
        P.task("scalar", lambda e, X=X, CL=CL: e.activation(sq.ap(), X.ap(), AF.Square, accum_out=CL[:, 2:3]),
               reads=[X], writes=[sq, CL])
        P.task("scalar", lambda e, CL=CL: e.activation(CL[:, 3:4], CL[:, 2:3], AF.Ln, bias=P.constf(1e-5), scale=P.constf(1.0/1024)),
               reads=[CL], writes=[CL], strict=True)
        P.task("scalar", lambda e, CL=CL: e.activation(CL[:, 4:5], CL[:, 3:4], AF.Exp, scale=P.constf(-0.5)),
               reads=[CL], writes=[CL], strict=True)
        def xnh(e, X=X, CL=CL, PT=PT):
            e.tensor_scalar(X.ap(), X.ap(), CL[:, 4:5], None, ALU.mult)
            return e.tensor_tensor(PT.ap(), X.ap(), PT.ap(), ALU.add)
        P.task("vector", xnh, reads=[X, CL, PT], writes=[X, PT], strict=True)
        P.dma("gpsimd", xn1_tm[i*128:(i+1)*128, :], X.ap(), reads=[X], writes=[xn1_tm])
        P.dma("gpsimd", h_tm[i*128:(i+1)*128, :], PT.ap(), reads=[PT], writes=[h_tm])
        P.task("scalar", lambda e, PT=PT, HB=HB: e.copy(HB.ap(), PT.ap()), reads=[PT], writes=[HB])
        TPB, TOB = tp[b], tpo[b]
        def trp(e, HB=HB, TPB=TPB):
            ins = None
            for j in range(8):
                ins = e.transpose(TPB[:, j*128:(j+1)*128], HB[:, j*128:(j+1)*128], c["ident_bf"].ap())
            return ins
        P.task("tensor", trp, reads=[HB, c["ident_bf"]], writes=[TPB])
        P.task("scalar", lambda e, TPB=TPB, TOB=TOB: e.copy(TOB.ap(), TPB.ap()), reads=[TPB], writes=[TOB])
        for j in range(8):
            P.dma("gpsimd", h_fm_bf[j*128:(j+1)*128, i*128:(i+1)*128], TOB[:, j*128:(j+1)*128],
                  reads=[TOB], writes=[h_fm_bf])

def build_program():
    TL = TLOC
    P = Prog()
    c = emit_consts(P)
    wgt_all = P.persist("wgt_all", [128, 256], F32)
    mask_cols = P.persist("mask_cols", [128, 32], F32)
    D_in = lambda n, sh: P.dram_in(n, sh, F32)
    W = {k: D_in(k, sh) for k, sh in [
        ("W_inT", [1024, 4096]), ("x_projT", [2048, 96]), ("dt_projT", [64, 2048]),
        ("conv_w", [2048, 4]), ("conv_b", [2048, 1]), ("dt_b", [2048, 1]),
        ("A_log", [2048, 16]), ("Dd", [2048, 1]),
        ("out_projT", [2048, 1024]), ("norm1_w", [1, 1024]),
        ("mlp_w1T", [1024, 4096]), ("mlp_b1", [4096, 1]), ("mlp_w2T", [4096, 1024]), ("mlp_b2", [1, 1024]),
        ("router_w", [8, 1024]),
        ("moe_w1", [8192, 4096]), ("moe_b1", [32768, 1]), ("moe_w2", [32768, 1024]), ("moe_b2", [8, 1024]),
        ("pe_tm", [2048, 1024]),
        ("Wq", [1024, 1024]), ("Wk", [1024, 1024]), ("Wv", [1024, 1024]), ("Wo", [1024, 1024]),
        ("bq", [1024, 1]), ("bk", [1024, 1]), ("bv_row", [1, 1024]), ("bo_row", [1, 1024]),
        ("gamma_row", [1, 1024]), ("beta_row", [1, 1024]),
        ("rt_row", [1, 1024]), ("iota_row", [1, 2048]), ("idx_col", [128, 1]),
        ("w1_light", [1024, 512]), ("b1_light", [512, 1]), ("w2_light", [512, 1024]),
        ("b2_light_row", [1, 1024]), ("gamma_light_col", [1024, 1]),
        ("w1_heavy", [1024, 4096]), ("b1_heavy", [4096, 1]), ("w2_heavy", [4096, 1024]),
        ("b2_heavy_row", [1, 1024]), ("gamma_heavy_col", [1024, 1]),
    ]}
    oh_dram = None
    x_in = P.dram_in("x", [TL, 1024], F32)
    out_tm = P.dram_out("out", [TL, 1024], F32)
    t = lambda n, sh, dt=F32: P.dram_tmp(n, sh, dt)
    import os
    dbg = os.environ.get("FULL_DBG")
    if dbg:
        t = lambda n, sh, dt=F32: (P.dram_out(n, sh, dt) if n in ("attn0_tm", "x1_tm", "attn1_tm", "zmam_tm", "mlpz_tm") else P.dram_tmp(n, sh, dt))
    xn_tm = t("xn_tm", [TL, 1024]); xn_fm_bf = t("xn_fm", [1024, TL], BF16)
    xc_fm = t("xc_fm", [2048, TL]); zs_fm = t("zs_fm", [2048, TL]); dt_fm = t("dt_fm", [2048, TL])
    xdbl_fm = t("xdbl_fm", [96, TL]); bc_fm = t("bc_fm", [32, TL])
    y_fm_bf = t("y_fm", [2048, TL], BF16)
    zmam_tm = t("zmam_tm", [TL, 1024]); zmam_fm_bf = t("zmam_fm", [1024, TL], BF16)
    mlpz_tm = t("mlpz_tm", [TL, 1024]); attn0_tm = t("attn0_tm", [TL, 1024])
    x1_tm = t("x1_tm", [TL, 1024]); xn1_tm = t("xn1_tm", [TL, 1024])
    h_tm = t("h_tm", [TL, 1024]); h_fm_bf = t("h_fm", [1024, TL], BF16)
    q_fm_bf = t("q_fm", [1024, TL], BF16); k_fm_bf = t("k_fm", [1024, TL], BF16)
    v_tm_bf = t("v_tm", [TL, 1024], BF16)
    attnP_fm_bf = t("attnP_fm", [1024, TL], BF16); attn1_tm = t("attn1_tm", [TL, 1024])
    normed_fm_bf = t("normed_fm", [1024, TL], BF16)

    m0 = P.arena_save()
    P.stage_marks = []
    def MARK(nm):
        P.stage_marks.append((nm, dict(P.count)))
    P.MARK = MARK
    stage_ln(P, c, x_in, xn_tm, xn_fm_bf, name="ln0"); MARK("ln0"); P.arena_restore(m0)
    stage_mamba_front(P, c, W, xn_fm_bf, xc_fm, zs_fm, dt_fm, xdbl_fm, bc_fm, name="mf"); MARK("mf"); P.arena_restore(m0)
    for _ in stage_scan_gen(P, c, W, dt_fm, xc_fm, zs_fm, bc_fm, y_fm_bf, name="sc"):
        pass
    MARK("sc"); P.arena_restore(m0)
    for _ in stage_outproj_rms_gen(P, c, W, y_fm_bf, xn_tm, zmam_tm, zmam_fm_bf, name="s4a"):
        pass
    MARK("s4a"); P.arena_restore(m0)
    stage_router(P, c, W, xn_tm, wgt_all, name="rt"); MARK("rt"); P.arena_restore(m0)
    stage_mlp(P, c, W, zmam_fm_bf, zmam_tm, mlpz_tm, name="s4b"); MARK("s4b"); P.arena_restore(m0)
    moe_acc = stage_moe(P, c, W, xn_fm_bf, wgt_all, name="moe"); MARK("moe"); P.arena_restore(m0)
    stage_x1_ln1_pe(P, c, W, mlpz_tm, moe_acc, x_in, x1_tm, xn1_tm, h_tm, h_fm_bf, name="xp"); MARK("xp"); P.arena_restore(m0)
    stage_qkv(P, c, W, h_fm_bf, q_fm_bf, k_fm_bf, v_tm_bf, name="qkv"); MARK("qkv"); P.arena_restore(m0)
    g_att = stage_attn_core_gen(P, c, q_fm_bf, k_fm_bf, v_tm_bf, attnP_fm_bf, name="att")
    g_wo = stage_wo_ln_gen(P, c, W, attnP_fm_bf, h_tm, attn1_tm, name="wo")
    g_rte = stage_route_gen(P, c, W, xn1_tm, normed_fm_bf, mask_cols, name="rte")
    def pump(g, n):
        for _ in range(n):
            try:
                next(g)
            except StopIteration:
                return False
        return True
    for step in range(32):
        pump(g_att, 1)
        pump(g_rte, 3)
        if step >= 16:
            pump(g_wo, 1)
    while pump(g_rte, 1):
        pass
    while pump(g_wo, 1):
        pass
    MARK("att+wo+rte"); P.arena_restore(m0)
    stage_ffn1(P, c, W, normed_fm_bf, mask_cols, x1_tm, attn1_tm, out_tm, name="f1"); MARK("f1")
    P.finish_all()
    return P

def host_weights(z):
    asc = np.ascontiguousarray
    S_ = 2048; D_ = 1024
    pos = np.arange(S_, dtype=np.float64)[:, None]
    div = np.exp(np.arange(0, D_, 2, dtype=np.float64) * (-np.log(10000.0) / D_))
    pe = np.zeros((S_, D_), dtype=np.float32)
    pe[:, 0::2] = np.sin(pos * div).astype(np.float32)
    pe[:, 1::2] = np.cos(pos * div).astype(np.float32)
    w = {
        "W_inT": asc(z["L0_attn_in_proj_w"].T), "x_projT": asc(z["L0_attn_x_proj_w"].T),
        "dt_projT": asc(z["L0_attn_dt_proj_w"].T), "conv_w": asc(z["L0_attn_conv_w"]),
        "conv_b": z["L0_attn_conv_b"].reshape(-1, 1), "dt_b": z["L0_attn_dt_proj_b"].reshape(-1, 1),
        "A_log": asc(z["L0_attn_A_log"]), "Dd": z["L0_attn_D"].reshape(-1, 1),
        "out_projT": asc(z["L0_attn_out_proj_w"].T), "norm1_w": z["L0_attn_norm1_w"].reshape(1, -1),
        "mlp_w1T": asc(z["L0_attn_mlp_w1"].T), "mlp_b1": z["L0_attn_mlp_b1"].reshape(-1, 1),
        "mlp_w2T": asc(z["L0_attn_mlp_w2"].T), "mlp_b2": z["L0_attn_mlp_b2"].reshape(1, -1),
        "router_w": asc(z["L0_ffn_router_w"]),
        "moe_w1": asc(z["L0_ffn_w1"].reshape(8 * 1024, 4096)),
        "moe_b1": z["L0_ffn_b1"].reshape(-1, 1),
        "moe_w2": asc(z["L0_ffn_w2"].reshape(8 * 4096, 1024)),
        "moe_b2": asc(z["L0_ffn_b2"]),
        "pe_tm": pe,
        "Wq": asc(z["L1_attn_Wq"]), "Wk": asc(z["L1_attn_Wk"]),
        "Wv": asc(z["L1_attn_Wv"]), "Wo": asc(z["L1_attn_Wo"]),
        "bq": z["L1_attn_bq"].reshape(-1, 1), "bk": z["L1_attn_bk"].reshape(-1, 1),
        "bv_row": z["L1_attn_bv"].reshape(1, -1), "bo_row": z["L1_attn_bo"].reshape(1, -1),
        "gamma_row": z["L1_attn_gamma"].reshape(1, -1), "beta_row": z["L1_attn_beta"].reshape(1, -1),
        "rt_row": z["L1_ffn_routing_token"].reshape(1, -1),
        "iota_row": np.arange(2048, dtype=np.float32).reshape(1, -1),
        "idx_col": np.arange(128, dtype=np.float32).reshape(-1, 1),
        "w1_light": asc(z["L1_ffn_w1_light"]), "b1_light": z["L1_ffn_b1_light"].reshape(-1, 1),
        "w2_light": asc(z["L1_ffn_w2_light"]), "b2_light_row": z["L1_ffn_b2_light"].reshape(1, -1),
        "gamma_light_col": z["L1_ffn_gamma_light"].reshape(-1, 1),
        "w1_heavy": asc(z["L1_ffn_w1_heavy"]), "b1_heavy": z["L1_ffn_b1_heavy"].reshape(-1, 1),
        "w2_heavy": asc(z["L1_ffn_w2_heavy"]), "b2_heavy_row": z["L1_ffn_b2_heavy"].reshape(1, -1),
        "gamma_heavy_col": z["L1_ffn_gamma_heavy"].reshape(-1, 1),
    }
    w.update(host_consts())
    w.update(host_onehot())
    return w



_CACHE = {}

def kernel(**inputs):
    import numpy as np
    from concourse.bass_utils import run_bass_kernel_spmd
    enable_ldw_opt()
    if "prog" not in _CACHE:
        P = build_program()
        nc = P.build()
        _CACHE["prog"] = (P, nc)
    P, nc = _CACHE["prog"]
    w = host_weights(inputs)
    w.update(P.host_const_pool())
    x = np.ascontiguousarray(inputs["L0_attn_Z"].astype(np.float32, copy=False))
    in_maps = []
    for cid in range(8):
        m = dict(w)
        m["x"] = np.ascontiguousarray(x[2*cid:2*cid+2].reshape(TLOC, 1024))
        in_maps.append(m)
    res = run_bass_kernel_spmd(nc, in_maps, core_ids=list(range(8)))
    out = np.empty((16, 2048, 1024), dtype=np.float32)
    for cid in range(8):
        out[2*cid:2*cid+2] = res.results[cid]["out"].reshape(2, 2048, 1024)
    return out
